# version 06
# baseline (speedup 1.0000x reference)
import contextlib
import numpy as np
import ml_dtypes
import concourse.bass as bass
import concourse.mybir as mybir
from concourse.bass_utils import run_bass_kernel_spmd

F32 = mybir.dt.float32
BF16 = mybir.dt.bfloat16
AF = mybir.ActivationFunctionType
ALU = mybir.AluOpType
AX = mybir.AxisListType.X
bfd = ml_dtypes.bfloat16

B, S, D, H, HD = 2, 8192, 1024, 16, 64
RPG = 4
TPC = S // RPG
NT = TPC // 128
NKT = S // 128
NCH = 4
QW = 512
E_FF, H_FF, G_FF = 8, 2048, 256
I_FF = 2816
WSL = I_FF // 8
NCORE = 8
DEBUG = False
L2_PHASE = 4
L2_P1N = 3
L2_K2TR = True
L2_K2SQ = True
L2_K2 = True
L2_VBIAS = True
L2_Q2 = True

ENGS = ("sync", "scalar", "vector", "tensor", "gpsimd")


class Builder:
    def __init__(self, nc, es):
        self.nc, self.es = nc, es
        self.steps = []
        self.sems = {}

    def sem(self, name):
        h = self.es.enter_context(self.nc.semaphore(name))
        self.sems[name] = [h, 0]
        return name

    def wait(self, eng, name, val=None):
        h, n = self.sems[name]
        tgt = n if val is None else val
        if tgt > 0:
            self.steps.append((eng, lambda e, h=h, t=tgt: e.wait_ge(h, t)))

    SELF_SEM = {"vector": "vE", "scalar": "sE", "gpsimd": "pE"}

    def op(self, eng, fn, inc=None, inc_by=1, serial=True):
        ss = self.SELF_SEM.get(eng)
        if ss is not None:
            rec = self.sems[ss]
            if serial and rec[1] > 0:
                self.steps.append(
                    (eng, lambda e, h=rec[0], t=rec[1]: e.wait_ge(h, t)))
            if inc is None:
                inc = ss
            assert inc == ss, f"compute op on {eng} must inc {ss}"
        if inc is not None:
            rec = self.sems[inc]
            rec[1] += inc_by
            self.steps.append(
                (eng, lambda e, fn=fn, h=rec[0], k=inc_by: fn(e).then_inc(h, k)))
        else:
            self.steps.append((eng, lambda e, fn=fn: fn(e)))

    def dma(self, eng, fn, inc):
        if inc is not None:
            rec = self.sems[inc]
            rec[1] += 16
            self.steps.append(
                (eng, lambda e, fn=fn, h=rec[0]: fn(e).then_inc(h, 16)))
        else:
            self.steps.append((eng, lambda e, fn=fn: fn(e)))

    def count(self, name):
        return self.sems[name][1]

    def finalize(self, block):
        for engname in ENGS:
            def make(engname=engname):
                def _(eng):
                    for e, fn in self.steps:
                        if e == engname:
                            fn(eng)
                return _
            getattr(block, engname)(make())


def sbt(nc, es, name, shape, dtype):
    return es.enter_context(nc.sbuf_tensor(name, shape, dtype))


def pst(nc, es, name, shape, dtype=F32):
    return es.enter_context(nc.psum_tensor(name, shape, dtype))


def ceil_div(a, b):
    return -(-a // b)



def emit_ln(b, x_tiles, xbf_tiles, T):
    scr = T["scr"]
    st = T["st"]
    for i, xt in enumerate(x_tiles):
        ssum, ssq = st[0], st[1]
        mean, var, m2, rstd = st[2], st[3], st[4], st[5]
        b.op("vector", lambda e, xt=xt, o=ssum: e.tensor_reduce(
            out=o, in_=xt, axis=AX, op=ALU.add))
        b.op("vector", lambda e, a=ssq: e.memset(a, 0.0), inc="vE")
        b.wait("scalar", "vE")
        b.op("scalar", lambda e, xt=xt, o=scr, a=ssq: e.activation(
            out=o, in_=xt, func=AF.Square, accum_out=a), inc="sE")
        b.wait("vector", "sE")
        b.op("vector", lambda e, i_=ssum, o=mean: e.tensor_scalar(
            out=o, in0=i_, scalar1=1.0 / D, scalar2=None, op0=ALU.mult))
        b.op("vector", lambda e, a=mean, o=m2: e.tensor_tensor(
            out=o, in0=a, in1=a, op=ALU.mult))
        b.op("vector", lambda e, i_=ssq, o=var: e.tensor_scalar(
            out=o, in0=i_, scalar1=1.0 / D, scalar2=None, op0=ALU.mult))
        b.op("vector", lambda e, o=var, a=m2: e.tensor_tensor(
            out=o, in0=o, in1=a, op=ALU.subtract), inc="vE")
        b.wait("scalar", "vE")
        b.op("scalar", lambda e, i_=var, o=rstd: e.activation(
            out=o, in_=i_, func=AF.Ln, bias=T["eps"]))
        b.op("scalar", lambda e, o=rstd: e.activation(
            out=o, in_=o, func=AF.Exp, scale=-0.5), inc="sE")
        b.wait("vector", "sE")
        b.op("vector", lambda e, xt=xt, m=mean, r=rstd: e.tensor_scalar(
            out=xt, in0=xt, scalar1=m, scalar2=r, op0=ALU.subtract, op1=ALU.mult),
            inc="vE")
        if xbf_tiles is not None:
            b.op("vector", lambda e, xt=xt, o=xbf_tiles[i]: e.tensor_copy(
                out=o, in_=xt), inc="vE")


def emit_tm_to_fm(b, xbf_tiles, xfm, T, ntiles):
    ident, p_tr = T["ident"], T["p_tr"]
    ntok = ntiles * 128
    for jt in range(ntiles):
        for dc in range(8):
            src = xbf_tiles[jt][:, dc * 128:(dc + 1) * 128]
            cons = b.count("vE")
            b.wait("tensor", "vE", cons)
            b.op("tensor", lambda e, s=src: e.transpose(p_tr[:], s, ident[:]),
                 inc="tE")
            b.wait("vector", "tE")
            dst = xfm[:, dc * ntok + jt * 128: dc * ntok + (jt + 1) * 128]
            b.op("vector", lambda e, d=dst: e.tensor_copy(out=d, in_=p_tr[:]),
                 inc="vE")


def emit_quant(b, xin, qout, T, P, F):
    nb = F // 16
    qs = T["qs"][0:P, 0:nb]
    qs2 = T["qs2"][0:P, 0:nb]
    qm = T["qm"][0:P, 0:F]
    qc = T["qc"][0:P, 0:F]
    qd = T["qd"][0:P, 0:F]
    qe = T["qe"][0:P, 0:F]

    def bview(ap):
        return ap.rearrange("p (b w) -> p b w", w=16)

    def sview(ap):
        return ap.rearrange("p (b o) -> p b o", o=1).broadcast_to((P, nb, 16))

    b.op("vector", lambda e: e.tensor_reduce(
        out=qs, in_=bview(xin), axis=AX, op=ALU.max, apply_absolute_value=True))
    b.op("vector", lambda e: e.tensor_scalar(
        out=qs, in0=qs, scalar1=1.0 / 6.0, scalar2=1e-12, op0=ALU.mult, op1=ALU.max))
    b.op("vector", lambda e: e.reciprocal(out=qs2, in_=qs))
    b.op("vector", lambda e: e.tensor_tensor(
        out=bview(qm), in0=bview(xin), in1=sview(qs2), op=ALU.mult))
    b.op("vector", lambda e: e.tensor_tensor(out=qc, in0=qm, in1=qm, op=ALU.mult))
    b.op("vector", lambda e: e.tensor_scalar(
        out=qd, in0=qc, scalar1=0.0625, scalar2=None, op0=ALU.is_gt))
    for t2 in (0.5625, 1.5625, 3.0625):
        b.op("vector", lambda e, t2=t2: e.scalar_tensor_tensor(
            out=qd, in0=qc, scalar=t2, in1=qd, op0=ALU.is_gt, op1=ALU.add))
    b.op("vector", lambda e: e.tensor_scalar(
        out=qe, in0=qc, scalar1=6.25, scalar2=None, op0=ALU.is_gt))
    for t2 in (12.25, 25.0, 25.0):
        b.op("vector", lambda e, t2=t2: e.scalar_tensor_tensor(
            out=qe, in0=qc, scalar=t2, in1=qe, op0=ALU.is_gt, op1=ALU.add))
    b.op("vector", lambda e: e.scalar_tensor_tensor(
        out=qc, in0=qd, scalar=0.5, in1=qe, op0=ALU.mult, op1=ALU.add))
    b.op("vector", lambda e: e.tensor_scalar(
        out=qe, in0=qm, scalar1=0.0, scalar2=2.0, op0=ALU.is_ge, op1=ALU.mult))
    b.op("vector", lambda e: e.tensor_scalar(
        out=qe, in0=qe, scalar1=1.0, scalar2=None, op0=ALU.subtract))
    b.op("vector", lambda e: e.tensor_tensor(
        out=bview(qd), in0=bview(qe), in1=sview(qs), op=ALU.mult))
    b.op("vector", lambda e: e.tensor_tensor(
        out=qout, in0=qc, in1=qd, op=ALU.mult), inc="vE")



def build_launch1():
    nc = bass.Bass("TRN2", target_bir_lowering=False)
    es = contextlib.ExitStack()

    def din(name, shape, dtype=F32):
        return nc.dram_tensor(name, shape, dtype, kind="ExternalInput").ap()

    def dout(name, shape, dtype=F32):
        return nc.dram_tensor(name, shape, dtype, kind="ExternalOutput").ap()

    x_fm = din("x_fm", [D, S])
    x_fm_own = din("x_fm_own", [D, TPC])
    x_tm = din("x_tm", [TPC, D])
    wql = din("wql", [D, D])
    wkl = din("wkl", [D, D])
    wvr = din("wvr", [D, D])
    wor = din("wor", [D, D])
    wg1 = din("wg1", [D, G_FF])
    bg1c = din("bg1c", [128, 2])
    wg2 = din("wg2", [G_FF, E_FF])
    w_in = din("w_in", [E_FF, D, H_FF])
    w_out = din("w_out", [E_FF, H_FF, D])
    masks = din("masks", [4, 128, 128], BF16)
    ident_d = din("ident", [128, 128], BF16)
    ones1f = din("ones1f", [1, 64])
    wg_sl = din("wg_sl", [WSL, D])
    wu_sl = din("wu_sl", [WSL, D])
    wd_sl = din("wd_sl", [D, WSL])

    x1_out = dout("x1", [TPC, D])
    if DEBUG:
        dbg_pre = dout("dbg_pre", [TPC, D])
        dbg_k = dout("dbg_k", [128, 512], BF16)
        dbg_q = dout("dbg_q", [128, 512], BF16)
        dbg_v = dout("dbg_v", [128, 16 * 65], BF16)
        dbg_at = dout("dbg_at", [128, QW], BF16)
        dbg_gate = dout("dbg_gate", [128, E_FF])
        dbg_xfm = dout("dbg_xfm", [128, 512], BF16)
        dbg_gfm = dout("dbg_gfm", [128, 512], BF16)
        dbg_log = dout("dbg_log", [128, E_FF])
        dbg_ln0 = dout("dbg_ln0", [128, D])
        dbg_st = dout("dbg_st", [128, 8])
        dbg_scr = dout("dbg_scr", [128, D])
        dbg_ln1 = dout("dbg_ln1", [128, D])
    wgT_q = dout("wgT_q", [D, WSL], BF16)
    wuT_q = dout("wuT_q", [D, WSL], BF16)
    wdT_q = dout("wdT_q", [WSL, D], BF16)

    K0 = nc.dram_tensor("K0", [D, S], BF16).ap()
    V0 = nc.dram_tensor("V0", [S, 16 * 65], BF16).ap()
    Q0 = nc.dram_tensor("Q0", [D, TPC], BF16).ap()

    with es:
        b = Builder(nc, es)
        for s in ("dS", "dC", "dG", "tE", "vE", "sE", "pE"):
            b.sem(s)

        T = {}
        AW = sbt(nc, es, "AW", [128, 24576], BF16)
        AB = sbt(nc, es, "AB", [128, 8192], BF16)
        wq_v = AW[:, 0:8192]
        wk_v = AW[:, 8192:16384]
        wv_v = AW[:, 16384:24576]
        wo_v = AW[:, 0:8192]
        win_v = AW[:, 0:16384]
        wouth_v = AW[:, 16384:24576]
        s_gfm = AW[:, 0:2 * TPC]
        s_xc = [AB[:, 0:4096], AB[:, 4096:8192]]
        h_v = AB[:, 0:8192]

        s_x = [sbt(nc, es, f"s_x{i}", [128, D], F32) for i in range(NT)]
        s_xfm = sbt(nc, es, "s_xfm", [128, 8 * TPC], BF16)
        s_tbf = sbt(nc, es, "s_tbf", [128, D], BF16)
        s_kv = [sbt(nc, es, f"s_kv{i}", [128, 512], BF16) for i in range(2)]
        s_vst = [sbt(nc, es, f"s_vst{i}", [128, 16 * 65], BF16) for i in range(2)]
        s_qt = [sbt(nc, es, f"s_qt{i}", [128, 512], BF16) for i in range(2)]
        s_kt = [sbt(nc, es, f"s_kt{i}", [128, 512], BF16) for i in range(2)]
        s_vt = [sbt(nc, es, f"s_vt{i}", [128, 4 * 130], BF16) for i in range(2)]
        s_pt = [sbt(nc, es, f"s_pt{i}", [128, 512], BF16) for i in range(4)]
        s_mask = sbt(nc, es, "s_mask", [128, 4 * 128], BF16)
        s_attn = [sbt(nc, es, f"s_attn{p}", [128, QW], BF16) for p in range(8)]
        s_recip = sbt(nc, es, "s_recip", [1, QW], F32)
        s_bcast = sbt(nc, es, "s_bcast", [64, QW], F32)
        s_ones1 = sbt(nc, es, "s_ones1", [1, 64], F32)
        T["ident"] = sbt(nc, es, "s_ident", [128, 128], BF16)
        T["st"] = [sbt(nc, es, f"s_st{i}", [128, 1], F32)[:] for i in range(8)]
        T["eps"] = T["st"][6]
        s_wg1 = sbt(nc, es, "s_wg1", [128, 8 * G_FF], BF16)
        s_bg1 = sbt(nc, es, "s_bg1", [128, 2], F32)
        s_wg2 = sbt(nc, es, "s_wg2", [128, 2 * E_FF], BF16)
        s_gate = [sbt(nc, es, f"s_gate{i}", [128, E_FF], F32) for i in range(NT)]
        s_gsc = sbt(nc, es, "s_gsc", [128, 32], F32)
        T["qs"] = sbt(nc, es, "s_qs", [128, 64], F32)
        T["qs2"] = sbt(nc, es, "s_qs2", [128, 64], F32)
        T["qm"] = s_xfm[:, 0:1024].bitcast(F32)
        T["qc"] = s_xfm[:, 1024:2048].bitcast(F32)
        T["qd"] = s_xfm[:, 2048:3072].bitcast(F32)
        T["qe"] = s_xfm[:, 3072:4096].bitcast(F32)
        s_qin = sbt(nc, es, "s_qin", [128, D], F32)
        T["scr"] = s_qin[:]
        s_qq = sbt(nc, es, "s_qq", [128, D], BF16)
        s_qst = sbt(nc, es, "s_qst", [128, 128], BF16)

        p_yt = pst(nc, es, "p_y", [128, 1024])
        p_y = p_yt[:]
        p_s = [pst(nc, es, "p_s0", [128, 512])[:], pst(nc, es, "p_s1", [128, 512])[:],
               p_yt[:, 0:512], p_yt[:, 512:1024]]
        p_o = [pst(nc, es, f"p_o{i}", [65, 512]) for i in range(2)]
        p_mix = pst(nc, es, "p_mix", [128, 512])
        p_bc = p_mix[0:64, 0:512]
        p_g = p_mix[0:128, 0:16]
        T["p_tr"] = pst(nc, es, "p_tr", [128, 128], BF16)[:]

        with nc.Block() as block:
            for i in range(NT):
                b.dma("scalar", lambda e, i=i: e.dma_start(
                    out=s_x[i][:], in_=x_tm[i * 128:(i + 1) * 128, :]), inc="dC")
            b.dma("scalar", lambda e: e.dma_start(
                out=s_mask[:].rearrange("p (s q) -> p s q", s=4),
                in_=masks.rearrange("s p q -> p s q")), inc="dC")
            b.dma("scalar", lambda e: e.dma_start(out=T["ident"][:], in_=ident_d), inc="dC")
            b.dma("scalar", lambda e: e.dma_start(out=s_ones1[:], in_=ones1f), inc="dC")
            b.dma("scalar", lambda e: e.dma_start(out=s_bg1[:], in_=bg1c), inc="dC")
            for w_d, w_s in [(wql, wq_v), (wkl, wk_v), (wvr, wv_v)]:
                b.dma("gpsimd", lambda e, wd=w_d, ws=w_s: e.dma_start(
                    out=ws.rearrange("p (c f) -> p c f", f=D),
                    in_=wd.rearrange("(c p) f -> p c f", p=128)), inc="dG")
            b.dma("gpsimd", lambda e: e.dma_start(
                out=s_wg1[:].rearrange("p (c f) -> p c f", f=G_FF),
                in_=wg1.rearrange("(c p) f -> p c f", p=128)), inc="dG")
            b.dma("gpsimd", lambda e: e.dma_start(
                out=s_wg2[:].rearrange("p (c f) -> p c f", f=E_FF),
                in_=wg2.rearrange("(c p) f -> p c f", p=128)), inc="dG")
            for i in range(2):
                b.op("gpsimd", lambda e, i=i: e.memset(
                    s_vst[i][:].rearrange("p (h c) -> p h c", c=65)[:, :, 64:65], 1.0),
                    inc="pE")
            b.op("gpsimd", lambda e: e.memset(T["eps"], 1e-5), inc="pE")
            b.wait("scalar", "pE")
            consts_dC = b.count("dC")
            consts_dG = b.count("dG")
            consts_pE = b.count("pE")

            b.wait("scalar", "dC", consts_dC)

            def quant_transpose_out(src_dram, out_dram, rows):
                cols = src_dram.shape[1]
                ntr = ceil_div(rows, 128)
                for rt in range(ntr):
                    p0, p1 = rt * 128, min(rows, (rt + 1) * 128)
                    P = p1 - p0
                    b.wait("scalar", "vE")
                    b.wait("scalar", "tE")
                    b.dma("scalar", lambda e, s=src_dram, p0=p0, p1=p1: e.dma_start(
                        out=s_qin[0:p1 - p0, 0:cols], in_=s[p0:p1, :]), inc="dC")
                    b.wait("vector", "dC")
                    for c0q in range(0, cols, 512):
                        cw = min(512, cols - c0q)
                        emit_quant(b, s_qin[0:P, c0q:c0q + cw],
                                   s_qq[0:P, c0q:c0q + cw], T, P, cw)
                    nct = ceil_div(cols, 128)
                    for ct in range(nct):
                        c0, c1 = ct * 128, min(cols, (ct + 1) * 128)
                        cons = b.count("dC")
                        b.wait("tensor", "vE")
                        b.wait("tensor", "dC", cons)
                        b.op("tensor", lambda e, P=P, c0=c0, c1=c1: e.transpose(
                            T["p_tr"][0:c1 - c0, 0:P], s_qq[0:P, c0:c1],
                            T["ident"][0:P, 0:P]), inc="tE")
                        b.wait("vector", "tE")
                        b.op("vector", lambda e, c0=c0, c1=c1, P=P: e.tensor_copy(
                            out=s_qst[0:c1 - c0, 0:P], in_=T["p_tr"][0:c1 - c0, 0:P]),
                            inc="vE")
                        b.wait("scalar", "vE")
                        b.dma("scalar", lambda e, o=out_dram, c0=c0, c1=c1, p0=p0, p1=p1:
                              e.dma_start(out=o[c0:c1, p0:p1],
                                          in_=s_qst[0:c1 - c0, 0:p1 - p0]), inc="dC")

            quant_transpose_out(wg_sl, wgT_q, WSL)
            quant_transpose_out(wu_sl, wuT_q, WSL)
            quant_transpose_out(wd_sl, wdT_q, D)

            b.wait("tensor", "dG", consts_dG)
            b.wait("tensor", "pE", consts_pE)
            xc_free = [0, 0]
            kbank_free = [0, 0]
            ybank_free = 0
            kvstage_free = [0, 0]
            vstage_free = [0, 0]

            def proj_fm(wview, kc, slot, dst_dram, dst_cols):
                for pc in range(8):
                    bank = p_s[pc % 2]
                    b.wait("tensor", "vE", kbank_free[pc % 2])
                    for dc in range(8):
                        lhsT = wview[:, dc * 1024 + pc * 128: dc * 1024 + (pc + 1) * 128]
                        rhs = s_xc[slot][:, dc * 512:(dc + 1) * 512]
                        b.op("tensor", lambda e, o=bank, l=lhsT, r=rhs,
                             st=(dc == 0), sp=(dc == 7): e.matmul(o, l, r, start=st, stop=sp),
                             inc="tE" if dc == 7 else None)
                    b.wait("vector", "tE")
                    b.wait("vector", "dC", kvstage_free[pc % 2])
                    stg = s_kv[pc % 2]
                    b.op("vector", lambda e, p=bank, s=stg: e.tensor_copy(
                        out=s[:], in_=p[:]), inc="vE")
                    kbank_free[pc % 2] = b.count("vE")
                    b.wait("scalar", "vE")
                    b.dma("scalar", lambda e, s=stg, pc=pc: e.dma_start(
                        out=dst_dram[pc * 128:(pc + 1) * 128,
                                     kc * dst_cols:(kc + 1) * dst_cols],
                        in_=s[:]), inc="dC")
                    kvstage_free[pc % 2] = b.count("dC")

            for kc in range(16):
                slot = kc % 2
                b.wait("gpsimd", "tE", xc_free[slot])
                for dc in range(8):
                    src = x_fm[dc * 128:(dc + 1) * 128, kc * 512:(kc + 1) * 512]
                    b.dma("gpsimd", lambda e, s=src, d=s_xc[slot], dc=dc: e.dma_start(
                        out=d[:, dc * 512:(dc + 1) * 512], in_=s), inc="dG")
                b.wait("tensor", "dG")
                proj_fm(wk_v, kc, slot, K0, 512)
                for sub in range(4):
                    b.wait("tensor", "vE", ybank_free)
                    for half in range(2):
                        for dc in range(8):
                            lhsT = s_xc[slot][:, dc * 512 + sub * 128: dc * 512 + (sub + 1) * 128]
                            rhs = wv_v[:, dc * 1024 + half * 512: dc * 1024 + (half + 1) * 512]
                            b.op("tensor", lambda e, l=lhsT, r=rhs, st=(dc == 0),
                                 sp=(dc == 7), hf=half: e.matmul(
                                     p_y[:, hf * 512:(hf + 1) * 512], l, r, start=st, stop=sp),
                                 inc="tE" if (dc == 7 and half == 1) else None)
                    b.wait("vector", "tE")
                    b.wait("vector", "dC", vstage_free[sub % 2])
                    vstg = s_vst[sub % 2]
                    dstv = vstg[:].rearrange("p (h c) -> p h c", c=65)[:, :, 0:64]
                    b.op("vector", lambda e, d=dstv: e.tensor_copy(
                        out=d, in_=p_y.rearrange("p (h c) -> p h c", c=64)), inc="vE")
                    ybank_free = b.count("vE")
                    b.wait("scalar", "vE")
                    row0 = kc * 512 + sub * 128
                    b.dma("scalar", lambda e, s=vstg, r0=row0: e.dma_start(
                        out=V0[r0:r0 + 128, :], in_=s[:]), inc="dC")
                    vstage_free[sub % 2] = b.count("dC")
                xc_free[slot] = b.count("tE")

            for Jc in range(NCH):
                slot = Jc % 2
                b.wait("tensor", "vE", ybank_free)
                b.wait("gpsimd", "tE", xc_free[slot])
                for dc in range(8):
                    src = x_fm_own[dc * 128:(dc + 1) * 128, Jc * 512:(Jc + 1) * 512]
                    b.dma("gpsimd", lambda e, s=src, d=s_xc[slot], dc=dc: e.dma_start(
                        out=d[:, dc * 512:(dc + 1) * 512], in_=s), inc="dG")
                b.wait("tensor", "dG")
                proj_fm(wq_v, Jc, slot, Q0, 512)
                xc_free[slot] = b.count("tE")
            p1_dC = b.count("dC")

            p1_tE = b.count("tE")
            b.wait("gpsimd", "tE", p1_tE)
            b.dma("gpsimd", lambda e: e.dma_start(
                out=wo_v.rearrange("p (c f) -> p c f", f=D),
                in_=wor.rearrange("(c p) f -> p c f", p=128)), inc="dG")
            wo_dG = b.count("dG")

            b.wait("sync", "dC", p1_dC)
            b.wait("tensor", "dG", wo_dG)
            kt_free = [0, 0]
            vt_free = [0, 0]
            qt_free = [0, 0]
            sb_free = [0, 0]
            pt_free = [0, 0]
            att_free = [0] * 8
            for Jc in range(NCH):
                kmax = 16 * Jc + 16
                for pc in range(8):
                    qslot = pc % 2
                    b.wait("tensor", "vE")
                    b.wait("sync", "tE", qt_free[qslot])
                    b.dma("sync", lambda e, pc=pc, Jc=Jc, d=s_qt[qslot]: e.dma_start(
                        out=d[:], in_=Q0[pc * 128:(pc + 1) * 128,
                                         Jc * 512:(Jc + 1) * 512]), inc="dS")
                    qld = b.count("dS")
                    b.wait("tensor", "dS", qld)
                    for kg in range(kmax // 4):
                        kslot = kg % 2
                        b.wait("sync", "tE", kt_free[kslot])
                        b.dma("sync", lambda e, pc=pc, kg=kg, d=s_kt[kslot]: e.dma_start(
                            out=d[:], in_=K0[pc * 128:(pc + 1) * 128,
                                             kg * 512:(kg + 1) * 512]), inc="dS")
                        kld = b.count("dS")
                        b.wait("sync", "tE", vt_free[kslot])
                        srcv = V0[kg * 512:(kg + 1) * 512, 2 * pc * 65:(2 * pc + 2) * 65]
                        b.dma("sync", lambda e, s=srcv, d=s_vt[kslot]: e.dma_start(
                            out=d[:].rearrange("p (g c) -> p g c", g=4),
                            in_=s.rearrange("(g p) c -> p g c", p=128)), inc="dS")
                        vld = b.count("dS")
                        for ki in range(4):
                            kt = kg * 4 + ki
                            j0 = max(0, ceil_div(kt - 3, 4))
                            c0 = max(0, (j0 - 4 * Jc)) * 128
                            sbank = [p_s[2 * (kt % 2)], p_s[2 * (kt % 2) + 1]]
                            b.wait("tensor", "dS", kld)
                            b.wait("tensor", "sE", sb_free[kt % 2])
                            for hh in range(2):
                                lhsT = s_kt[kslot][hh * 64:(hh + 1) * 64,
                                                   ki * 128:(ki + 1) * 128]
                                rhs = s_qt[qslot][hh * 64:(hh + 1) * 64, c0:512]
                                b.op("tensor", lambda e, o=sbank[hh], l=lhsT, r=rhs, c0=c0:
                                     e.matmul(o[:, c0:512], l, r, start=True, stop=True),
                                     inc="tE")
                            mm_s = b.count("tE")
                            b.wait("scalar", "tE", mm_s)
                            b.wait("scalar", "tE", pt_free[kt % 2])
                            for hh in range(2):
                                b.op("scalar", lambda e, p=sbank[hh],
                                     o=s_pt[2 * (kt % 2) + hh], c0=c0:
                                     e.activation(out=o[:, c0:512], in_=p[:, c0:512],
                                                  func=AF.Exp, scale=0.125), inc="sE")
                            sb_free[kt % 2] = b.count("sE")
                            exp_n = b.count("sE")
                            need_mask = j0 >= 4 * Jc
                            if need_mask:
                                sstar = kt - 4 * j0
                                b.wait("vector", "sE", exp_n)
                                for hh in range(2):
                                    b.op("vector", lambda e, o=s_pt[2 * (kt % 2) + hh],
                                         c0=c0, ss=sstar: e.tensor_tensor(
                                             out=o[:, c0:c0 + 128], in0=o[:, c0:c0 + 128],
                                             in1=s_mask[:, ss * 128:(ss + 1) * 128],
                                             op=ALU.mult), inc="vE")
                                b.wait("tensor", "vE")
                            else:
                                b.wait("tensor", "sE", exp_n)
                            b.wait("tensor", "dS", vld)
                            for hh in range(2):
                                lhsT = s_vt[kslot][:, ki * 130 + hh * 65: ki * 130 + (hh + 1) * 65]
                                rhs = s_pt[2 * (kt % 2) + hh][:, c0:512]
                                b.op("tensor", lambda e, o=p_o[hh], l=lhsT, r=rhs, c0=c0,
                                     st=(kt == 0), sp=(kt == kmax - 1): e.matmul(
                                         o[:, c0:512], l, r, start=st, stop=sp),
                                     inc="tE")
                            pt_free[kt % 2] = b.count("tE")
                        kt_free[kslot] = b.count("tE")
                        vt_free[kslot] = b.count("tE")
                    qt_free[qslot] = b.count("tE")
                    for hh in range(2):
                        b.wait("vector", "tE")
                        b.op("vector", lambda e, p=p_o[hh]: e.reciprocal(
                            out=s_recip[:], in_=p[64:65, :]), inc="vE")
                        rec_n = b.count("vE")
                        b.wait("tensor", "vE", rec_n)
                        b.op("tensor", lambda e: e.matmul(
                            p_bc, s_ones1[:], s_recip[:], start=True, stop=True),
                            inc="tE")
                        bc_n = b.count("tE")
                        b.wait("vector", "tE", bc_n)
                        b.op("vector", lambda e: e.tensor_copy(
                            out=s_bcast[:], in_=p_bc), inc="vE")
                        b.wait("vector", "tE", att_free[pc])
                        b.op("vector", lambda e, p=p_o[hh], pc=pc, hh=hh: e.tensor_tensor(
                            out=s_attn[pc][hh * 64:(hh + 1) * 64, :],
                            in0=p[0:64, :], in1=s_bcast[:], op=ALU.mult), inc="vE")
                attn_ready = b.count("vE")
                b.wait("tensor", "vE", attn_ready)
                for sub in range(4):
                    b.wait("tensor", "vE", ybank_free)
                    for pc in range(8):
                        for half in range(2):
                            lhsT = s_attn[pc][:, sub * 128:(sub + 1) * 128]
                            rhs = wo_v[:, pc * 1024 + half * 512: pc * 1024 + (half + 1) * 512]
                            b.op("tensor", lambda e, l=lhsT, r=rhs, hf=half,
                                 st=(pc == 0), sp=(pc == 7): e.matmul(
                                     p_y[:, hf * 512:(hf + 1) * 512], l, r,
                                     start=st, stop=sp),
                                 inc="tE" if (pc == 7 and half == 1) else None)
                    for pc in range(8):
                        att_free[pc] = b.count("tE")
                    b.wait("vector", "tE")
                    ti = Jc * 4 + sub
                    b.op("vector", lambda e, ti=ti: e.tensor_tensor(
                        out=s_x[ti][:], in0=p_y, in1=s_x[ti][:], op=ALU.add),
                        inc="vE")
                    ybank_free = b.count("vE")

            if DEBUG:
                pre_v = b.count("vE")
                b.wait("scalar", "vE", pre_v)
                for i in range(NT):
                    b.dma("scalar", lambda e, i=i: e.dma_start(
                        out=dbg_pre[i * 128:(i + 1) * 128, :], in_=s_x[i][:]), inc="dC")
                b.dma("scalar", lambda e: e.dma_start(out=dbg_k, in_=K0[0:128, 0:512]), inc="dC")
                b.dma("scalar", lambda e: e.dma_start(out=dbg_q, in_=Q0[0:128, 0:512]), inc="dC")
                b.dma("scalar", lambda e: e.dma_start(out=dbg_v, in_=V0[0:128, :]), inc="dC")
                b.dma("scalar", lambda e: e.dma_start(out=dbg_at, in_=s_attn[0][:]), inc="dC")
            for i in range(NT):
                b.wait("vector", "tE")
                emit_ln(b, [s_x[i][:]], [s_tbf[:]], T)
                if DEBUG and i == 0:
                    dn = b.count("vE")
                    b.wait("scalar", "vE", dn)
                    b.wait("scalar", "sE")
                    b.dma("scalar", lambda e: e.dma_start(out=dbg_scr, in_=T["scr"]), inc="dC")
                    for jj in range(8):
                        b.op("vector", lambda e, jj=jj: e.tensor_copy(
                            out=s_qin[:, 512 + jj:512 + jj + 1], in_=T["st"][jj]), inc="vE")
                    b.wait("scalar", "vE")
                    b.dma("scalar", lambda e: e.dma_start(
                        out=dbg_st, in_=s_qin[:, 512:520]), inc="dC")
                for dc in range(8):
                    srct = s_tbf[:, dc * 128:(dc + 1) * 128]
                    cons = b.count("vE")
                    b.wait("tensor", "vE", cons)
                    b.op("tensor", lambda e, s=srct: e.transpose(
                        T["p_tr"], s, T["ident"][:]), inc="tE")
                    b.wait("vector", "tE")
                    dst = s_xfm[:, dc * TPC + i * 128: dc * TPC + (i + 1) * 128]
                    b.op("vector", lambda e, d=dst: e.tensor_copy(
                        out=d, in_=T["p_tr"]), inc="vE")
            fm_ready_vE = b.count("vE")
            if DEBUG:
                b.wait("scalar", "vE", fm_ready_vE)
                b.dma("scalar", lambda e: e.dma_start(out=dbg_xfm, in_=s_xfm[:, 0:512]), inc="dC")
                b.dma("scalar", lambda e: e.dma_start(out=dbg_ln0, in_=s_x[0][:]), inc="dC")
                b.dma("scalar", lambda e: e.dma_start(out=dbg_ln1, in_=s_x[1][:]), inc="dC")

            b.wait("tensor", "vE", fm_ready_vE)
            glog_free = 0
            for Jc in range(NCH):
                for gc in range(2):
                    bank = p_s[gc]
                    b.wait("tensor", "sE", sb_free[gc])
                    for dc in range(8):
                        lhsT = s_wg1[:, dc * G_FF + gc * 128: dc * G_FF + (gc + 1) * 128]
                        rhs = s_xfm[:, dc * TPC + Jc * 512: dc * TPC + (Jc + 1) * 512]
                        b.op("tensor", lambda e, o=bank, l=lhsT, r=rhs, st=(dc == 0),
                             sp=(dc == 7): e.matmul(o, l, r, start=st, stop=sp),
                             inc="tE")
                    b.wait("scalar", "tE")
                    b.op("scalar", lambda e, p=bank, gc=gc, Jc=Jc: e.activation(
                        out=s_gfm[:, gc * TPC + Jc * 512: gc * TPC + (Jc + 1) * 512],
                        in_=p[:], func=AF.Gelu, bias=s_bg1[:, gc:gc + 1]), inc="sE")
                    sb_free[gc] = b.count("sE")
            gelu_all = b.count("sE")
            b.wait("tensor", "sE", gelu_all)
            for ti in range(NT):
                b.wait("tensor", "vE", glog_free)
                for gc in range(2):
                    lhsT = s_gfm[:, gc * TPC + ti * 128: gc * TPC + (ti + 1) * 128]
                    rhs = s_wg2[:, gc * E_FF:(gc + 1) * E_FF]
                    b.op("tensor", lambda e, l=lhsT, r=rhs, st=(gc == 0), sp=(gc == 1):
                         e.matmul(p_g[:, 0:E_FF], l, r, start=st, stop=sp),
                         inc="tE" if gc == 1 else None)
                b.wait("vector", "tE")
                g0 = s_gsc[:, 0:8]
                gex = s_gsc[:, 8:16]
                gpr = s_gsc[:, 16:24]
                st0 = T["st"]
                m1, negm = st0[0], st0[3]
                sm, rc = st0[1], st0[2]
                m1p, m2 = st0[4], st0[5]
                b.op("vector", lambda e: e.tensor_copy(out=g0, in_=p_g[:, 0:E_FF]),
                     inc="vE")
                glog_free = b.count("vE")
                if DEBUG and ti == 0:
                    b.wait("scalar", "vE", glog_free)
                    b.dma("scalar", lambda e: e.dma_start(out=dbg_log, in_=g0), inc="dC")
                b.op("vector", lambda e: e.tensor_reduce(
                    out=m1, in_=g0, axis=AX, op=ALU.max))
                b.op("vector", lambda e: e.tensor_scalar(
                    out=negm, in0=m1, scalar1=-1.0, scalar2=None, op0=ALU.mult),
                    inc="vE")
                b.wait("scalar", "vE")
                b.op("scalar", lambda e: e.activation(
                    out=gex, in_=g0, func=AF.Exp, bias=negm), inc="sE")
                b.wait("vector", "sE")
                b.op("vector", lambda e: e.tensor_reduce(
                    out=sm, in_=gex, axis=AX, op=ALU.add))
                b.op("vector", lambda e: e.reciprocal(out=rc, in_=sm))
                b.op("vector", lambda e: e.tensor_scalar(
                    out=gpr, in0=gex, scalar1=rc, scalar2=None, op0=ALU.mult))
                b.op("vector", lambda e: e.tensor_reduce(
                    out=m1p, in_=gpr, axis=AX, op=ALU.max))
                b.op("vector", lambda e: e.tensor_scalar(
                    out=gex, in0=gpr, scalar1=m1p, scalar2=None, op0=ALU.is_lt))
                b.op("vector", lambda e: e.tensor_tensor(
                    out=gex, in0=gex, in1=gpr, op=ALU.mult))
                b.op("vector", lambda e: e.tensor_reduce(
                    out=m2, in_=gex, axis=AX, op=ALU.max))
                b.op("vector", lambda e: e.tensor_scalar(
                    out=gex, in0=gpr, scalar1=m2, scalar2=None, op0=ALU.is_ge))
                b.op("vector", lambda e, ti=ti: e.tensor_tensor(
                    out=s_gate[ti][:], in0=gex, in1=gpr, op=ALU.mult), inc="vE")

            h_free = 0
            w_in_r = w_in.rearrange("e (c p) f -> p e c f", p=128)
            w_out_r = w_out.rearrange("e (c p) (h f) -> p e c h f", p=128, h=2)
            for ex in range(E_FF):
                b.wait("gpsimd", "tE")
                b.dma("gpsimd", lambda e, ex=ex: e.dma_start(
                    out=win_v.rearrange("p (o c f) -> p o c f", o=1, f=H_FF),
                    in_=w_in_r[:, ex:ex + 1]), inc="dG")
                wld = b.count("dG")
                b.wait("tensor", "dG", wld)
                for tt in range(NCH):
                    b.wait("tensor", "tE", h_free)
                    for hf in range(16):
                        bank = p_s[hf % 2]
                        b.wait("tensor", "sE", sb_free[hf % 2])
                        for dc in range(8):
                            lhsT = win_v[:, dc * H_FF + hf * 128: dc * H_FF + (hf + 1) * 128]
                            rhs = s_xfm[:, dc * TPC + tt * 512: dc * TPC + (tt + 1) * 512]
                            b.op("tensor", lambda e, o=bank, l=lhsT, r=rhs, st=(dc == 0),
                                 sp=(dc == 7): e.matmul(o, l, r, start=st, stop=sp),
                                 inc="tE")
                        b.wait("scalar", "tE")
                        b.op("scalar", lambda e, p=bank, hf=hf: e.activation(
                            out=h_v[:, hf * 512:(hf + 1) * 512], in_=p[:], func=AF.Gelu),
                            inc="sE")
                        sb_free[hf % 2] = b.count("sE")
                    hdone = b.count("sE")
                    b.wait("tensor", "sE", hdone)
                    for half in range(2):
                        b.wait("gpsimd", "tE")
                        b.dma("gpsimd", lambda e, ex=ex, half=half: e.dma_start(
                            out=wouth_v.rearrange("p (a c o f) -> p a c o f",
                                                  a=1, o=1, f=512),
                            in_=w_out_r[:, ex:ex + 1, :, half:half + 1, :]), inc="dG")
                        whld = b.count("dG")
                        b.wait("tensor", "dG", whld)
                        for sub in range(4):
                            ti = tt * 4 + sub
                            b.wait("tensor", "vE", ybank_free)
                            for hf in range(16):
                                lhsT = h_v[:, hf * 512 + sub * 128: hf * 512 + (sub + 1) * 128]
                                rhs = wouth_v[:, hf * 512:(hf + 1) * 512]
                                b.op("tensor", lambda e, l=lhsT, r=rhs, half=half,
                                     st=(hf == 0), sp=(hf == 15): e.matmul(
                                         p_y[:, half * 512:(half + 1) * 512], l, r,
                                         start=st, stop=sp),
                                     inc="tE" if hf == 15 else None)
                            b.wait("vector", "tE")
                            b.op("vector", lambda e, ti=ti, ex=ex, half=half:
                                 e.scalar_tensor_tensor(
                                     out=s_x[ti][:, half * 512:(half + 1) * 512],
                                     in0=p_y[:, half * 512:(half + 1) * 512],
                                     scalar=s_gate[ti][:, ex:ex + 1],
                                     in1=s_x[ti][:, half * 512:(half + 1) * 512],
                                     op0=ALU.mult, op1=ALU.add), inc="vE")
                            ybank_free = b.count("vE")
                    h_free = b.count("tE")

            if DEBUG:
                gv = b.count("vE")
                b.wait("scalar", "vE", gv)
                b.dma("scalar", lambda e: e.dma_start(out=dbg_gate, in_=s_gate[0][:]), inc="dC")
                b.dma("scalar", lambda e: e.dma_start(out=dbg_gfm, in_=s_gfm[:, 0:512]), inc="dC")
            emit_ln(b, [s_x[i][:] for i in range(NT)], None, T)
            lnv = b.count("vE")
            b.wait("scalar", "vE", lnv)
            for i in range(NT):
                b.dma("scalar", lambda e, i=i: e.dma_start(
                    out=x1_out[i * 128:(i + 1) * 128, :], in_=s_x[i][:]), inc="dC")
            b.wait("scalar", "dC", b.count("dC"))
            b.finalize(block)
    return nc



def build_launch2():
    nc = bass.Bass("TRN2", target_bir_lowering=False)
    es = contextlib.ExitStack()

    def din(name, shape, dtype=F32):
        return nc.dram_tensor(name, shape, dtype, kind="ExternalInput").ap()

    def dout(name, shape, dtype=F32):
        return nc.dram_tensor(name, shape, dtype, kind="ExternalOutput").ap()

    x_fm = din("x_fm", [D, S])
    x_fm_own = din("x_fm_own", [D, TPC])
    x_tm = din("x_tm", [TPC, D])
    wqT = din("wqT", [D, D])
    wkT = din("wkT", [D, D])
    wvT = din("wvT", [D, D])
    woT = din("woT", [D, D])
    bq_c = din("bq_c", [128, 8])
    bk_c = din("bk_c", [128, 8])
    bv_r = din("bv_r", [1, D])
    bo_r = din("bo_r", [1, D])
    wgq = din("wgq", [D, I_FF], BF16)
    wuq = din("wuq", [D, I_FF], BF16)
    wdq = din("wdq", [I_FF, D], BF16)
    band = din("band", [128, 31], BF16)
    ident_d = din("ident", [128, 128], BF16)
    ones1f = din("ones1f", [1, 64])
    ones1b = din("ones1b", [1, 128], BF16)
    identf_d = din("identf", [128, 128])

    y_out = dout("y", [TPC, D])
    if DEBUG:
        dbg_k2 = dout("dbg_k2", [128, NKT * 16], BF16)
        dbg_q1 = dout("dbg_q1", [65, 512], BF16)
        dbg_pt = dout("dbg_pt", [128, 512], BF16)
        dbg_den = dout("dbg_den", [1, 512])
        dbg_k2d = dout("dbg_k2d", [16, 512], BF16)
        dbg_xq = dout("dbg_xq", [128, 1024], BF16)
        dbg_hq = dout("dbg_hq", [128, 2816], BF16)
        dbg_g = dout("dbg_g", [128, 512])
        dbg_up = dout("dbg_up", [128, 512])

    K1 = nc.dram_tensor("K1", [D, S], BF16).ap()
    K2D = nc.dram_tensor("K2D", [16, S], BF16).ap()
    V1 = nc.dram_tensor("V1", [S, 16 * 65], BF16).ap()
    Q1 = nc.dram_tensor("Q1", [16 * 65, TPC], BF16).ap()

    JCH = [512, 512, 512, 512, 512, 256]
    JOF = [0, 512, 1024, 1536, 2048, 2560]

    with es:
        b = Builder(nc, es)
        for s in ("dS", "dC", "dG", "tE", "vE", "sE", "pE"):
            b.sem(s)

        T = {}
        AW = sbt(nc, es, "AW", [128, 24576], BF16)
        AB = sbt(nc, es, "AB", [128, 8192], BF16)
        wq_v = AW[:, 0:8192]
        wk_v = AW[:, 8192:16384]
        wv_v = AW[:, 16384:24576]
        wo_v = AW[:, 0:8192]
        wgc_v = [AW[:, 0:4096], AW[:, 4096:8192]]
        wuc_v = [AW[:, 8192:12288], AW[:, 12288:16384]]
        wdc_v = [AW[:, 16384:20480], AW[:, 16384:20480]]
        s_xc = [AB[:, 0:4096], AB[:, 4096:8192]]
        hq_fm = AB[:, 0:5632]

        s_x = [sbt(nc, es, f"s_x{i}", [128, D], F32) for i in range(NT)]
        s_xfm = sbt(nc, es, "s_xfm", [128, 8 * TPC], BF16)
        s_tbf = sbt(nc, es, "s_tbf", [128, D], BF16)
        s_kv = [sbt(nc, es, f"s_kv{i}", [128, 512], BF16) for i in range(2)]
        s_vst = [sbt(nc, es, f"s_vst{i}", [128, 16 * 65], BF16) for i in range(2)]
        s_qp = sbt(nc, es, "s_qp", [65, 512], BF16)
        s_ka = [sbt(nc, es, f"s_ka{i}", [65, 512], BF16) for i in range(2)]
        s_vt = [sbt(nc, es, f"s_vt{i}", [128, 4 * 65], BF16) for i in range(2)]
        s_pt = [sbt(nc, es, f"s_pt{i}", [128, 512], BF16) for i in range(2)]

        s_attn = [sbt(nc, es, f"s_attn{p}", [128, QW], BF16) for p in range(8)]
        s_recip = sbt(nc, es, "s_recip", [1, QW], F32)
        s_bcast = sbt(nc, es, "s_bcast", [64, QW], F32)
        s_ones1 = sbt(nc, es, "s_ones1", [1, 64], F32)
        s_ones1b = sbt(nc, es, "s_ones1b", [1, 128], BF16)
        s_band = sbt(nc, es, "s_band", [128, 31], BF16)
        s_identf = sbt(nc, es, "s_identf", [128, 128], F32)
        s_bq = sbt(nc, es, "s_bq", [128, 8], F32)
        s_bk = sbt(nc, es, "s_bk", [128, 8], F32)
        s_bv = sbt(nc, es, "s_bv", [1, D], BF16)
        s_bo = sbt(nc, es, "s_bo", [1, D], BF16)
        s_k2 = sbt(nc, es, "s_k2", [128, NKT * 16], BF16)
        T["ident"] = sbt(nc, es, "s_ident", [128, 128], BF16)
        T["st"] = [sbt(nc, es, f"s_st{i}", [128, 1], F32)[:] for i in range(8)]
        T["eps"] = T["st"][6]
        s_g = sbt(nc, es, "s_g", [128, 512], F32)
        s_up = sbt(nc, es, "s_up", [128, 512], F32)
        s_h = sbt(nc, es, "s_h", [128, 512], F32)
        s_u = s_g
        s_v2 = s_up
        T["qs"] = sbt(nc, es, "s_qs", [128, 64], F32)
        T["qs2"] = sbt(nc, es, "s_qs2", [128, 64], F32)
        T["qm"] = AW[:, 20480:21504].bitcast(F32)
        T["qc"] = AW[:, 21504:22528].bitcast(F32)
        T["qd"] = AW[:, 22528:23552].bitcast(F32)
        T["qe"] = AW[:, 23552:24576].bitcast(F32)
        s_sqscr = sbt(nc, es, "s_sqscr", [128, 512], BF16)
        s_qq = sbt(nc, es, "s_qq", [128, 512], BF16)
        T["scr"] = s_tbf[:]

        p_yt = pst(nc, es, "p_y", [128, 1024])
        p_y = p_yt[:]
        p_s = [pst(nc, es, "p_s0", [128, 512])[:], pst(nc, es, "p_s1", [128, 512])[:],
               p_yt[:, 0:512], p_yt[:, 512:1024]]
        p_of = pst(nc, es, "p_of", [128, 512])
        p_o = p_of[0:65, 0:512]
        p_mix = pst(nc, es, "p_mix", [128, 512])
        p_mix2 = pst(nc, es, "p_mix2", [128, 512])
        p_bc = p_mix[0:64, 0:512]
        p_k2 = [p_mix[0:8, 0:512], p_mix2[0:8, 0:512]]
        p_q2 = [p_mix[0:1, 0:512], p_mix2[0:1, 0:512]]
        T["p_tr"] = pst(nc, es, "p_tr", [128, 128], BF16)[:]

        with nc.Block() as block:
            for i in range(NT):
                b.dma("scalar", lambda e, i=i: e.dma_start(
                    out=s_x[i][:], in_=x_tm[i * 128:(i + 1) * 128, :]), inc="dC")
            for dst, srcd in [(T["ident"][:], ident_d), (s_ones1[:], ones1f),
                              (s_ones1b[:], ones1b), (s_band[:], band),
                              (s_identf[:], identf_d),
                              (s_bq[:], bq_c), (s_bk[:], bk_c)]:
                b.dma("scalar", lambda e, d=dst, s=srcd: e.dma_start(out=d, in_=s), inc="dC")
            b.dma("gpsimd", lambda e: e.dma_start(out=s_bv[:], in_=bv_r), inc="dG")
            b.dma("gpsimd", lambda e: e.dma_start(out=s_bo[:], in_=bo_r), inc="dG")
            for w_d, w_s in [(wqT, wq_v), (wkT, wk_v), (wvT, wv_v)]:
                b.dma("gpsimd", lambda e, wd=w_d, ws=w_s: e.dma_start(
                    out=ws.rearrange("p (c f) -> p c f", f=D),
                    in_=wd.rearrange("(c p) f -> p c f", p=128)), inc="dG")
            for i in range(2):
                b.op("gpsimd", lambda e, i=i: e.memset(
                    s_vst[i][:].rearrange("p (h c) -> p h c", c=65)[:, :, 64:65], 1.0))
                b.op("gpsimd", lambda e, i=i: e.memset(s_ka[i][64:65, :], 1.0))
            b.op("gpsimd", lambda e: e.memset(T["eps"], 1e-5), inc="pE")
            b.wait("scalar", "pE")
            b.wait("tensor", "pE")
            consts_dC = b.count("dC")
            consts_dG = b.count("dG")
            b.wait("tensor", "dG", consts_dG)
            b.wait("scalar", "dC", consts_dC)
            b.wait("scalar", "dG", consts_dG)

            if L2_PHASE < 1:
                b.wait("scalar", "dC", consts_dC)
                b.wait("scalar", "dG", consts_dG)
                for i in range(NT):
                    b.dma("scalar", lambda e, i=i: e.dma_start(
                        out=y_out[i * 128:(i + 1) * 128, :], in_=s_x[i][:]), inc="dC")
                b.wait("scalar", "dC", b.count("dC"))
                b.finalize(block)
                return nc
            xc_free = [0, 0]
            kbank_free = [0, 0]
            ybank_free = 0
            kvstage_free = [0, 0]
            vstage_free = [0, 0]
            qm_free = 0
            sqb_free = [0, 0]
            for kc in range(16):
                slot = kc % 2
                b.wait("gpsimd", "tE", xc_free[slot])
                for dc in range(8):
                    src = x_fm[dc * 128:(dc + 1) * 128, kc * 512:(kc + 1) * 512]
                    b.dma("gpsimd", lambda e, s=src, d=s_xc[slot], dc=dc: e.dma_start(
                        out=d[:, dc * 512:(dc + 1) * 512], in_=s), inc="dG")
                b.wait("tensor", "dG")
                b.wait("tensor", "vE")
                for pc in range(8):
                    bank = p_s[pc % 2]
                    b.wait("tensor", "vE", kbank_free[pc % 2])
                    b.wait("tensor", "sE", sqb_free[pc % 2])
                    for dc in range(8):
                        lhsT = wk_v[:, dc * 1024 + pc * 128: dc * 1024 + (pc + 1) * 128]
                        rhs = s_xc[slot][:, dc * 512:(dc + 1) * 512]
                        b.op("tensor", lambda e, o=bank, l=lhsT, r=rhs,
                             st=(dc == 0), sp=(dc == 7): e.matmul(o, l, r, start=st, stop=sp),
                             inc="tE" if dc == 7 else None)
                    mmk = b.count("tE")
                    b.wait("vector", "tE", mmk)
                    b.wait("vector", "dC", kvstage_free[pc % 2])
                    stg = s_kv[pc % 2]
                    b.op("vector", lambda e, p=bank, s=stg, pc=pc: e.tensor_scalar(
                        out=s[:], in0=p, scalar1=s_bk[:, pc:pc + 1],
                        scalar2=-2.0, op0=ALU.add, op1=ALU.mult), inc="vE")
                    kbank_free[pc % 2] = b.count("vE")
                    b.wait("scalar", "vE")
                    b.dma("scalar", lambda e, s=stg, pc=pc, kc=kc: e.dma_start(
                        out=K1[pc * 128:(pc + 1) * 128, kc * 512:(kc + 1) * 512],
                        in_=s[:]), inc="dC")
                    kvstage_free[pc % 2] = b.count("dC")
                    if L2_K2:
                        b.wait("scalar", "tE", mmk)
                        b.wait("scalar", "tE", qm_free)
                        b.wait("scalar", "vE", kbank_free[pc % 2])
                        if L2_K2SQ:
                            b.op("scalar", lambda e, p=bank, pc=pc: e.activation(
                                out=s_sqscr[:], in_=p, func=AF.Square,
                                bias=s_bk[:, pc:pc + 1]), inc="sE")
                        else:
                            b.op("scalar", lambda e, p=bank: e.copy(
                                out=s_sqscr[:], in_=p), inc="sE")
                        sq_n = b.count("sE")
                        b.wait("tensor", "sE", sq_n)
                        for hl in range(2):
                            lhsT = s_band[hl * 64:(hl + 1) * 64, 15 - pc:23 - pc]
                            rhs = s_sqscr[hl * 64:(hl + 1) * 64, 0:512]
                            b.op("tensor", lambda e, l=lhsT, r=rhs, hl=hl,
                                 st=(pc == 0), sp=(pc == 7): e.matmul(
                                     p_k2[hl], l, r, start=st, stop=sp), inc="tE")
                        qm_free = b.count("tE")
                        sqb_free[pc % 2] = b.count("sE")
                if L2_K2:
                    k2mm = b.count("tE")
                    for par in range(2):
                        b.wait("vector", "tE", k2mm)
                        b.wait("vector", "dC")
                        b.op("vector", lambda e, par=par: e.tensor_copy(
                            out=s_pt[1][0:8, 0:512], in_=p_k2[par]), inc="vE")
                        b.wait("scalar", "vE")
                        b.dma("scalar", lambda e, par=par, kc=kc: e.dma_start(
                            out=K2D[par * 8:(par + 1) * 8, kc * 512:(kc + 1) * 512],
                            in_=s_pt[1][0:8, 0:512]), inc="dC")
                for sub in range(4 if L2_P1N >= 2 else 0):
                    b.wait("tensor", "vE", ybank_free)
                    for half in range(2):
                        for dc in range(8):
                            lhsT = s_xc[slot][:, dc * 512 + sub * 128: dc * 512 + (sub + 1) * 128]
                            rhs = wv_v[:, dc * 1024 + half * 512: dc * 1024 + (half + 1) * 512]
                            b.op("tensor", lambda e, l=lhsT, r=rhs, st=(dc == 0),
                                 hf=half: e.matmul(
                                     p_y[:, hf * 512:(hf + 1) * 512], l, r, start=st,
                                     stop=False), inc=None)
                        if L2_VBIAS:
                            b.op("tensor", lambda e, hf=half: e.matmul(
                                p_y[:, hf * 512:(hf + 1) * 512], s_ones1b[:],
                                s_bv[:, hf * 512:(hf + 1) * 512], start=False, stop=True),
                                inc="tE")
                        else:
                            b.op("tensor", lambda e, hf=half: e.matmul(
                                p_y[:, hf * 512:(hf + 1) * 512],
                                s_xc[slot][:, 0:128], wv_v[:, 0:512],
                                start=False, stop=True), inc="tE")
                    b.wait("vector", "tE")
                    b.wait("vector", "dC", vstage_free[sub % 2])
                    vstg = s_vst[sub % 2]
                    dstv = vstg[:].rearrange("p (h c) -> p h c", c=65)[:, :, 0:64]
                    b.op("vector", lambda e, d=dstv: e.tensor_copy(
                        out=d, in_=p_y.rearrange("p (h c) -> p h c", c=64)), inc="vE")
                    ybank_free = b.count("vE")
                    b.wait("scalar", "vE")
                    row0 = kc * 512 + sub * 128
                    b.dma("scalar", lambda e, s=vstg, r0=row0: e.dma_start(
                        out=V1[r0:r0 + 128, :], in_=s[:]), inc="dC")
                    vstage_free[sub % 2] = b.count("dC")
                xc_free[slot] = b.count("tE")

            for Jc in range(NCH if L2_P1N >= 3 else 0):
                slot = Jc % 2
                b.wait("tensor", "vE", ybank_free)
                b.wait("gpsimd", "tE", xc_free[slot])
                for dc in range(8):
                    src = x_fm_own[dc * 128:(dc + 1) * 128, Jc * 512:(Jc + 1) * 512]
                    b.dma("gpsimd", lambda e, s=src, d=s_xc[slot], dc=dc: e.dma_start(
                        out=d[:, dc * 512:(dc + 1) * 512], in_=s), inc="dG")
                b.wait("tensor", "dG")
                for pc in range(8):
                    bank = p_s[2 + pc % 2]
                    b.wait("tensor", "vE", kbank_free[pc % 2])
                    b.wait("tensor", "sE", sqb_free[pc % 2])
                    for dc in range(8):
                        lhsT = wq_v[:, dc * 1024 + pc * 128: dc * 1024 + (pc + 1) * 128]
                        rhs = s_xc[slot][:, dc * 512:(dc + 1) * 512]
                        b.op("tensor", lambda e, o=bank, l=lhsT, r=rhs,
                             st=(dc == 0), sp=(dc == 7): e.matmul(o, l, r, start=st, stop=sp),
                             inc="tE" if dc == 7 else None)
                    mmq = b.count("tE")
                    b.wait("vector", "tE", mmq)
                    b.wait("vector", "dC", kvstage_free[pc % 2])
                    stg = s_kv[pc % 2]
                    b.op("vector", lambda e, p=bank, s=stg, pc=pc: e.tensor_scalar(
                        out=s[:], in0=p, scalar1=s_bq[:, pc:pc + 1], scalar2=None,
                        op0=ALU.add), inc="vE")
                    evq = b.count("vE")
                    b.wait("scalar", "vE", evq)
                    for hl in range(2):
                        h = 2 * pc + hl
                        b.dma("scalar", lambda e, s=stg, h=h, hl=hl, Jc=Jc: e.dma_start(
                            out=Q1[h * 65:h * 65 + 64, Jc * 512:(Jc + 1) * 512],
                            in_=s[hl * 64:(hl + 1) * 64, :]), inc="dC")
                    kvstage_free[pc % 2] = b.count("dC")
                    if L2_Q2:
                        b.wait("scalar", "tE", mmq)
                        b.wait("scalar", "tE", qm_free)
                        b.wait("scalar", "vE", evq)
                        b.op("scalar", lambda e, p=bank, pc=pc: e.activation(
                            out=s_sqscr[:], in_=p, func=AF.Square,
                            bias=s_bq[:, pc:pc + 1]), inc="sE")
                        sq_n = b.count("sE")
                        b.wait("tensor", "sE", sq_n)
                        for hl in range(2):
                            h = 2 * pc + hl
                            b.wait("tensor", "vE")
                            lhsT = s_band[hl * 64:(hl + 1) * 64, 15:16]
                            rhs = s_sqscr[hl * 64:(hl + 1) * 64, 0:512]
                            b.op("tensor", lambda e, l=lhsT, r=rhs, hl=hl: e.matmul(
                                p_q2[hl], l, r, start=True, stop=True), inc="tE")
                            q2mm = b.count("tE")
                            b.wait("vector", "tE", q2mm)
                            b.wait("vector", "dC")
                            b.op("vector", lambda e, hl=hl: e.tensor_copy(
                                out=s_pt[0][0:1, 0:512], in_=p_q2[hl]), inc="vE")
                            q2c = b.count("vE")
                            b.wait("scalar", "vE", q2c)
                            b.dma("scalar", lambda e, h=h, Jc=Jc: e.dma_start(
                                out=Q1[h * 65 + 64:h * 65 + 65, Jc * 512:(Jc + 1) * 512],
                                in_=s_pt[0][0:1, 0:512]), inc="dC")
                        qm_free = b.count("tE")
                        sqb_free[pc % 2] = b.count("sE")
                xc_free[slot] = b.count("tE")
            p1_dC = b.count("dC")

            p1_tE = b.count("tE")
            b.wait("gpsimd", "tE", p1_tE)
            b.dma("gpsimd", lambda e: e.dma_start(
                out=wo_v.rearrange("p (c f) -> p c f", f=D),
                in_=woT.rearrange("(c p) f -> p c f", p=128)), inc="dG")
            wo_dG = b.count("dG")

            if L2_PHASE < 2:
                emit_ln(b, [], None, T)
                lnv = b.count("vE")
                b.wait("scalar", "vE", lnv)
                b.wait("scalar", "tE")
                b.wait("scalar", "sE")
                b.wait("scalar", "dS")
                b.wait("scalar", "dG")
                b.wait("scalar", "dC")
                for i in range(NT):
                    b.dma("scalar", lambda e, i=i: e.dma_start(
                        out=y_out[i * 128:(i + 1) * 128, :], in_=s_x[i][:]), inc="dC")
                b.wait("scalar", "dC", b.count("dC"))
                b.finalize(block)
                return nc
            b.wait("sync", "dC", p1_dC)
            if L2_K2:
                for kt in range(NKT):
                    b.dma("sync", lambda e, kt=kt: e.dma_start(
                        out=s_k2[:, kt * 16:(kt + 1) * 16],
                        in_=K2D[0:16, kt * 128:(kt + 1) * 128], transpose=True),
                        inc="dS")
                b.wait("scalar", "dS")
            if DEBUG:
                b.dma("scalar", lambda e: e.dma_start(out=dbg_k2, in_=s_k2[:]), inc="dC")
                b.dma("scalar", lambda e: e.dma_start(
                    out=dbg_k2d, in_=K2D[0:16, 0:512]), inc="dC")
                b.dma("scalar", lambda e: e.dma_start(
                    out=dbg_q1, in_=Q1[0:65, 0:512]), inc="dC")
            b.wait("tensor", "dG", wo_dG)
            ka_free = [0, 0]
            vt_free = [0, 0]
            qp_free = 0
            sb_free = [0, 0]
            chain_free = 0
            att_free = [0] * 8
            s_vb3 = [s_v2[:], T["qm"], s_u[:]]
            last_em3 = [0, 0, 0]
            last_av = [0, 0]
            for Jc in range(NCH):
                for pc in range(8):
                    for hl in range(2):
                        h = 2 * pc + hl
                        b.wait("tensor", "vE")
                        b.wait("sync", "tE", qp_free)
                        b.dma("sync", lambda e, h=h, Jc=Jc: e.dma_start(
                            out=s_qp[:], in_=Q1[h * 65:(h + 1) * 65,
                                               Jc * 512:(Jc + 1) * 512]), inc="dS")
                        qld = b.count("dS")
                        b.wait("tensor", "dS", qld)
                        hcol = (h // 2) + 8 * (h % 2)
                        mm_c = [0] * NKT
                        sq_c = [0] * NKT
                        em_c = [0] * NKT
                        av_c = [0] * NKT
                        kld_c = [0] * (NKT // 4)
                        vld_c = [0] * (NKT // 4)
                        for it in range(-1, NKT + 1):
                            ks = it + 1
                            if 0 <= ks < NKT:
                                kg, ki = ks // 4, ks % 4
                                kslot = kg % 2
                                if ki == 0:
                                    b.wait("sync", "tE", ka_free[kslot])
                                    b.dma("sync", lambda e, h=h, kg=kg, d=s_ka[kslot]:
                                          e.dma_start(out=d[0:64, :],
                                                      in_=K1[h * 64:(h + 1) * 64,
                                                             kg * 512:(kg + 1) * 512]),
                                          inc="dS")
                                    kld_c[kg] = b.count("dS")
                                    b.wait("sync", "tE", vt_free[kslot])
                                    srcv = V1[kg * 512:(kg + 1) * 512,
                                              h * 65:(h + 1) * 65]
                                    b.dma("sync", lambda e, s=srcv, d=s_vt[kslot]:
                                          e.dma_start(
                                              out=d[:].rearrange("p (g c) -> p g c", g=4),
                                              in_=s.rearrange("(g p) c -> p g c", p=128)),
                                          inc="dS")
                                    vld_c[kg] = b.count("dS")
                                sbank = p_s[ks % 2]
                                b.wait("tensor", "dS", kld_c[kg])
                                b.wait("tensor", "sE", sb_free[ks % 2])
                                lhsT = s_ka[kslot][:, ki * 128:(ki + 1) * 128]
                                b.op("tensor", lambda e, o=sbank, l=lhsT: e.matmul(
                                    o, l, s_qp[:], start=True, stop=True), inc="tE")
                                mm_c[ks] = b.count("tE")
                                if ks % 4 == 3:
                                    ka_free[kslot] = mm_c[ks]
                            kt = it
                            if 0 <= kt < NKT:
                                if kt == 0 and Jc == 0:
                                    b.wait("scalar", "dS")
                                b.wait("scalar", "tE", mm_c[kt])
                                b.wait("scalar", "sE", last_em3[kt % 3])
                                k2col = s_k2[:, kt * 16 + hcol: kt * 16 + hcol + 1]
                                b.op("scalar", lambda e, p=p_s[kt % 2], kc2=k2col,
                                     o=s_vb3[kt % 3]: e.activation(
                                         out=o, in_=p, func=AF.Sqrt, bias=kc2),
                                     inc="sE", serial=False)
                                sq_c[kt] = b.count("sE")
                                sb_free[kt % 2] = sq_c[kt]
                            j = it - 1
                            if 0 <= j < NKT:
                                b.wait("scalar", "sE", sq_c[j])
                                if last_av[j % 2] > 0:
                                    b.wait("scalar", "tE", last_av[j % 2])
                                b.op("scalar", lambda e, i_=s_vb3[j % 3],
                                     o=s_pt[j % 2]: e.activation(
                                         out=o[:], in_=i_, func=AF.Exp, scale=-1.0),
                                     inc="sE", serial=False)
                                em_c[j] = b.count("sE")
                                last_em3[j % 3] = em_c[j]
                                if DEBUG and Jc == 0 and h == 0 and j == 0:
                                    b.wait("scalar", "sE", em_c[0])
                                    b.dma("scalar", lambda e: e.dma_start(
                                        out=dbg_pt, in_=s_pt[0][:]), inc="dC")
                                b.wait("tensor", "sE", em_c[j])
                                b.wait("tensor", "dS", vld_c[j // 4])
                                lhsT = s_vt[(j // 4) % 2][:, (j % 4) * 65:
                                                          (j % 4 + 1) * 65]
                                b.op("tensor", lambda e, l=lhsT, r=s_pt[j % 2],
                                     st=(j == 0), sp=(j == NKT - 1): e.matmul(
                                         p_o, l, r[:], start=st, stop=sp), inc="tE")
                                av_c[j] = b.count("tE")
                                last_av[j % 2] = av_c[j]
                                if j % 4 == 3:
                                    vt_free[(j // 4) % 2] = av_c[j]
                        qp_free = mm_c[NKT - 1]
                        if DEBUG and Jc == 0 and h == 0:
                            dn = b.count("tE")
                            b.wait("vector", "tE", dn)
                            b.op("vector", lambda e: e.tensor_copy(
                                out=s_u[0:1, 0:512], in_=p_of[64:65, 0:512]), inc="vE")
                            b.wait("scalar", "vE")
                            b.dma("scalar", lambda e: e.dma_start(
                                out=dbg_den, in_=s_u[0:1, 0:512]), inc="dC")
                        b.wait("vector", "tE")
                        b.op("vector", lambda e: e.reciprocal(
                            out=s_recip[:], in_=p_of[64:65, 0:512]), inc="vE")
                        rec_n = b.count("vE")
                        b.wait("tensor", "vE", rec_n)
                        b.op("tensor", lambda e: e.matmul(
                            p_bc, s_ones1[:], s_recip[:], start=True, stop=True),
                            inc="tE")
                        bc_n = b.count("tE")
                        b.wait("vector", "tE", bc_n)
                        b.op("vector", lambda e: e.tensor_copy(
                            out=s_bcast[:], in_=p_bc), inc="vE")
                        b.wait("vector", "tE", att_free[pc])
                        b.op("vector", lambda e, pc=pc, hl=hl: e.tensor_tensor(
                            out=s_attn[pc][hl * 64:(hl + 1) * 64, :],
                            in0=p_of[0:64, 0:512], in1=s_bcast[:], op=ALU.mult), inc="vE")
                attn_ready = b.count("vE")
                b.wait("tensor", "vE", attn_ready)
                for sub in range(4):
                    b.wait("tensor", "vE", ybank_free)
                    for half in range(2):
                        for pc in range(8):
                            lhsT = s_attn[pc][:, sub * 128:(sub + 1) * 128]
                            rhs = wo_v[:, pc * 1024 + half * 512: pc * 1024 + (half + 1) * 512]
                            b.op("tensor", lambda e, l=lhsT, r=rhs, hf=half,
                                 st=(pc == 0): e.matmul(
                                     p_y[:, hf * 512:(hf + 1) * 512], l, r,
                                     start=st, stop=False), inc=None)
                        b.op("tensor", lambda e, hf=half: e.matmul(
                            p_y[:, hf * 512:(hf + 1) * 512], s_ones1b[:],
                            s_bo[:, hf * 512:(hf + 1) * 512], start=False, stop=True),
                            inc="tE")
                    for pc in range(8):
                        att_free[pc] = b.count("tE")
                    b.wait("vector", "tE")
                    ti = Jc * 4 + sub
                    b.op("vector", lambda e, ti=ti: e.tensor_tensor(
                        out=s_x[ti][:], in0=p_y, in1=s_x[ti][:], op=ALU.add),
                        inc="vE")
                    ybank_free = b.count("vE")

            if L2_PHASE < 3:
                lnv = b.count("vE")
                b.wait("scalar", "vE", lnv)
                b.wait("scalar", "tE")
                b.wait("scalar", "sE")
                for i in range(NT):
                    b.dma("scalar", lambda e, i=i: e.dma_start(
                        out=y_out[i * 128:(i + 1) * 128, :], in_=s_x[i][:]), inc="dC")
                b.wait("scalar", "dC", b.count("dC"))
                b.finalize(block)
                return nc
            for i in range(NT):
                b.wait("vector", "tE")
                emit_ln(b, [s_x[i][:]], None, T)
                lnn = b.count("vE")
                b.wait("vector", "vE", lnn)
                for xh in range(2):
                    b.wait("vector", "tE")
                    emit_quant(b, s_x[i][:, xh * 512:(xh + 1) * 512], s_qq[:], T, 128, 512)
                    for dc4 in range(4):
                        dc = xh * 4 + dc4
                        srct = s_qq[:, dc4 * 128:(dc4 + 1) * 128]
                        cons = b.count("vE")
                        b.wait("tensor", "vE", cons)
                        b.op("tensor", lambda e, s=srct: e.transpose(
                            T["p_tr"], s, T["ident"][:]), inc="tE")
                        b.wait("vector", "tE")
                        dst = s_xfm[:, dc * TPC + i * 128: dc * TPC + (i + 1) * 128]
                        b.op("vector", lambda e, d=dst: e.tensor_copy(
                            out=d, in_=T["p_tr"]), inc="vE")
            fm_ready = b.count("vE")
            b.wait("tensor", "vE", fm_ready)
            if DEBUG:
                b.wait("scalar", "vE", fm_ready)
                b.dma("scalar", lambda e: e.dma_start(
                    out=dbg_xq, in_=s_xfm[:].rearrange("p (c t) -> p c t", c=8)[:, :, 0:128]
                    .rearrange("p c t -> p c t")), inc="dC")

            if L2_PHASE < 4:
                lnv = b.count("vE")
                b.wait("scalar", "vE", lnv)
                b.wait("scalar", "tE")
                b.wait("scalar", "sE")
                for i in range(NT):
                    b.dma("scalar", lambda e, i=i: e.dma_start(
                        out=y_out[i * 128:(i + 1) * 128, :], in_=s_x[i][:]), inc="dC")
                b.wait("scalar", "dC", b.count("dC"))
                b.finalize(block)
                return nc
            wg_r = wgq.rearrange("(c p) f -> p c f", p=128)
            wu_r = wuq.rearrange("(c p) f -> p c f", p=128)
            wd_r = wdq.rearrange("(c p) f -> p c f", p=128)
            hq_free = 0
            gbank_free = sb_free[0]
            ubank_free = 0
            wd_free = [0, 0]
            for tg in range(8):
                b.wait("tensor", "tE", hq_free)
                for jci, jw in enumerate(JCH):
                    jo = JOF[jci]
                    wslot = jci % 2
                    b.wait("gpsimd", "tE")
                    b.dma("gpsimd", lambda e, jo=jo, jw=jw, d=wgc_v[wslot]: e.dma_start(
                        out=d.rearrange("p (c f) -> p c f", f=512)[:, :, 0:jw],
                        in_=wg_r[:, :, jo:jo + jw]), inc="dG")
                    b.dma("gpsimd", lambda e, jo=jo, jw=jw, d=wuc_v[wslot]: e.dma_start(
                        out=d.rearrange("p (c f) -> p c f", f=512)[:, :, 0:jw],
                        in_=wu_r[:, :, jo:jo + jw]), inc="dG")
                    wld = b.count("dG")
                    b.wait("tensor", "dG", wld)
                    for ts2 in range(2):
                        ti = tg * 2 + ts2
                        b.wait("tensor", "sE", gbank_free)
                        for dc in range(8):
                            lhsT = s_xfm[:, dc * TPC + ti * 128: dc * TPC + (ti + 1) * 128]
                            rhs = wgc_v[wslot][:, dc * 512: dc * 512 + jw]
                            b.op("tensor", lambda e, l=lhsT, r=rhs, jw=jw, st=(dc == 0),
                                 sp=(dc == 7): e.matmul(p_s[0][:, 0:jw], l, r,
                                                        start=st, stop=sp),
                                 inc="tE" if dc == 7 else None)
                        mg = b.count("tE")
                        b.wait("scalar", "tE", mg)
                        b.op("scalar", lambda e, jw=jw: e.activation(
                            out=s_g[:, 0:jw], in_=p_s[0][:, 0:jw], func=AF.Silu), inc="sE")
                        gbank_free = b.count("sE")
                        if DEBUG and tg == 0 and ts2 == 0 and jci == 0:
                            b.wait("scalar", "sE", gbank_free)
                            b.dma("scalar", lambda e: e.dma_start(
                                out=dbg_g, in_=s_g[:, 0:512]), inc="dC")
                        b.wait("tensor", "vE", ubank_free)
                        for dc in range(8):
                            lhsT = s_xfm[:, dc * TPC + ti * 128: dc * TPC + (ti + 1) * 128]
                            rhs = wuc_v[wslot][:, dc * 512: dc * 512 + jw]
                            b.op("tensor", lambda e, l=lhsT, r=rhs, jw=jw, st=(dc == 0),
                                 sp=(dc == 7): e.matmul(p_s[1][:, 0:jw], l, r,
                                                        start=st, stop=sp),
                                 inc="tE" if dc == 7 else None)
                        mu = b.count("tE")
                        b.wait("vector", "tE", mu)
                        b.op("vector", lambda e, jw=jw: e.tensor_copy(
                            out=s_up[:, 0:jw], in_=p_s[1][:, 0:jw]), inc="vE")
                        ubank_free = b.count("vE")
                        b.wait("vector", "sE")
                        b.op("vector", lambda e, jw=jw: e.tensor_tensor(
                            out=s_h[:, 0:jw], in0=s_g[:, 0:jw], in1=s_up[:, 0:jw],
                            op=ALU.mult), inc="vE")
                        emit_quant(b, s_h[:, 0:jw], s_qq[:, 0:jw], T, 128, jw)
                        for hc in range(jw // 128):
                            cons = b.count("vE")
                            b.wait("tensor", "vE", cons)
                            srct = s_qq[:, hc * 128:(hc + 1) * 128]
                            b.op("tensor", lambda e, s=srct: e.transpose(
                                T["p_tr"], s, T["ident"][:]), inc="tE")
                            b.wait("vector", "tE")
                            col = ts2 * 2816 + jo + hc * 128
                            b.op("vector", lambda e, col=col: e.tensor_copy(
                                out=hq_fm[:, col:col + 128], in_=T["p_tr"]), inc="vE")
                hq_ready = b.count("vE")
                b.wait("tensor", "vE", hq_ready)
                if DEBUG and tg == 0:
                    b.wait("scalar", "vE", hq_ready)
                    b.dma("scalar", lambda e: e.dma_start(
                        out=dbg_hq, in_=hq_fm[:, 0:2816]), inc="dC")
                nhc = I_FF // 128
                HCG = [4, 4, 4, 4, 4, 2]
                for ts2 in range(2):
                    ti = tg * 2 + ts2
                    b.wait("tensor", "vE", ybank_free)
                    hc0 = 0
                    for gi, gn in enumerate(HCG):
                        dslot = 0
                        b.wait("gpsimd", "tE", wd_free[dslot])
                        b.dma("gpsimd", lambda e, d=wdc_v[dslot], h0=hc0, gn=gn: e.dma_start(
                            out=d.rearrange("p (c f) -> p c f", f=1024)[:, 0:gn, :],
                            in_=wd_r[:, h0:h0 + gn, :]), inc="dG")
                        dld = b.count("dG")
                        b.wait("tensor", "dG", dld)
                        for gj in range(gn):
                            hc = hc0 + gj
                            lhsT = hq_fm[:, ts2 * 2816 + hc * 128: ts2 * 2816 + (hc + 1) * 128]
                            wcf = wdc_v[dslot][:, gj * 1024:(gj + 1) * 1024]
                            for half in range(2):
                                b.op("tensor", lambda e, l=lhsT, w=wcf, hf=half, hc=hc,
                                     st=(hc == 0), sp=(hc == nhc - 1): e.matmul(
                                         p_y[:, hf * 512:(hf + 1) * 512], l,
                                         w[:, hf * 512:(hf + 1) * 512], start=st, stop=sp),
                                     inc="tE" if (gj == gn - 1 and half == 1) else None)
                        wd_free[dslot] = b.count("tE")
                        hc0 += gn
                    b.wait("vector", "tE")
                    b.op("vector", lambda e, ti=ti: e.tensor_tensor(
                        out=s_x[ti][:], in0=p_y, in1=s_x[ti][:], op=ALU.add), inc="vE")
                    ybank_free = b.count("vE")
                hq_free = b.count("tE")

            emit_ln(b, [s_x[i][:] for i in range(NT)], None, T)
            lnv = b.count("vE")
            b.wait("scalar", "vE", lnv)
            for i in range(NT):
                b.dma("scalar", lambda e, i=i: e.dma_start(
                    out=y_out[i * 128:(i + 1) * 128, :], in_=s_x[i][:]), inc="dC")
            b.wait("scalar", "dC", b.count("dC"))
            b.finalize(block)
    return nc



def own_index(r):
    idx = []
    for j in range(NT):
        t = 4 * j + r
        idx.append(np.arange(t * 128, (t + 1) * 128))
    return np.concatenate(idx)


def cont(a):
    return np.ascontiguousarray(a)


def make_masks(r):
    m = np.zeros((4, 128, 128), dtype=bfd)
    tri = (np.arange(128)[:, None] <= np.arange(128)[None, :]).astype(bfd)
    for s in range(4):
        if s < r:
            m[s] = 1.0
        elif s == r:
            m[s] = tri
        else:
            m[s] = 0.0
    return m


_cache = {}


def prep_launch1(inp):
    x = np.asarray(inp["L0_attn_x"], np.float32)
    WQ = np.asarray(inp["L0_attn_WQ"], np.float32)
    WK = np.asarray(inp["L0_attn_WK"], np.float32)
    WV = np.asarray(inp["L0_attn_WV"], np.float32)
    WO = np.asarray(inp["L0_attn_WO"], np.float32)
    shared = {
        "wql": cont(WQ.transpose(2, 0, 1).reshape(D, D)),
        "wkl": cont(WK.transpose(2, 0, 1).reshape(D, D)),
        "wvr": cont(WV.transpose(2, 0, 1).reshape(D, D)),
        "wor": cont(WO.transpose(0, 2, 1).reshape(D, D)),
        "wg1": np.asarray(inp["L0_ffn_wg1"], np.float32),
        "bg1c": cont(np.asarray(inp["L0_ffn_bg1"], np.float32).reshape(2, 128).T),
        "wg2": np.asarray(inp["L0_ffn_wg2"], np.float32),
        "w_in": np.asarray(inp["L0_ffn_w_in"], np.float32),
        "w_out": np.asarray(inp["L0_ffn_w_out"], np.float32),
        "ident": np.eye(128, dtype=bfd),
        "ones1f": np.ones((1, 64), np.float32),
    }
    wgate = np.asarray(inp["L1_ffn_w_gate"], np.float32)
    wup = np.asarray(inp["L1_ffn_w_up"], np.float32)
    wdown = np.asarray(inp["L1_ffn_w_down"], np.float32)
    x_fm_b = [cont(x[g].T) for g in range(B)]
    maps = []
    for c in range(NCORE):
        g, r = c // RPG, c % RPG
        oi = own_index(r)
        m = dict(shared)
        m["x_fm"] = x_fm_b[g]
        m["x_fm_own"] = cont(x[g].T[:, oi])
        m["x_tm"] = cont(x[g][oi])
        m["masks"] = make_masks(r)
        m["wg_sl"] = cont(wgate[c * WSL:(c + 1) * WSL])
        m["wu_sl"] = cont(wup[c * WSL:(c + 1) * WSL])
        m["wd_sl"] = cont(wdown[:, c * WSL:(c + 1) * WSL])
        maps.append(m)
    return maps


def run_launch1(inp):
    if "l1" not in _cache:
        _cache["l1"] = build_launch1()
    nc = _cache["l1"]
    maps = prep_launch1(inp)
    res = run_bass_kernel_spmd(nc, maps, core_ids=list(range(NCORE)))
    return res.results


def assemble_x1(results):
    x1 = np.zeros((B, S, D), np.float32)
    for c in range(NCORE):
        g, r = c // RPG, c % RPG
        x1[g][own_index(r)] = results[c]["x1"]
    return x1


def assemble_qw(results):
    wgT = np.concatenate([results[c]["wgT_q"] for c in range(NCORE)], axis=1)
    wuT = np.concatenate([results[c]["wuT_q"] for c in range(NCORE)], axis=1)
    wdT = np.concatenate([results[c]["wdT_q"] for c in range(NCORE)], axis=0)
    return wgT, wuT, wdT


def prep_launch2(inp, x1, wgT, wuT, wdT):
    wq = np.asarray(inp["L1_attn_wq"], np.float32)
    wk = np.asarray(inp["L1_attn_wk"], np.float32)
    wv = np.asarray(inp["L1_attn_wv"], np.float32)
    wo = np.asarray(inp["L1_attn_wo"], np.float32)
    band = np.zeros((128, 31), dtype=bfd)
    band[:, 15] = 1.0
    shared = {
        "wqT": cont(wq.T), "wkT": cont(wk.T), "wvT": cont(wv.T), "woT": cont(wo.T),
        "bq_c": cont(np.asarray(inp["L1_attn_bq"], np.float32).reshape(8, 128).T),
        "bk_c": cont(np.asarray(inp["L1_attn_bk"], np.float32).reshape(8, 128).T),
        "bv_r": np.asarray(inp["L1_attn_bv"], np.float32).reshape(1, D),
        "bo_r": np.asarray(inp["L1_attn_bo"], np.float32).reshape(1, D),
        "wgq": wgT, "wuq": wuT, "wdq": wdT,
        "band": band,
        "ident": np.eye(128, dtype=bfd),
        "identf": np.eye(128, dtype=np.float32),
        "ones1f": np.ones((1, 64), np.float32),
        "ones1b": np.ones((1, 128), dtype=bfd),
    }
    x1_fm_b = [cont(x1[g].T) for g in range(B)]
    maps = []
    for c in range(NCORE):
        g, r = c // RPG, c % RPG
        oi = own_index(r)
        m = dict(shared)
        m["x_fm"] = x1_fm_b[g]
        m["x_fm_own"] = cont(x1[g].T[:, oi])
        m["x_tm"] = cont(x1[g][oi])
        maps.append(m)
    return maps


def kernel(**inp):
    if "l1" not in _cache:
        _cache["l1"] = build_launch1()
    maps1 = prep_launch1(inp)
    res1 = run_bass_kernel_spmd(_cache["l1"], maps1, core_ids=list(range(NCORE)))
    x1 = assemble_x1(res1.results)
    wgT, wuT, wdT = assemble_qw(res1.results)
    if "l2" not in _cache:
        _cache["l2"] = build_launch2()
    maps2 = prep_launch2(inp, x1, wgT, wuT, wdT)
    res2 = run_bass_kernel_spmd(_cache["l2"], maps2, core_ids=list(range(NCORE)))
    out = np.zeros((B, S, D), np.float32)
    for c in range(NCORE):
        g, r = c // RPG, c % RPG
        out[g][own_index(r)] = res2.results[c]["y"]
    return out


# version 07
# speedup vs baseline: 1.3843x; 1.3843x over previous
import contextlib
import numpy as np
import ml_dtypes
import concourse.bass as bass
import concourse.mybir as mybir
from concourse.bass_utils import run_bass_kernel_spmd

F32 = mybir.dt.float32
BF16 = mybir.dt.bfloat16
AF = mybir.ActivationFunctionType
ALU = mybir.AluOpType
AX = mybir.AxisListType.X
bfd = ml_dtypes.bfloat16

B, S, D, H, HD = 2, 8192, 1024, 16, 64
RPG = 4
TPC = S // RPG
NT = TPC // 128
NKT = S // 128
NCH = 4
QW = 512
E_FF, H_FF, G_FF = 8, 2048, 256
I_FF = 2816
WSL = I_FF // 8
NCORE = 8
DEBUG = False
L2_PHASE = 4
L2_P1N = 3
L2_K2TR = True
L2_K2SQ = True
L2_K2 = True
L2_VBIAS = True
L2_Q2 = True

ENGS = ("sync", "scalar", "vector", "tensor", "gpsimd")


class Builder:
    def __init__(self, nc, es):
        self.nc, self.es = nc, es
        self.steps = []
        self.sems = {}

    def sem(self, name):
        h = self.es.enter_context(self.nc.semaphore(name))
        self.sems[name] = [h, 0]
        return name

    def wait(self, eng, name, val=None):
        h, n = self.sems[name]
        tgt = n if val is None else val
        if tgt > 0:
            self.steps.append((eng, lambda e, h=h, t=tgt: e.wait_ge(h, t)))

    SELF_SEM = {"vector": "vE", "scalar": "sE", "gpsimd": "pE"}

    def op(self, eng, fn, inc=None, inc_by=1, serial=True):
        ss = self.SELF_SEM.get(eng)
        if ss is not None:
            rec = self.sems[ss]
            if serial and rec[1] > 0:
                self.steps.append(
                    (eng, lambda e, h=rec[0], t=rec[1]: e.wait_ge(h, t)))
            if inc is None:
                inc = ss
            assert inc == ss, f"compute op on {eng} must inc {ss}"
        if inc is not None:
            rec = self.sems[inc]
            rec[1] += inc_by
            self.steps.append(
                (eng, lambda e, fn=fn, h=rec[0], k=inc_by: fn(e).then_inc(h, k)))
        else:
            self.steps.append((eng, lambda e, fn=fn: fn(e)))

    def dma(self, eng, fn, inc):
        if inc is not None:
            rec = self.sems[inc]
            rec[1] += 16
            self.steps.append(
                (eng, lambda e, fn=fn, h=rec[0]: fn(e).then_inc(h, 16)))
        else:
            self.steps.append((eng, lambda e, fn=fn: fn(e)))

    def count(self, name):
        return self.sems[name][1]

    def finalize(self, block):
        for engname in ENGS:
            def make(engname=engname):
                def _(eng):
                    for e, fn in self.steps:
                        if e == engname:
                            fn(eng)
                return _
            getattr(block, engname)(make())


def sbt(nc, es, name, shape, dtype):
    return es.enter_context(nc.sbuf_tensor(name, shape, dtype))


def pst(nc, es, name, shape, dtype=F32):
    return es.enter_context(nc.psum_tensor(name, shape, dtype))


def ceil_div(a, b):
    return -(-a // b)



def emit_ln(b, x_tiles, xbf_tiles, T):
    scr = T["scr"]
    st = T["st"]
    for i, xt in enumerate(x_tiles):
        ssum, ssq = st[0], st[1]
        mean, var, m2, rstd = st[2], st[3], st[4], st[5]
        b.op("vector", lambda e, xt=xt, o=ssum: e.tensor_reduce(
            out=o, in_=xt, axis=AX, op=ALU.add))
        b.op("vector", lambda e, a=ssq: e.memset(a, 0.0), inc="vE")
        b.wait("scalar", "vE")
        b.op("scalar", lambda e, xt=xt, o=scr, a=ssq: e.activation(
            out=o, in_=xt, func=AF.Square, accum_out=a), inc="sE")
        b.wait("vector", "sE")
        b.op("vector", lambda e, i_=ssum, o=mean: e.tensor_scalar(
            out=o, in0=i_, scalar1=1.0 / D, scalar2=None, op0=ALU.mult))
        b.op("vector", lambda e, a=mean, o=m2: e.tensor_tensor(
            out=o, in0=a, in1=a, op=ALU.mult))
        b.op("vector", lambda e, i_=ssq, o=var: e.tensor_scalar(
            out=o, in0=i_, scalar1=1.0 / D, scalar2=None, op0=ALU.mult))
        b.op("vector", lambda e, o=var, a=m2: e.tensor_tensor(
            out=o, in0=o, in1=a, op=ALU.subtract), inc="vE")
        b.wait("scalar", "vE")
        b.op("scalar", lambda e, i_=var, o=rstd: e.activation(
            out=o, in_=i_, func=AF.Ln, bias=T["eps"]))
        b.op("scalar", lambda e, o=rstd: e.activation(
            out=o, in_=o, func=AF.Exp, scale=-0.5), inc="sE")
        b.wait("vector", "sE")
        b.op("vector", lambda e, xt=xt, m=mean, r=rstd: e.tensor_scalar(
            out=xt, in0=xt, scalar1=m, scalar2=r, op0=ALU.subtract, op1=ALU.mult),
            inc="vE")
        if xbf_tiles is not None:
            b.op("vector", lambda e, xt=xt, o=xbf_tiles[i]: e.tensor_copy(
                out=o, in_=xt), inc="vE")


def emit_tm_to_fm(b, xbf_tiles, xfm, T, ntiles):
    ident, p_tr = T["ident"], T["p_tr"]
    ntok = ntiles * 128
    for jt in range(ntiles):
        for dc in range(8):
            src = xbf_tiles[jt][:, dc * 128:(dc + 1) * 128]
            cons = b.count("vE")
            b.wait("tensor", "vE", cons)
            b.op("tensor", lambda e, s=src: e.transpose(p_tr[:], s, ident[:]),
                 inc="tE")
            b.wait("vector", "tE")
            dst = xfm[:, dc * ntok + jt * 128: dc * ntok + (jt + 1) * 128]
            b.op("vector", lambda e, d=dst: e.tensor_copy(out=d, in_=p_tr[:]),
                 inc="vE")


def emit_quant(b, xin, qout, T, P, F):
    nb = F // 16
    qs = T["qs"][0:P, 0:nb]
    qs2 = T["qs2"][0:P, 0:nb]
    qm = T["qm"][0:P, 0:F]
    qc = T["qc"][0:P, 0:F]
    qd = T["qd"][0:P, 0:F]
    qe = T["qe"][0:P, 0:F]

    def bview(ap):
        return ap.rearrange("p (b w) -> p b w", w=16)

    def sview(ap):
        return ap.rearrange("p (b o) -> p b o", o=1).broadcast_to((P, nb, 16))

    b.op("vector", lambda e: e.tensor_reduce(
        out=qs, in_=bview(xin), axis=AX, op=ALU.max, apply_absolute_value=True))
    b.op("vector", lambda e: e.tensor_scalar(
        out=qs, in0=qs, scalar1=1.0 / 6.0, scalar2=1e-12, op0=ALU.mult, op1=ALU.max))
    b.op("vector", lambda e: e.reciprocal(out=qs2, in_=qs))
    b.op("vector", lambda e: e.tensor_tensor(
        out=bview(qm), in0=bview(xin), in1=sview(qs2), op=ALU.mult))
    b.op("vector", lambda e: e.tensor_tensor(out=qc, in0=qm, in1=qm, op=ALU.mult))
    b.op("vector", lambda e: e.tensor_scalar(
        out=qd, in0=qc, scalar1=0.0625, scalar2=None, op0=ALU.is_gt))
    for t2 in (0.5625, 1.5625, 3.0625):
        b.op("vector", lambda e, t2=t2: e.scalar_tensor_tensor(
            out=qd, in0=qc, scalar=t2, in1=qd, op0=ALU.is_gt, op1=ALU.add))
    b.op("vector", lambda e: e.tensor_scalar(
        out=qe, in0=qc, scalar1=6.25, scalar2=None, op0=ALU.is_gt))
    for t2 in (12.25, 25.0, 25.0):
        b.op("vector", lambda e, t2=t2: e.scalar_tensor_tensor(
            out=qe, in0=qc, scalar=t2, in1=qe, op0=ALU.is_gt, op1=ALU.add))
    b.op("vector", lambda e: e.scalar_tensor_tensor(
        out=qc, in0=qd, scalar=0.5, in1=qe, op0=ALU.mult, op1=ALU.add))
    b.op("vector", lambda e: e.tensor_scalar(
        out=qe, in0=qm, scalar1=0.0, scalar2=2.0, op0=ALU.is_ge, op1=ALU.mult))
    b.op("vector", lambda e: e.tensor_scalar(
        out=qe, in0=qe, scalar1=1.0, scalar2=None, op0=ALU.subtract))
    b.op("vector", lambda e: e.tensor_tensor(
        out=bview(qd), in0=bview(qe), in1=sview(qs), op=ALU.mult))
    b.op("vector", lambda e: e.tensor_tensor(
        out=qout, in0=qc, in1=qd, op=ALU.mult), inc="vE")



def build_launch1():
    nc = bass.Bass("TRN2", target_bir_lowering=False)
    es = contextlib.ExitStack()

    def din(name, shape, dtype=F32):
        return nc.dram_tensor(name, shape, dtype, kind="ExternalInput").ap()

    def dout(name, shape, dtype=F32):
        return nc.dram_tensor(name, shape, dtype, kind="ExternalOutput").ap()

    x_fm = din("x_fm", [D, S])
    x_fm_own = din("x_fm_own", [D, TPC])
    x_tm = din("x_tm", [TPC, D])
    wql = din("wql", [D, D])
    wkl = din("wkl", [D, D])
    wvr = din("wvr", [D, D])
    wor = din("wor", [D, D])
    wg1 = din("wg1", [D, G_FF])
    bg1c = din("bg1c", [128, 2])
    wg2 = din("wg2", [G_FF, E_FF])
    w_in = din("w_in", [E_FF, D, H_FF])
    w_out = din("w_out", [E_FF, H_FF, D])
    masks = din("masks", [4, 128, 128], BF16)
    ident_d = din("ident", [128, 128], BF16)
    ones1f = din("ones1f", [1, 64])
    wg_sl = din("wg_sl", [WSL, D])
    wu_sl = din("wu_sl", [WSL, D])
    wd_sl = din("wd_sl", [D, WSL])

    x1_out = dout("x1", [TPC, D])
    if DEBUG:
        dbg_pre = dout("dbg_pre", [TPC, D])
        dbg_k = dout("dbg_k", [128, 512], BF16)
        dbg_q = dout("dbg_q", [128, 512], BF16)
        dbg_v = dout("dbg_v", [128, 16 * 65], BF16)
        dbg_at = dout("dbg_at", [128, QW], BF16)
        dbg_gate = dout("dbg_gate", [128, E_FF])
        dbg_xfm = dout("dbg_xfm", [128, 512], BF16)
        dbg_gfm = dout("dbg_gfm", [128, 512], BF16)
        dbg_log = dout("dbg_log", [128, E_FF])
        dbg_ln0 = dout("dbg_ln0", [128, D])
        dbg_st = dout("dbg_st", [128, 8])
        dbg_scr = dout("dbg_scr", [128, D])
        dbg_ln1 = dout("dbg_ln1", [128, D])
    wgT_q = dout("wgT_q", [D, WSL], BF16)
    wuT_q = dout("wuT_q", [D, WSL], BF16)
    wdT_q = dout("wdT_q", [WSL, D], BF16)

    K0 = nc.dram_tensor("K0", [D, S], BF16).ap()
    V0 = nc.dram_tensor("V0", [S, 16 * 65], BF16).ap()
    Q0 = nc.dram_tensor("Q0", [D, TPC], BF16).ap()

    with es:
        b = Builder(nc, es)
        for s in ("dS", "dC", "dG", "tE", "vE", "sE", "pE"):
            b.sem(s)

        T = {}
        AW = sbt(nc, es, "AW", [128, 24576], BF16)
        AB = sbt(nc, es, "AB", [128, 8192], BF16)
        wq_v = AW[:, 0:8192]
        wk_v = AW[:, 8192:16384]
        wv_v = AW[:, 16384:24576]
        wo_v = AW[:, 0:8192]
        win_v = AW[:, 0:16384]
        wouth_v = AW[:, 16384:24576]
        s_gfm = AW[:, 0:2 * TPC]
        s_xc = [AB[:, 0:4096], AB[:, 4096:8192]]
        h_v = AB[:, 0:8192]

        s_x = [sbt(nc, es, f"s_x{i}", [128, D], F32) for i in range(NT)]
        s_xfm = sbt(nc, es, "s_xfm", [128, 8 * TPC], BF16)
        s_tbf = sbt(nc, es, "s_tbf", [128, D], BF16)
        s_kv = [sbt(nc, es, f"s_kv{i}", [128, 512], BF16) for i in range(2)]
        s_vst = [sbt(nc, es, f"s_vst{i}", [128, 16 * 65], BF16) for i in range(2)]
        s_qt = [sbt(nc, es, f"s_qt{i}", [128, 512], BF16) for i in range(2)]
        s_kt = [sbt(nc, es, f"s_kt{i}", [128, 512], BF16) for i in range(2)]
        s_vt = [sbt(nc, es, f"s_vt{i}", [128, 4 * 130], BF16) for i in range(2)]
        s_pt = [sbt(nc, es, f"s_pt{i}", [128, 512], BF16) for i in range(4)]
        s_mask = sbt(nc, es, "s_mask", [128, 4 * 128], BF16)
        s_attn = [sbt(nc, es, f"s_attn{p}", [128, QW], BF16) for p in range(8)]
        s_recip = sbt(nc, es, "s_recip", [1, QW], F32)
        s_bcast = sbt(nc, es, "s_bcast", [64, QW], F32)
        s_ones1 = sbt(nc, es, "s_ones1", [1, 64], F32)
        T["ident"] = sbt(nc, es, "s_ident", [128, 128], BF16)
        T["st"] = [sbt(nc, es, f"s_st{i}", [128, 1], F32)[:] for i in range(8)]
        T["eps"] = T["st"][6]
        s_wg1 = sbt(nc, es, "s_wg1", [128, 8 * G_FF], BF16)
        s_bg1 = sbt(nc, es, "s_bg1", [128, 2], F32)
        s_wg2 = sbt(nc, es, "s_wg2", [128, 2 * E_FF], BF16)
        s_gate = [sbt(nc, es, f"s_gate{i}", [128, E_FF], F32) for i in range(NT)]
        s_gsc = sbt(nc, es, "s_gsc", [128, 32], F32)
        T["qs"] = sbt(nc, es, "s_qs", [128, 64], F32)
        T["qs2"] = sbt(nc, es, "s_qs2", [128, 64], F32)
        T["qm"] = s_xfm[:, 0:1024].bitcast(F32)
        T["qc"] = s_xfm[:, 1024:2048].bitcast(F32)
        T["qd"] = s_xfm[:, 2048:3072].bitcast(F32)
        T["qe"] = s_xfm[:, 3072:4096].bitcast(F32)
        s_qin = sbt(nc, es, "s_qin", [128, D], F32)
        T["scr"] = s_qin[:]
        s_qq = sbt(nc, es, "s_qq", [128, D], BF16)
        s_qst = sbt(nc, es, "s_qst", [128, 128], BF16)

        p_yt = pst(nc, es, "p_y", [128, 1024])
        p_y = p_yt[:]
        p_s = [pst(nc, es, "p_s0", [128, 512])[:], pst(nc, es, "p_s1", [128, 512])[:],
               p_yt[:, 0:512], p_yt[:, 512:1024]]
        p_o = [pst(nc, es, f"p_o{i}", [65, 512]) for i in range(2)]
        p_mix = pst(nc, es, "p_mix", [128, 512])
        p_bc = p_mix[0:64, 0:512]
        p_g = p_mix[0:128, 0:16]
        T["p_tr"] = pst(nc, es, "p_tr", [128, 128], BF16)[:]

        with nc.Block() as block:
            for i in range(NT):
                b.dma("scalar", lambda e, i=i: e.dma_start(
                    out=s_x[i][:], in_=x_tm[i * 128:(i + 1) * 128, :]), inc="dC")
            b.dma("scalar", lambda e: e.dma_start(
                out=s_mask[:].rearrange("p (s q) -> p s q", s=4),
                in_=masks.rearrange("s p q -> p s q")), inc="dC")
            b.dma("scalar", lambda e: e.dma_start(out=T["ident"][:], in_=ident_d), inc="dC")
            b.dma("scalar", lambda e: e.dma_start(out=s_ones1[:], in_=ones1f), inc="dC")
            b.dma("scalar", lambda e: e.dma_start(out=s_bg1[:], in_=bg1c), inc="dC")
            for w_d, w_s in [(wql, wq_v), (wkl, wk_v), (wvr, wv_v)]:
                b.dma("gpsimd", lambda e, wd=w_d, ws=w_s: e.dma_start(
                    out=ws.rearrange("p (c f) -> p c f", f=D),
                    in_=wd.rearrange("(c p) f -> p c f", p=128)), inc="dG")
            b.dma("gpsimd", lambda e: e.dma_start(
                out=s_wg1[:].rearrange("p (c f) -> p c f", f=G_FF),
                in_=wg1.rearrange("(c p) f -> p c f", p=128)), inc="dG")
            b.dma("gpsimd", lambda e: e.dma_start(
                out=s_wg2[:].rearrange("p (c f) -> p c f", f=E_FF),
                in_=wg2.rearrange("(c p) f -> p c f", p=128)), inc="dG")
            for i in range(2):
                b.op("gpsimd", lambda e, i=i: e.memset(
                    s_vst[i][:].rearrange("p (h c) -> p h c", c=65)[:, :, 64:65], 1.0),
                    inc="pE")
            b.op("gpsimd", lambda e: e.memset(T["eps"], 1e-5), inc="pE")
            b.wait("scalar", "pE")
            consts_dC = b.count("dC")
            consts_dG = b.count("dG")
            consts_pE = b.count("pE")

            b.wait("scalar", "dC", consts_dC)

            def quant_transpose_out(src_dram, out_dram, rows):
                cols = src_dram.shape[1]
                ntr = ceil_div(rows, 128)
                for rt in range(ntr):
                    p0, p1 = rt * 128, min(rows, (rt + 1) * 128)
                    P = p1 - p0
                    b.wait("scalar", "vE")
                    b.wait("scalar", "tE")
                    b.dma("scalar", lambda e, s=src_dram, p0=p0, p1=p1: e.dma_start(
                        out=s_qin[0:p1 - p0, 0:cols], in_=s[p0:p1, :]), inc="dC")
                    b.wait("vector", "dC")
                    for c0q in range(0, cols, 512):
                        cw = min(512, cols - c0q)
                        emit_quant(b, s_qin[0:P, c0q:c0q + cw],
                                   s_qq[0:P, c0q:c0q + cw], T, P, cw)
                    nct = ceil_div(cols, 128)
                    for ct in range(nct):
                        c0, c1 = ct * 128, min(cols, (ct + 1) * 128)
                        cons = b.count("dC")
                        b.wait("tensor", "vE")
                        b.wait("tensor", "dC", cons)
                        b.op("tensor", lambda e, P=P, c0=c0, c1=c1: e.transpose(
                            T["p_tr"][0:c1 - c0, 0:P], s_qq[0:P, c0:c1],
                            T["ident"][0:P, 0:P]), inc="tE")
                        b.wait("vector", "tE")
                        b.op("vector", lambda e, c0=c0, c1=c1, P=P: e.tensor_copy(
                            out=s_qst[0:c1 - c0, 0:P], in_=T["p_tr"][0:c1 - c0, 0:P]),
                            inc="vE")
                        b.wait("scalar", "vE")
                        b.dma("scalar", lambda e, o=out_dram, c0=c0, c1=c1, p0=p0, p1=p1:
                              e.dma_start(out=o[c0:c1, p0:p1],
                                          in_=s_qst[0:c1 - c0, 0:p1 - p0]), inc="dC")

            quant_transpose_out(wg_sl, wgT_q, WSL)
            quant_transpose_out(wu_sl, wuT_q, WSL)
            quant_transpose_out(wd_sl, wdT_q, D)

            b.wait("tensor", "dG", consts_dG)
            b.wait("tensor", "pE", consts_pE)
            xc_free = [0, 0]
            kbank_free = [0, 0]
            ybank_free = 0
            kvstage_free = [0, 0]
            vstage_free = [0, 0]

            def proj_fm(wview, kc, slot, dst_dram, dst_cols):
                for pc in range(8):
                    bank = p_s[pc % 2]
                    b.wait("tensor", "vE", kbank_free[pc % 2])
                    for dc in range(8):
                        lhsT = wview[:, dc * 1024 + pc * 128: dc * 1024 + (pc + 1) * 128]
                        rhs = s_xc[slot][:, dc * 512:(dc + 1) * 512]
                        b.op("tensor", lambda e, o=bank, l=lhsT, r=rhs,
                             st=(dc == 0), sp=(dc == 7): e.matmul(o, l, r, start=st, stop=sp),
                             inc="tE" if dc == 7 else None)
                    b.wait("vector", "tE")
                    b.wait("vector", "dC", kvstage_free[pc % 2])
                    stg = s_kv[pc % 2]
                    b.op("vector", lambda e, p=bank, s=stg: e.tensor_copy(
                        out=s[:], in_=p[:]), inc="vE")
                    kbank_free[pc % 2] = b.count("vE")
                    b.wait("scalar", "vE")
                    b.dma("scalar", lambda e, s=stg, pc=pc: e.dma_start(
                        out=dst_dram[pc * 128:(pc + 1) * 128,
                                     kc * dst_cols:(kc + 1) * dst_cols],
                        in_=s[:]), inc="dC")
                    kvstage_free[pc % 2] = b.count("dC")

            for kc in range(16):
                slot = kc % 2
                b.wait("gpsimd", "tE", xc_free[slot])
                for dc in range(8):
                    src = x_fm[dc * 128:(dc + 1) * 128, kc * 512:(kc + 1) * 512]
                    b.dma("gpsimd", lambda e, s=src, d=s_xc[slot], dc=dc: e.dma_start(
                        out=d[:, dc * 512:(dc + 1) * 512], in_=s), inc="dG")
                b.wait("tensor", "dG")
                proj_fm(wk_v, kc, slot, K0, 512)
                for sub in range(4):
                    b.wait("tensor", "vE", ybank_free)
                    for half in range(2):
                        for dc in range(8):
                            lhsT = s_xc[slot][:, dc * 512 + sub * 128: dc * 512 + (sub + 1) * 128]
                            rhs = wv_v[:, dc * 1024 + half * 512: dc * 1024 + (half + 1) * 512]
                            b.op("tensor", lambda e, l=lhsT, r=rhs, st=(dc == 0),
                                 sp=(dc == 7), hf=half: e.matmul(
                                     p_y[:, hf * 512:(hf + 1) * 512], l, r, start=st, stop=sp),
                                 inc="tE" if (dc == 7 and half == 1) else None)
                    b.wait("vector", "tE")
                    b.wait("vector", "dC", vstage_free[sub % 2])
                    vstg = s_vst[sub % 2]
                    dstv = vstg[:].rearrange("p (h c) -> p h c", c=65)[:, :, 0:64]
                    b.op("vector", lambda e, d=dstv: e.tensor_copy(
                        out=d, in_=p_y.rearrange("p (h c) -> p h c", c=64)), inc="vE")
                    ybank_free = b.count("vE")
                    b.wait("scalar", "vE")
                    row0 = kc * 512 + sub * 128
                    b.dma("scalar", lambda e, s=vstg, r0=row0: e.dma_start(
                        out=V0[r0:r0 + 128, :], in_=s[:]), inc="dC")
                    vstage_free[sub % 2] = b.count("dC")
                xc_free[slot] = b.count("tE")

            for Jc in range(NCH):
                slot = Jc % 2
                b.wait("tensor", "vE", ybank_free)
                b.wait("gpsimd", "tE", xc_free[slot])
                for dc in range(8):
                    src = x_fm_own[dc * 128:(dc + 1) * 128, Jc * 512:(Jc + 1) * 512]
                    b.dma("gpsimd", lambda e, s=src, d=s_xc[slot], dc=dc: e.dma_start(
                        out=d[:, dc * 512:(dc + 1) * 512], in_=s), inc="dG")
                b.wait("tensor", "dG")
                proj_fm(wq_v, Jc, slot, Q0, 512)
                xc_free[slot] = b.count("tE")
            p1_dC = b.count("dC")

            p1_tE = b.count("tE")
            b.wait("gpsimd", "tE", p1_tE)
            b.dma("gpsimd", lambda e: e.dma_start(
                out=wo_v.rearrange("p (c f) -> p c f", f=D),
                in_=wor.rearrange("(c p) f -> p c f", p=128)), inc="dG")
            wo_dG = b.count("dG")

            b.wait("sync", "dC", p1_dC)
            b.wait("tensor", "dG", wo_dG)
            kt_free = [0, 0]
            vt_free = [0, 0]
            qt_free = [0, 0]
            sb_free = [0, 0]
            pt_free = [0, 0]
            att_free = [0] * 8
            for Jc in range(NCH):
                kmax = 16 * Jc + 16
                for pc in range(8):
                    qslot = pc % 2
                    b.wait("tensor", "vE")
                    b.wait("sync", "tE", qt_free[qslot])
                    b.dma("sync", lambda e, pc=pc, Jc=Jc, d=s_qt[qslot]: e.dma_start(
                        out=d[:], in_=Q0[pc * 128:(pc + 1) * 128,
                                         Jc * 512:(Jc + 1) * 512]), inc="dS")
                    qld = b.count("dS")
                    b.wait("tensor", "dS", qld)
                    for kg in range(kmax // 4):
                        kslot = kg % 2
                        b.wait("sync", "tE", kt_free[kslot])
                        b.dma("sync", lambda e, pc=pc, kg=kg, d=s_kt[kslot]: e.dma_start(
                            out=d[:], in_=K0[pc * 128:(pc + 1) * 128,
                                             kg * 512:(kg + 1) * 512]), inc="dS")
                        kld = b.count("dS")
                        b.wait("sync", "tE", vt_free[kslot])
                        srcv = V0[kg * 512:(kg + 1) * 512, 2 * pc * 65:(2 * pc + 2) * 65]
                        b.dma("sync", lambda e, s=srcv, d=s_vt[kslot]: e.dma_start(
                            out=d[:].rearrange("p (g c) -> p g c", g=4),
                            in_=s.rearrange("(g p) c -> p g c", p=128)), inc="dS")
                        vld = b.count("dS")
                        for ki in range(4):
                            kt = kg * 4 + ki
                            j0 = max(0, ceil_div(kt - 3, 4))
                            c0 = max(0, (j0 - 4 * Jc)) * 128
                            sbank = [p_s[2 * (kt % 2)], p_s[2 * (kt % 2) + 1]]
                            b.wait("tensor", "dS", kld)
                            b.wait("tensor", "sE", sb_free[kt % 2])
                            for hh in range(2):
                                lhsT = s_kt[kslot][hh * 64:(hh + 1) * 64,
                                                   ki * 128:(ki + 1) * 128]
                                rhs = s_qt[qslot][hh * 64:(hh + 1) * 64, c0:512]
                                b.op("tensor", lambda e, o=sbank[hh], l=lhsT, r=rhs, c0=c0:
                                     e.matmul(o[:, c0:512], l, r, start=True, stop=True),
                                     inc="tE")
                            mm_s = b.count("tE")
                            b.wait("scalar", "tE", mm_s)
                            b.wait("scalar", "tE", pt_free[kt % 2])
                            for hh in range(2):
                                b.op("scalar", lambda e, p=sbank[hh],
                                     o=s_pt[2 * (kt % 2) + hh], c0=c0:
                                     e.activation(out=o[:, c0:512], in_=p[:, c0:512],
                                                  func=AF.Exp, scale=0.125), inc="sE")
                            sb_free[kt % 2] = b.count("sE")
                            exp_n = b.count("sE")
                            need_mask = j0 >= 4 * Jc
                            if need_mask:
                                sstar = kt - 4 * j0
                                b.wait("vector", "sE", exp_n)
                                for hh in range(2):
                                    b.op("vector", lambda e, o=s_pt[2 * (kt % 2) + hh],
                                         c0=c0, ss=sstar: e.tensor_tensor(
                                             out=o[:, c0:c0 + 128], in0=o[:, c0:c0 + 128],
                                             in1=s_mask[:, ss * 128:(ss + 1) * 128],
                                             op=ALU.mult), inc="vE")
                                b.wait("tensor", "vE")
                            else:
                                b.wait("tensor", "sE", exp_n)
                            b.wait("tensor", "dS", vld)
                            for hh in range(2):
                                lhsT = s_vt[kslot][:, ki * 130 + hh * 65: ki * 130 + (hh + 1) * 65]
                                rhs = s_pt[2 * (kt % 2) + hh][:, c0:512]
                                b.op("tensor", lambda e, o=p_o[hh], l=lhsT, r=rhs, c0=c0,
                                     st=(kt == 0), sp=(kt == kmax - 1): e.matmul(
                                         o[:, c0:512], l, r, start=st, stop=sp),
                                     inc="tE")
                            pt_free[kt % 2] = b.count("tE")
                        kt_free[kslot] = b.count("tE")
                        vt_free[kslot] = b.count("tE")
                    qt_free[qslot] = b.count("tE")
                    for hh in range(2):
                        b.wait("vector", "tE")
                        b.op("vector", lambda e, p=p_o[hh]: e.reciprocal(
                            out=s_recip[:], in_=p[64:65, :]), inc="vE")
                        rec_n = b.count("vE")
                        b.wait("tensor", "vE", rec_n)
                        b.op("tensor", lambda e: e.matmul(
                            p_bc, s_ones1[:], s_recip[:], start=True, stop=True),
                            inc="tE")
                        bc_n = b.count("tE")
                        b.wait("vector", "tE", bc_n)
                        b.op("vector", lambda e: e.tensor_copy(
                            out=s_bcast[:], in_=p_bc), inc="vE")
                        b.wait("vector", "tE", att_free[pc])
                        b.op("vector", lambda e, p=p_o[hh], pc=pc, hh=hh: e.tensor_tensor(
                            out=s_attn[pc][hh * 64:(hh + 1) * 64, :],
                            in0=p[0:64, :], in1=s_bcast[:], op=ALU.mult), inc="vE")
                attn_ready = b.count("vE")
                b.wait("tensor", "vE", attn_ready)
                for sub in range(4):
                    b.wait("tensor", "vE", ybank_free)
                    for pc in range(8):
                        for half in range(2):
                            lhsT = s_attn[pc][:, sub * 128:(sub + 1) * 128]
                            rhs = wo_v[:, pc * 1024 + half * 512: pc * 1024 + (half + 1) * 512]
                            b.op("tensor", lambda e, l=lhsT, r=rhs, hf=half,
                                 st=(pc == 0), sp=(pc == 7): e.matmul(
                                     p_y[:, hf * 512:(hf + 1) * 512], l, r,
                                     start=st, stop=sp),
                                 inc="tE" if (pc == 7 and half == 1) else None)
                    for pc in range(8):
                        att_free[pc] = b.count("tE")
                    b.wait("vector", "tE")
                    ti = Jc * 4 + sub
                    b.op("vector", lambda e, ti=ti: e.tensor_tensor(
                        out=s_x[ti][:], in0=p_y, in1=s_x[ti][:], op=ALU.add),
                        inc="vE")
                    ybank_free = b.count("vE")

            if DEBUG:
                pre_v = b.count("vE")
                b.wait("scalar", "vE", pre_v)
                for i in range(NT):
                    b.dma("scalar", lambda e, i=i: e.dma_start(
                        out=dbg_pre[i * 128:(i + 1) * 128, :], in_=s_x[i][:]), inc="dC")
                b.dma("scalar", lambda e: e.dma_start(out=dbg_k, in_=K0[0:128, 0:512]), inc="dC")
                b.dma("scalar", lambda e: e.dma_start(out=dbg_q, in_=Q0[0:128, 0:512]), inc="dC")
                b.dma("scalar", lambda e: e.dma_start(out=dbg_v, in_=V0[0:128, :]), inc="dC")
                b.dma("scalar", lambda e: e.dma_start(out=dbg_at, in_=s_attn[0][:]), inc="dC")
            for i in range(NT):
                b.wait("vector", "tE")
                emit_ln(b, [s_x[i][:]], [s_tbf[:]], T)
                if DEBUG and i == 0:
                    dn = b.count("vE")
                    b.wait("scalar", "vE", dn)
                    b.wait("scalar", "sE")
                    b.dma("scalar", lambda e: e.dma_start(out=dbg_scr, in_=T["scr"]), inc="dC")
                    for jj in range(8):
                        b.op("vector", lambda e, jj=jj: e.tensor_copy(
                            out=s_qin[:, 512 + jj:512 + jj + 1], in_=T["st"][jj]), inc="vE")
                    b.wait("scalar", "vE")
                    b.dma("scalar", lambda e: e.dma_start(
                        out=dbg_st, in_=s_qin[:, 512:520]), inc="dC")
                for dc in range(8):
                    srct = s_tbf[:, dc * 128:(dc + 1) * 128]
                    cons = b.count("vE")
                    b.wait("tensor", "vE", cons)
                    b.op("tensor", lambda e, s=srct: e.transpose(
                        T["p_tr"], s, T["ident"][:]), inc="tE")
                    b.wait("vector", "tE")
                    dst = s_xfm[:, dc * TPC + i * 128: dc * TPC + (i + 1) * 128]
                    b.op("vector", lambda e, d=dst: e.tensor_copy(
                        out=d, in_=T["p_tr"]), inc="vE")
            fm_ready_vE = b.count("vE")
            if DEBUG:
                b.wait("scalar", "vE", fm_ready_vE)
                b.dma("scalar", lambda e: e.dma_start(out=dbg_xfm, in_=s_xfm[:, 0:512]), inc="dC")
                b.dma("scalar", lambda e: e.dma_start(out=dbg_ln0, in_=s_x[0][:]), inc="dC")
                b.dma("scalar", lambda e: e.dma_start(out=dbg_ln1, in_=s_x[1][:]), inc="dC")

            b.wait("tensor", "vE", fm_ready_vE)
            glog_free = 0
            for Jc in range(NCH):
                for gc in range(2):
                    bank = p_s[gc]
                    b.wait("tensor", "sE", sb_free[gc])
                    for dc in range(8):
                        lhsT = s_wg1[:, dc * G_FF + gc * 128: dc * G_FF + (gc + 1) * 128]
                        rhs = s_xfm[:, dc * TPC + Jc * 512: dc * TPC + (Jc + 1) * 512]
                        b.op("tensor", lambda e, o=bank, l=lhsT, r=rhs, st=(dc == 0),
                             sp=(dc == 7): e.matmul(o, l, r, start=st, stop=sp),
                             inc="tE")
                    b.wait("scalar", "tE")
                    b.op("scalar", lambda e, p=bank, gc=gc, Jc=Jc: e.activation(
                        out=s_gfm[:, gc * TPC + Jc * 512: gc * TPC + (Jc + 1) * 512],
                        in_=p[:], func=AF.Gelu, bias=s_bg1[:, gc:gc + 1]), inc="sE")
                    sb_free[gc] = b.count("sE")
            gelu_all = b.count("sE")
            b.wait("tensor", "sE", gelu_all)
            for ti in range(NT):
                b.wait("tensor", "vE", glog_free)
                for gc in range(2):
                    lhsT = s_gfm[:, gc * TPC + ti * 128: gc * TPC + (ti + 1) * 128]
                    rhs = s_wg2[:, gc * E_FF:(gc + 1) * E_FF]
                    b.op("tensor", lambda e, l=lhsT, r=rhs, st=(gc == 0), sp=(gc == 1):
                         e.matmul(p_g[:, 0:E_FF], l, r, start=st, stop=sp),
                         inc="tE" if gc == 1 else None)
                b.wait("vector", "tE")
                g0 = s_gsc[:, 0:8]
                gex = s_gsc[:, 8:16]
                gpr = s_gsc[:, 16:24]
                st0 = T["st"]
                m1, negm = st0[0], st0[3]
                sm, rc = st0[1], st0[2]
                m1p, m2 = st0[4], st0[5]
                b.op("vector", lambda e: e.tensor_copy(out=g0, in_=p_g[:, 0:E_FF]),
                     inc="vE")
                glog_free = b.count("vE")
                if DEBUG and ti == 0:
                    b.wait("scalar", "vE", glog_free)
                    b.dma("scalar", lambda e: e.dma_start(out=dbg_log, in_=g0), inc="dC")
                b.op("vector", lambda e: e.tensor_reduce(
                    out=m1, in_=g0, axis=AX, op=ALU.max))
                b.op("vector", lambda e: e.tensor_scalar(
                    out=negm, in0=m1, scalar1=-1.0, scalar2=None, op0=ALU.mult),
                    inc="vE")
                b.wait("scalar", "vE")
                b.op("scalar", lambda e: e.activation(
                    out=gex, in_=g0, func=AF.Exp, bias=negm), inc="sE")
                b.wait("vector", "sE")
                b.op("vector", lambda e: e.tensor_reduce(
                    out=sm, in_=gex, axis=AX, op=ALU.add))
                b.op("vector", lambda e: e.reciprocal(out=rc, in_=sm))
                b.op("vector", lambda e: e.tensor_scalar(
                    out=gpr, in0=gex, scalar1=rc, scalar2=None, op0=ALU.mult))
                b.op("vector", lambda e: e.tensor_reduce(
                    out=m1p, in_=gpr, axis=AX, op=ALU.max))
                b.op("vector", lambda e: e.tensor_scalar(
                    out=gex, in0=gpr, scalar1=m1p, scalar2=None, op0=ALU.is_lt))
                b.op("vector", lambda e: e.tensor_tensor(
                    out=gex, in0=gex, in1=gpr, op=ALU.mult))
                b.op("vector", lambda e: e.tensor_reduce(
                    out=m2, in_=gex, axis=AX, op=ALU.max))
                b.op("vector", lambda e: e.tensor_scalar(
                    out=gex, in0=gpr, scalar1=m2, scalar2=None, op0=ALU.is_ge))
                b.op("vector", lambda e, ti=ti: e.tensor_tensor(
                    out=s_gate[ti][:], in0=gex, in1=gpr, op=ALU.mult), inc="vE")

            h_free = 0
            w_in_r = w_in.rearrange("e (c p) f -> p e c f", p=128)
            w_out_r = w_out.rearrange("e (c p) (h f) -> p e c h f", p=128, h=2)
            for ex in range(E_FF):
                b.wait("gpsimd", "tE")
                b.dma("gpsimd", lambda e, ex=ex: e.dma_start(
                    out=win_v.rearrange("p (o c f) -> p o c f", o=1, f=H_FF),
                    in_=w_in_r[:, ex:ex + 1]), inc="dG")
                wld = b.count("dG")
                b.wait("tensor", "dG", wld)
                for tt in range(NCH):
                    b.wait("tensor", "tE", h_free)
                    for hf in range(16):
                        bank = p_s[hf % 2]
                        b.wait("tensor", "sE", sb_free[hf % 2])
                        for dc in range(8):
                            lhsT = win_v[:, dc * H_FF + hf * 128: dc * H_FF + (hf + 1) * 128]
                            rhs = s_xfm[:, dc * TPC + tt * 512: dc * TPC + (tt + 1) * 512]
                            b.op("tensor", lambda e, o=bank, l=lhsT, r=rhs, st=(dc == 0),
                                 sp=(dc == 7): e.matmul(o, l, r, start=st, stop=sp),
                                 inc="tE")
                        b.wait("scalar", "tE")
                        b.op("scalar", lambda e, p=bank, hf=hf: e.activation(
                            out=h_v[:, hf * 512:(hf + 1) * 512], in_=p[:], func=AF.Gelu),
                            inc="sE")
                        sb_free[hf % 2] = b.count("sE")
                    hdone = b.count("sE")
                    b.wait("tensor", "sE", hdone)
                    for half in range(2):
                        b.wait("gpsimd", "tE")
                        b.dma("gpsimd", lambda e, ex=ex, half=half: e.dma_start(
                            out=wouth_v.rearrange("p (a c o f) -> p a c o f",
                                                  a=1, o=1, f=512),
                            in_=w_out_r[:, ex:ex + 1, :, half:half + 1, :]), inc="dG")
                        whld = b.count("dG")
                        b.wait("tensor", "dG", whld)
                        for sub in range(4):
                            ti = tt * 4 + sub
                            b.wait("tensor", "vE", ybank_free)
                            for hf in range(16):
                                lhsT = h_v[:, hf * 512 + sub * 128: hf * 512 + (sub + 1) * 128]
                                rhs = wouth_v[:, hf * 512:(hf + 1) * 512]
                                b.op("tensor", lambda e, l=lhsT, r=rhs, half=half,
                                     st=(hf == 0), sp=(hf == 15): e.matmul(
                                         p_y[:, half * 512:(half + 1) * 512], l, r,
                                         start=st, stop=sp),
                                     inc="tE" if hf == 15 else None)
                            b.wait("vector", "tE")
                            b.op("vector", lambda e, ti=ti, ex=ex, half=half:
                                 e.scalar_tensor_tensor(
                                     out=s_x[ti][:, half * 512:(half + 1) * 512],
                                     in0=p_y[:, half * 512:(half + 1) * 512],
                                     scalar=s_gate[ti][:, ex:ex + 1],
                                     in1=s_x[ti][:, half * 512:(half + 1) * 512],
                                     op0=ALU.mult, op1=ALU.add), inc="vE")
                            ybank_free = b.count("vE")
                    h_free = b.count("tE")

            if DEBUG:
                gv = b.count("vE")
                b.wait("scalar", "vE", gv)
                b.dma("scalar", lambda e: e.dma_start(out=dbg_gate, in_=s_gate[0][:]), inc="dC")
                b.dma("scalar", lambda e: e.dma_start(out=dbg_gfm, in_=s_gfm[:, 0:512]), inc="dC")
            emit_ln(b, [s_x[i][:] for i in range(NT)], None, T)
            lnv = b.count("vE")
            b.wait("scalar", "vE", lnv)
            for i in range(NT):
                b.dma("scalar", lambda e, i=i: e.dma_start(
                    out=x1_out[i * 128:(i + 1) * 128, :], in_=s_x[i][:]), inc="dC")
            b.wait("scalar", "dC", b.count("dC"))
            b.finalize(block)
    return nc



def build_launch2():
    nc = bass.Bass("TRN2", target_bir_lowering=False)
    es = contextlib.ExitStack()

    def din(name, shape, dtype=F32):
        return nc.dram_tensor(name, shape, dtype, kind="ExternalInput").ap()

    def dout(name, shape, dtype=F32):
        return nc.dram_tensor(name, shape, dtype, kind="ExternalOutput").ap()

    x_fm = din("x_fm", [D, S])
    x_fm_own = din("x_fm_own", [D, TPC])
    x_tm = din("x_tm", [TPC, D])
    wqT = din("wqT", [D, D])
    wkT = din("wkT", [D, D])
    wvT = din("wvT", [D, D])
    woT = din("woT", [D, D])
    bq_c = din("bq_c", [128, 8])
    bk_c = din("bk_c", [128, 8])
    bv_r = din("bv_r", [1, D])
    bo_r = din("bo_r", [1, D])
    wgq = din("wgq", [D, I_FF], BF16)
    wuq = din("wuq", [D, I_FF], BF16)
    wdq = din("wdq", [I_FF, D], BF16)
    band = din("band", [128, 31], BF16)
    ident_d = din("ident", [128, 128], BF16)
    ones1f = din("ones1f", [1, 64])
    ones1b = din("ones1b", [1, 128], BF16)
    identf_d = din("identf", [128, 128])

    y_out = dout("y", [TPC, D])
    if DEBUG:
        dbg_k2 = dout("dbg_k2", [128, NKT * 16], BF16)
        dbg_q1 = dout("dbg_q1", [65, 512], BF16)
        dbg_pt = dout("dbg_pt", [128, 512], BF16)
        dbg_den = dout("dbg_den", [1, 512])
        dbg_k2d = dout("dbg_k2d", [16, 512], BF16)
        dbg_xq = dout("dbg_xq", [128, 1024], BF16)
        dbg_hq = dout("dbg_hq", [128, 2816], BF16)
        dbg_g = dout("dbg_g", [128, 512])
        dbg_up = dout("dbg_up", [128, 512])

    K1 = nc.dram_tensor("K1", [D, S], BF16).ap()
    K2D = nc.dram_tensor("K2D", [16, S], BF16).ap()
    V1 = nc.dram_tensor("V1", [S, 16 * 65], BF16).ap()
    Q1 = nc.dram_tensor("Q1", [16 * 65, TPC], BF16).ap()

    JCH = [512, 512, 512, 512, 512, 256]
    JOF = [0, 512, 1024, 1536, 2048, 2560]

    with es:
        b = Builder(nc, es)
        for s in ("dS", "dC", "dG", "tE", "vE", "sE", "pE"):
            b.sem(s)

        T = {}
        AW = sbt(nc, es, "AW", [128, 24576], BF16)
        AB = sbt(nc, es, "AB", [128, 8192], BF16)
        wq_v = AW[:, 0:8192]
        wk_v = AW[:, 8192:16384]
        wv_v = AW[:, 16384:24576]
        wo_v = AW[:, 0:8192]
        wgc_v = [AW[:, 0:4096], AW[:, 4096:8192]]
        wuc_v = [AW[:, 8192:12288], AW[:, 12288:16384]]
        wdc_v = [AW[:, 16384:18432], AW[:, 18432:20480]]
        s_xc = [AB[:, 0:4096], AB[:, 4096:8192]]
        hq_fm = AB[:, 0:5632]

        s_x = [sbt(nc, es, f"s_x{i}", [128, D], F32) for i in range(NT)]
        s_xfm = sbt(nc, es, "s_xfm", [128, 8 * TPC], BF16)
        s_tbf = sbt(nc, es, "s_tbf", [128, D], BF16)
        s_kv = [sbt(nc, es, f"s_kv{i}", [128, 512], BF16) for i in range(2)]
        s_vst = [sbt(nc, es, f"s_vst{i}", [128, 16 * 65], BF16) for i in range(2)]
        s_qp = sbt(nc, es, "s_qp", [65, 512], BF16)
        s_ka = [sbt(nc, es, f"s_ka{i}", [65, 512], BF16) for i in range(2)]
        s_vt = [sbt(nc, es, f"s_vt{i}", [128, 4 * 65], BF16) for i in range(2)]
        s_pt = [sbt(nc, es, f"s_pt{i}", [128, 512], BF16) for i in range(2)]

        s_attn = [sbt(nc, es, f"s_attn{p}", [128, QW], BF16) for p in range(8)]
        s_recip = sbt(nc, es, "s_recip", [1, QW], F32)
        s_bcast = sbt(nc, es, "s_bcast", [64, QW], F32)
        s_ones1 = sbt(nc, es, "s_ones1", [1, 64], F32)
        s_ones1b = sbt(nc, es, "s_ones1b", [1, 128], BF16)
        s_band = sbt(nc, es, "s_band", [128, 31], BF16)
        s_identf = sbt(nc, es, "s_identf", [128, 128], F32)
        s_bq = sbt(nc, es, "s_bq", [128, 8], F32)
        s_bk = sbt(nc, es, "s_bk", [128, 8], F32)
        s_bv = sbt(nc, es, "s_bv", [1, D], BF16)
        s_bo = sbt(nc, es, "s_bo", [1, D], BF16)
        s_k2 = sbt(nc, es, "s_k2", [128, NKT * 16], BF16)
        T["ident"] = sbt(nc, es, "s_ident", [128, 128], BF16)
        T["st"] = [sbt(nc, es, f"s_st{i}", [128, 1], F32)[:] for i in range(8)]
        T["eps"] = T["st"][6]
        s_g = sbt(nc, es, "s_g", [128, 512], F32)
        s_up = sbt(nc, es, "s_up", [128, 512], F32)
        s_h = sbt(nc, es, "s_h", [128, 512], F32)
        s_u = s_g
        s_v2 = s_up
        T["qs"] = sbt(nc, es, "s_qs", [128, 64], F32)
        T["qs2"] = sbt(nc, es, "s_qs2", [128, 64], F32)
        T["qm"] = AW[:, 20480:21504].bitcast(F32)
        T["qc"] = AW[:, 21504:22528].bitcast(F32)
        T["qd"] = AW[:, 22528:23552].bitcast(F32)
        T["qe"] = AW[:, 23552:24576].bitcast(F32)
        s_sqscr = sbt(nc, es, "s_sqscr", [128, 512], BF16)
        s_qq = sbt(nc, es, "s_qq", [128, 512], BF16)
        T["scr"] = s_tbf[:]

        p_yt = pst(nc, es, "p_y", [128, 1024])
        p_y = p_yt[:]
        p_s = [pst(nc, es, "p_s0", [128, 512])[:], pst(nc, es, "p_s1", [128, 512])[:],
               p_yt[:, 0:512], p_yt[:, 512:1024]]
        p_of = pst(nc, es, "p_of", [128, 512])
        p_o = p_of[0:65, 0:512]
        p_mix = pst(nc, es, "p_mix", [128, 512])
        p_mix2 = pst(nc, es, "p_mix2", [128, 512])
        p_bc = p_mix[0:64, 0:512]
        p_k2 = [p_mix[0:8, 0:512], p_mix2[0:8, 0:512]]
        p_q2 = [p_mix[0:1, 0:512], p_mix2[0:1, 0:512]]
        T["p_tr"] = pst(nc, es, "p_tr", [128, 128], BF16)[:]

        with nc.Block() as block:
            for i in range(NT):
                b.dma("scalar", lambda e, i=i: e.dma_start(
                    out=s_x[i][:], in_=x_tm[i * 128:(i + 1) * 128, :]), inc="dC")
            for dst, srcd in [(T["ident"][:], ident_d), (s_ones1[:], ones1f),
                              (s_ones1b[:], ones1b), (s_band[:], band),
                              (s_identf[:], identf_d),
                              (s_bq[:], bq_c), (s_bk[:], bk_c)]:
                b.dma("scalar", lambda e, d=dst, s=srcd: e.dma_start(out=d, in_=s), inc="dC")
            b.dma("gpsimd", lambda e: e.dma_start(out=s_bv[:], in_=bv_r), inc="dG")
            b.dma("gpsimd", lambda e: e.dma_start(out=s_bo[:], in_=bo_r), inc="dG")
            for w_d, w_s in [(wqT, wq_v), (wkT, wk_v), (wvT, wv_v)]:
                b.dma("gpsimd", lambda e, wd=w_d, ws=w_s: e.dma_start(
                    out=ws.rearrange("p (c f) -> p c f", f=D),
                    in_=wd.rearrange("(c p) f -> p c f", p=128)), inc="dG")
            for i in range(2):
                b.op("gpsimd", lambda e, i=i: e.memset(
                    s_vst[i][:].rearrange("p (h c) -> p h c", c=65)[:, :, 64:65], 1.0))
                b.op("gpsimd", lambda e, i=i: e.memset(s_ka[i][64:65, :], 1.0))
            b.op("gpsimd", lambda e: e.memset(T["eps"], 1e-5), inc="pE")
            b.wait("scalar", "pE")
            b.wait("tensor", "pE")
            consts_dC = b.count("dC")
            consts_dG = b.count("dG")
            b.wait("tensor", "dG", consts_dG)
            b.wait("scalar", "dC", consts_dC)
            b.wait("scalar", "dG", consts_dG)

            if L2_PHASE < 1:
                b.wait("scalar", "dC", consts_dC)
                b.wait("scalar", "dG", consts_dG)
                for i in range(NT):
                    b.dma("scalar", lambda e, i=i: e.dma_start(
                        out=y_out[i * 128:(i + 1) * 128, :], in_=s_x[i][:]), inc="dC")
                b.wait("scalar", "dC", b.count("dC"))
                b.finalize(block)
                return nc
            xc_free = [0, 0]
            kbank_free = [0, 0]
            ybank_free = 0
            kvstage_free = [0, 0]
            vstage_free = [0, 0]
            qm_free = 0
            sqb_free = [0, 0]
            for kc in range(16):
                slot = kc % 2
                b.wait("gpsimd", "tE", xc_free[slot])
                for dc in range(8):
                    src = x_fm[dc * 128:(dc + 1) * 128, kc * 512:(kc + 1) * 512]
                    b.dma("gpsimd", lambda e, s=src, d=s_xc[slot], dc=dc: e.dma_start(
                        out=d[:, dc * 512:(dc + 1) * 512], in_=s), inc="dG")
                b.wait("tensor", "dG")
                b.wait("tensor", "vE")
                for pc in range(8):
                    bank = p_s[pc % 2]
                    b.wait("tensor", "vE", kbank_free[pc % 2])
                    b.wait("tensor", "sE", sqb_free[pc % 2])
                    for dc in range(8):
                        lhsT = wk_v[:, dc * 1024 + pc * 128: dc * 1024 + (pc + 1) * 128]
                        rhs = s_xc[slot][:, dc * 512:(dc + 1) * 512]
                        b.op("tensor", lambda e, o=bank, l=lhsT, r=rhs,
                             st=(dc == 0), sp=(dc == 7): e.matmul(o, l, r, start=st, stop=sp),
                             inc="tE" if dc == 7 else None)
                    mmk = b.count("tE")
                    b.wait("vector", "tE", mmk)
                    b.wait("vector", "dC", kvstage_free[pc % 2])
                    stg = s_kv[pc % 2]
                    b.op("vector", lambda e, p=bank, s=stg, pc=pc: e.tensor_scalar(
                        out=s[:], in0=p, scalar1=s_bk[:, pc:pc + 1],
                        scalar2=-2.0, op0=ALU.add, op1=ALU.mult), inc="vE")
                    kbank_free[pc % 2] = b.count("vE")
                    b.wait("scalar", "vE")
                    b.dma("scalar", lambda e, s=stg, pc=pc, kc=kc: e.dma_start(
                        out=K1[pc * 128:(pc + 1) * 128, kc * 512:(kc + 1) * 512],
                        in_=s[:]), inc="dC")
                    kvstage_free[pc % 2] = b.count("dC")
                    if L2_K2:
                        b.wait("scalar", "tE", mmk)
                        b.wait("scalar", "tE", qm_free)
                        b.wait("scalar", "vE", kbank_free[pc % 2])
                        if L2_K2SQ:
                            b.op("scalar", lambda e, p=bank, pc=pc: e.activation(
                                out=s_sqscr[:], in_=p, func=AF.Square,
                                bias=s_bk[:, pc:pc + 1]), inc="sE")
                        else:
                            b.op("scalar", lambda e, p=bank: e.copy(
                                out=s_sqscr[:], in_=p), inc="sE")
                        sq_n = b.count("sE")
                        b.wait("tensor", "sE", sq_n)
                        for hl in range(2):
                            lhsT = s_band[hl * 64:(hl + 1) * 64, 15 - pc:23 - pc]
                            rhs = s_sqscr[hl * 64:(hl + 1) * 64, 0:512]
                            b.op("tensor", lambda e, l=lhsT, r=rhs, hl=hl,
                                 st=(pc == 0), sp=(pc == 7): e.matmul(
                                     p_k2[hl], l, r, start=st, stop=sp), inc="tE")
                        qm_free = b.count("tE")
                        sqb_free[pc % 2] = b.count("sE")
                if L2_K2:
                    k2mm = b.count("tE")
                    for par in range(2):
                        b.wait("vector", "tE", k2mm)
                        b.wait("vector", "dC")
                        b.op("vector", lambda e, par=par: e.tensor_copy(
                            out=s_pt[1][0:8, 0:512], in_=p_k2[par]), inc="vE")
                        b.wait("scalar", "vE")
                        b.dma("scalar", lambda e, par=par, kc=kc: e.dma_start(
                            out=K2D[par * 8:(par + 1) * 8, kc * 512:(kc + 1) * 512],
                            in_=s_pt[1][0:8, 0:512]), inc="dC")
                for sub in range(4 if L2_P1N >= 2 else 0):
                    b.wait("tensor", "vE", ybank_free)
                    for half in range(2):
                        for dc in range(8):
                            lhsT = s_xc[slot][:, dc * 512 + sub * 128: dc * 512 + (sub + 1) * 128]
                            rhs = wv_v[:, dc * 1024 + half * 512: dc * 1024 + (half + 1) * 512]
                            b.op("tensor", lambda e, l=lhsT, r=rhs, st=(dc == 0),
                                 hf=half: e.matmul(
                                     p_y[:, hf * 512:(hf + 1) * 512], l, r, start=st,
                                     stop=False), inc=None)
                        if L2_VBIAS:
                            b.op("tensor", lambda e, hf=half: e.matmul(
                                p_y[:, hf * 512:(hf + 1) * 512], s_ones1b[:],
                                s_bv[:, hf * 512:(hf + 1) * 512], start=False, stop=True),
                                inc="tE")
                        else:
                            b.op("tensor", lambda e, hf=half: e.matmul(
                                p_y[:, hf * 512:(hf + 1) * 512],
                                s_xc[slot][:, 0:128], wv_v[:, 0:512],
                                start=False, stop=True), inc="tE")
                    b.wait("vector", "tE")
                    b.wait("vector", "dC", vstage_free[sub % 2])
                    vstg = s_vst[sub % 2]
                    dstv = vstg[:].rearrange("p (h c) -> p h c", c=65)[:, :, 0:64]
                    b.op("vector", lambda e, d=dstv: e.tensor_copy(
                        out=d, in_=p_y.rearrange("p (h c) -> p h c", c=64)), inc="vE")
                    ybank_free = b.count("vE")
                    b.wait("scalar", "vE")
                    row0 = kc * 512 + sub * 128
                    b.dma("scalar", lambda e, s=vstg, r0=row0: e.dma_start(
                        out=V1[r0:r0 + 128, :], in_=s[:]), inc="dC")
                    vstage_free[sub % 2] = b.count("dC")
                xc_free[slot] = b.count("tE")

            for Jc in range(NCH if L2_P1N >= 3 else 0):
                slot = Jc % 2
                b.wait("tensor", "vE", ybank_free)
                b.wait("gpsimd", "tE", xc_free[slot])
                for dc in range(8):
                    src = x_fm_own[dc * 128:(dc + 1) * 128, Jc * 512:(Jc + 1) * 512]
                    b.dma("gpsimd", lambda e, s=src, d=s_xc[slot], dc=dc: e.dma_start(
                        out=d[:, dc * 512:(dc + 1) * 512], in_=s), inc="dG")
                b.wait("tensor", "dG")
                for pc in range(8):
                    bank = p_s[2 + pc % 2]
                    b.wait("tensor", "vE", kbank_free[pc % 2])
                    b.wait("tensor", "sE", sqb_free[pc % 2])
                    for dc in range(8):
                        lhsT = wq_v[:, dc * 1024 + pc * 128: dc * 1024 + (pc + 1) * 128]
                        rhs = s_xc[slot][:, dc * 512:(dc + 1) * 512]
                        b.op("tensor", lambda e, o=bank, l=lhsT, r=rhs,
                             st=(dc == 0), sp=(dc == 7): e.matmul(o, l, r, start=st, stop=sp),
                             inc="tE" if dc == 7 else None)
                    mmq = b.count("tE")
                    b.wait("vector", "tE", mmq)
                    b.wait("vector", "dC", kvstage_free[pc % 2])
                    stg = s_kv[pc % 2]
                    b.op("vector", lambda e, p=bank, s=stg, pc=pc: e.tensor_scalar(
                        out=s[:], in0=p, scalar1=s_bq[:, pc:pc + 1], scalar2=None,
                        op0=ALU.add), inc="vE")
                    evq = b.count("vE")
                    b.wait("scalar", "vE", evq)
                    for hl in range(2):
                        h = 2 * pc + hl
                        b.dma("scalar", lambda e, s=stg, h=h, hl=hl, Jc=Jc: e.dma_start(
                            out=Q1[h * 65:h * 65 + 64, Jc * 512:(Jc + 1) * 512],
                            in_=s[hl * 64:(hl + 1) * 64, :]), inc="dC")
                    kvstage_free[pc % 2] = b.count("dC")
                    if L2_Q2:
                        b.wait("scalar", "tE", mmq)
                        b.wait("scalar", "tE", qm_free)
                        b.wait("scalar", "vE", evq)
                        b.op("scalar", lambda e, p=bank, pc=pc: e.activation(
                            out=s_sqscr[:], in_=p, func=AF.Square,
                            bias=s_bq[:, pc:pc + 1]), inc="sE")
                        sq_n = b.count("sE")
                        b.wait("tensor", "sE", sq_n)
                        for hl in range(2):
                            h = 2 * pc + hl
                            b.wait("tensor", "vE")
                            lhsT = s_band[hl * 64:(hl + 1) * 64, 15:16]
                            rhs = s_sqscr[hl * 64:(hl + 1) * 64, 0:512]
                            b.op("tensor", lambda e, l=lhsT, r=rhs, hl=hl: e.matmul(
                                p_q2[hl], l, r, start=True, stop=True), inc="tE")
                            q2mm = b.count("tE")
                            b.wait("vector", "tE", q2mm)
                            b.wait("vector", "dC")
                            b.op("vector", lambda e, hl=hl: e.tensor_copy(
                                out=s_pt[0][0:1, 0:512], in_=p_q2[hl]), inc="vE")
                            q2c = b.count("vE")
                            b.wait("scalar", "vE", q2c)
                            b.dma("scalar", lambda e, h=h, Jc=Jc: e.dma_start(
                                out=Q1[h * 65 + 64:h * 65 + 65, Jc * 512:(Jc + 1) * 512],
                                in_=s_pt[0][0:1, 0:512]), inc="dC")
                        qm_free = b.count("tE")
                        sqb_free[pc % 2] = b.count("sE")
                xc_free[slot] = b.count("tE")
            p1_dC = b.count("dC")

            p1_tE = b.count("tE")
            b.wait("gpsimd", "tE", p1_tE)
            b.dma("gpsimd", lambda e: e.dma_start(
                out=wo_v.rearrange("p (c f) -> p c f", f=D),
                in_=woT.rearrange("(c p) f -> p c f", p=128)), inc="dG")
            wo_dG = b.count("dG")

            if L2_PHASE < 2:
                emit_ln(b, [], None, T)
                lnv = b.count("vE")
                b.wait("scalar", "vE", lnv)
                b.wait("scalar", "tE")
                b.wait("scalar", "sE")
                b.wait("scalar", "dS")
                b.wait("scalar", "dG")
                b.wait("scalar", "dC")
                for i in range(NT):
                    b.dma("scalar", lambda e, i=i: e.dma_start(
                        out=y_out[i * 128:(i + 1) * 128, :], in_=s_x[i][:]), inc="dC")
                b.wait("scalar", "dC", b.count("dC"))
                b.finalize(block)
                return nc
            b.wait("sync", "dC", p1_dC)
            if L2_K2:
                for kt in range(NKT):
                    b.dma("sync", lambda e, kt=kt: e.dma_start(
                        out=s_k2[:, kt * 16:(kt + 1) * 16],
                        in_=K2D[0:16, kt * 128:(kt + 1) * 128], transpose=True),
                        inc="dS")
                b.wait("scalar", "dS")
            if DEBUG:
                b.dma("scalar", lambda e: e.dma_start(out=dbg_k2, in_=s_k2[:]), inc="dC")
                b.dma("scalar", lambda e: e.dma_start(
                    out=dbg_k2d, in_=K2D[0:16, 0:512]), inc="dC")
                b.dma("scalar", lambda e: e.dma_start(
                    out=dbg_q1, in_=Q1[0:65, 0:512]), inc="dC")
            b.wait("tensor", "dG", wo_dG)
            ka_free = [0, 0]
            vt_free = [0, 0]
            qp_free = 0
            sb_free = [0, 0]
            chain_free = 0
            att_free = [0] * 8
            s_ub = [s_u[:], s_h[:]]
            s_vb = [s_v2[:], T["qm"]]
            last_e5 = [0, 0]
            last_em = [0, 0]
            last_av = [0, 0]
            for Jc in range(NCH):
                for pc in range(8):
                    for hl in range(2):
                        h = 2 * pc + hl
                        b.wait("tensor", "vE")
                        b.wait("sync", "tE", qp_free)
                        b.dma("sync", lambda e, h=h, Jc=Jc: e.dma_start(
                            out=s_qp[:], in_=Q1[h * 65:(h + 1) * 65,
                                               Jc * 512:(Jc + 1) * 512]), inc="dS")
                        qld = b.count("dS")
                        b.wait("tensor", "dS", qld)
                        hcol = (h // 2) + 8 * (h % 2)
                        mm_c = [0] * NKT
                        ln_c = [0] * NKT
                        e5_c = [0] * NKT
                        em_c = [0] * NKT
                        av_c = [0] * NKT
                        kld_c = [0] * (NKT // 4)
                        vld_c = [0] * (NKT // 4)
                        for it in range(-1, NKT + 2):
                            ks = it + 1
                            if 0 <= ks < NKT:
                                kg, ki = ks // 4, ks % 4
                                kslot = kg % 2
                                if ki == 0:
                                    b.wait("sync", "tE", ka_free[kslot])
                                    b.dma("sync", lambda e, h=h, kg=kg, d=s_ka[kslot]:
                                          e.dma_start(out=d[0:64, :],
                                                      in_=K1[h * 64:(h + 1) * 64,
                                                             kg * 512:(kg + 1) * 512]),
                                          inc="dS")
                                    kld_c[kg] = b.count("dS")
                                    b.wait("sync", "tE", vt_free[kslot])
                                    srcv = V1[kg * 512:(kg + 1) * 512,
                                              h * 65:(h + 1) * 65]
                                    b.dma("sync", lambda e, s=srcv, d=s_vt[kslot]:
                                          e.dma_start(
                                              out=d[:].rearrange("p (g c) -> p g c", g=4),
                                              in_=s.rearrange("(g p) c -> p g c", p=128)),
                                          inc="dS")
                                    vld_c[kg] = b.count("dS")
                                sbank = p_s[ks % 2]
                                b.wait("tensor", "dS", kld_c[kg])
                                b.wait("tensor", "sE", sb_free[ks % 2])
                                lhsT = s_ka[kslot][:, ki * 128:(ki + 1) * 128]
                                b.op("tensor", lambda e, o=sbank, l=lhsT: e.matmul(
                                    o, l, s_qp[:], start=True, stop=True), inc="tE")
                                mm_c[ks] = b.count("tE")
                                if ks % 4 == 3:
                                    ka_free[kslot] = mm_c[ks]
                            kt = it
                            if 0 <= kt < NKT:
                                if kt == 0 and Jc == 0:
                                    b.wait("scalar", "dS")
                                b.wait("scalar", "tE", mm_c[kt])
                                b.wait("scalar", "sE", last_e5[kt % 2])
                                k2col = s_k2[:, kt * 16 + hcol: kt * 16 + hcol + 1]
                                b.op("scalar", lambda e, p=p_s[kt % 2], kc2=k2col,
                                     o=s_ub[kt % 2]: e.activation(
                                         out=o, in_=p, func=AF.Ln, bias=kc2),
                                     inc="sE", serial=False)
                                ln_c[kt] = b.count("sE")
                                sb_free[kt % 2] = ln_c[kt]
                            j = it - 1
                            if 0 <= j < NKT:
                                b.wait("scalar", "sE", max(ln_c[j], last_em[j % 2]))
                                b.op("scalar", lambda e, i_=s_ub[j % 2], o=s_vb[j % 2]:
                                     e.activation(out=o, in_=i_, func=AF.Exp, scale=0.5),
                                     inc="sE", serial=False)
                                e5_c[j] = b.count("sE")
                                last_e5[j % 2] = e5_c[j]
                            i2 = it - 2
                            if 0 <= i2 < NKT:
                                b.wait("scalar", "sE", e5_c[i2])
                                if last_av[i2 % 2] > 0:
                                    b.wait("scalar", "tE", last_av[i2 % 2])
                                b.op("scalar", lambda e, i_=s_vb[i2 % 2],
                                     o=s_pt[i2 % 2]: e.activation(
                                         out=o[:], in_=i_, func=AF.Exp, scale=-1.0),
                                     inc="sE", serial=False)
                                em_c[i2] = b.count("sE")
                                last_em[i2 % 2] = em_c[i2]
                                if DEBUG and Jc == 0 and h == 0 and i2 == 0:
                                    b.wait("scalar", "sE", em_c[0])
                                    b.dma("scalar", lambda e: e.dma_start(
                                        out=dbg_pt, in_=s_pt[0][:]), inc="dC")
                                b.wait("tensor", "sE", em_c[i2])
                                b.wait("tensor", "dS", vld_c[i2 // 4])
                                lhsT = s_vt[(i2 // 4) % 2][:, (i2 % 4) * 65:
                                                           (i2 % 4 + 1) * 65]
                                b.op("tensor", lambda e, l=lhsT, r=s_pt[i2 % 2],
                                     st=(i2 == 0), sp=(i2 == NKT - 1): e.matmul(
                                         p_o, l, r[:], start=st, stop=sp), inc="tE")
                                av_c[i2] = b.count("tE")
                                last_av[i2 % 2] = av_c[i2]
                                if i2 % 4 == 3:
                                    vt_free[(i2 // 4) % 2] = av_c[i2]
                        qp_free = mm_c[NKT - 1]
                        if DEBUG and Jc == 0 and h == 0:
                            dn = b.count("tE")
                            b.wait("vector", "tE", dn)
                            b.op("vector", lambda e: e.tensor_copy(
                                out=s_u[0:1, 0:512], in_=p_of[64:65, 0:512]), inc="vE")
                            b.wait("scalar", "vE")
                            b.dma("scalar", lambda e: e.dma_start(
                                out=dbg_den, in_=s_u[0:1, 0:512]), inc="dC")
                        b.wait("vector", "tE")
                        b.op("vector", lambda e: e.reciprocal(
                            out=s_recip[:], in_=p_of[64:65, 0:512]), inc="vE")
                        rec_n = b.count("vE")
                        b.wait("tensor", "vE", rec_n)
                        b.op("tensor", lambda e: e.matmul(
                            p_bc, s_ones1[:], s_recip[:], start=True, stop=True),
                            inc="tE")
                        bc_n = b.count("tE")
                        b.wait("vector", "tE", bc_n)
                        b.op("vector", lambda e: e.tensor_copy(
                            out=s_bcast[:], in_=p_bc), inc="vE")
                        b.wait("vector", "tE", att_free[pc])
                        b.op("vector", lambda e, pc=pc, hl=hl: e.tensor_tensor(
                            out=s_attn[pc][hl * 64:(hl + 1) * 64, :],
                            in0=p_of[0:64, 0:512], in1=s_bcast[:], op=ALU.mult), inc="vE")
                attn_ready = b.count("vE")
                b.wait("tensor", "vE", attn_ready)
                for sub in range(4):
                    b.wait("tensor", "vE", ybank_free)
                    for half in range(2):
                        for pc in range(8):
                            lhsT = s_attn[pc][:, sub * 128:(sub + 1) * 128]
                            rhs = wo_v[:, pc * 1024 + half * 512: pc * 1024 + (half + 1) * 512]
                            b.op("tensor", lambda e, l=lhsT, r=rhs, hf=half,
                                 st=(pc == 0): e.matmul(
                                     p_y[:, hf * 512:(hf + 1) * 512], l, r,
                                     start=st, stop=False), inc=None)
                        b.op("tensor", lambda e, hf=half: e.matmul(
                            p_y[:, hf * 512:(hf + 1) * 512], s_ones1b[:],
                            s_bo[:, hf * 512:(hf + 1) * 512], start=False, stop=True),
                            inc="tE")
                    for pc in range(8):
                        att_free[pc] = b.count("tE")
                    b.wait("vector", "tE")
                    ti = Jc * 4 + sub
                    b.op("vector", lambda e, ti=ti: e.tensor_tensor(
                        out=s_x[ti][:], in0=p_y, in1=s_x[ti][:], op=ALU.add),
                        inc="vE")
                    ybank_free = b.count("vE")

            if L2_PHASE < 3:
                lnv = b.count("vE")
                b.wait("scalar", "vE", lnv)
                b.wait("scalar", "tE")
                b.wait("scalar", "sE")
                for i in range(NT):
                    b.dma("scalar", lambda e, i=i: e.dma_start(
                        out=y_out[i * 128:(i + 1) * 128, :], in_=s_x[i][:]), inc="dC")
                b.wait("scalar", "dC", b.count("dC"))
                b.finalize(block)
                return nc
            for i in range(NT):
                b.wait("vector", "tE")
                emit_ln(b, [s_x[i][:]], None, T)
                lnn = b.count("vE")
                b.wait("vector", "vE", lnn)
                for xh in range(2):
                    b.wait("vector", "tE")
                    emit_quant(b, s_x[i][:, xh * 512:(xh + 1) * 512], s_qq[:], T, 128, 512)
                    for dc4 in range(4):
                        dc = xh * 4 + dc4
                        srct = s_qq[:, dc4 * 128:(dc4 + 1) * 128]
                        cons = b.count("vE")
                        b.wait("tensor", "vE", cons)
                        b.op("tensor", lambda e, s=srct: e.transpose(
                            T["p_tr"], s, T["ident"][:]), inc="tE")
                        b.wait("vector", "tE")
                        dst = s_xfm[:, dc * TPC + i * 128: dc * TPC + (i + 1) * 128]
                        b.op("vector", lambda e, d=dst: e.tensor_copy(
                            out=d, in_=T["p_tr"]), inc="vE")
            fm_ready = b.count("vE")
            b.wait("tensor", "vE", fm_ready)
            if DEBUG:
                b.wait("scalar", "vE", fm_ready)
                b.dma("scalar", lambda e: e.dma_start(
                    out=dbg_xq, in_=s_xfm[:].rearrange("p (c t) -> p c t", c=8)[:, :, 0:128]
                    .rearrange("p c t -> p c t")), inc="dC")

            if L2_PHASE < 4:
                lnv = b.count("vE")
                b.wait("scalar", "vE", lnv)
                b.wait("scalar", "tE")
                b.wait("scalar", "sE")
                for i in range(NT):
                    b.dma("scalar", lambda e, i=i: e.dma_start(
                        out=y_out[i * 128:(i + 1) * 128, :], in_=s_x[i][:]), inc="dC")
                b.wait("scalar", "dC", b.count("dC"))
                b.finalize(block)
                return nc
            wg_r = wgq.rearrange("(c p) f -> p c f", p=128)
            wu_r = wuq.rearrange("(c p) f -> p c f", p=128)
            wd_r = wdq.rearrange("(c p) f -> p c f", p=128)
            hq_free = 0
            gbank_free = sb_free[0]
            ubank_free = 0
            wd_free = [0, 0]
            for tg in range(8):
                b.wait("tensor", "tE", hq_free)
                for jci, jw in enumerate(JCH):
                    jo = JOF[jci]
                    wslot = jci % 2
                    b.wait("gpsimd", "tE")
                    b.dma("gpsimd", lambda e, jo=jo, jw=jw, d=wgc_v[wslot]: e.dma_start(
                        out=d.rearrange("p (c f) -> p c f", f=512)[:, :, 0:jw],
                        in_=wg_r[:, :, jo:jo + jw]), inc="dG")
                    b.dma("gpsimd", lambda e, jo=jo, jw=jw, d=wuc_v[wslot]: e.dma_start(
                        out=d.rearrange("p (c f) -> p c f", f=512)[:, :, 0:jw],
                        in_=wu_r[:, :, jo:jo + jw]), inc="dG")
                    wld = b.count("dG")
                    b.wait("tensor", "dG", wld)
                    for ts2 in range(2):
                        ti = tg * 2 + ts2
                        b.wait("tensor", "sE", gbank_free)
                        for dc in range(8):
                            lhsT = s_xfm[:, dc * TPC + ti * 128: dc * TPC + (ti + 1) * 128]
                            rhs = wgc_v[wslot][:, dc * 512: dc * 512 + jw]
                            b.op("tensor", lambda e, l=lhsT, r=rhs, jw=jw, st=(dc == 0),
                                 sp=(dc == 7): e.matmul(p_s[0][:, 0:jw], l, r,
                                                        start=st, stop=sp),
                                 inc="tE" if dc == 7 else None)
                        mg = b.count("tE")
                        b.wait("scalar", "tE", mg)
                        b.op("scalar", lambda e, jw=jw: e.activation(
                            out=s_g[:, 0:jw], in_=p_s[0][:, 0:jw], func=AF.Silu), inc="sE")
                        gbank_free = b.count("sE")
                        if DEBUG and tg == 0 and ts2 == 0 and jci == 0:
                            b.wait("scalar", "sE", gbank_free)
                            b.dma("scalar", lambda e: e.dma_start(
                                out=dbg_g, in_=s_g[:, 0:512]), inc="dC")
                        b.wait("tensor", "vE", ubank_free)
                        for dc in range(8):
                            lhsT = s_xfm[:, dc * TPC + ti * 128: dc * TPC + (ti + 1) * 128]
                            rhs = wuc_v[wslot][:, dc * 512: dc * 512 + jw]
                            b.op("tensor", lambda e, l=lhsT, r=rhs, jw=jw, st=(dc == 0),
                                 sp=(dc == 7): e.matmul(p_s[1][:, 0:jw], l, r,
                                                        start=st, stop=sp),
                                 inc="tE" if dc == 7 else None)
                        mu = b.count("tE")
                        b.wait("vector", "tE", mu)
                        b.op("vector", lambda e, jw=jw: e.tensor_copy(
                            out=s_up[:, 0:jw], in_=p_s[1][:, 0:jw]), inc="vE")
                        ubank_free = b.count("vE")
                        b.wait("vector", "sE")
                        b.op("vector", lambda e, jw=jw: e.tensor_tensor(
                            out=s_h[:, 0:jw], in0=s_g[:, 0:jw], in1=s_up[:, 0:jw],
                            op=ALU.mult), inc="vE")
                        emit_quant(b, s_h[:, 0:jw], s_qq[:, 0:jw], T, 128, jw)
                        for hc in range(jw // 128):
                            cons = b.count("vE")
                            b.wait("tensor", "vE", cons)
                            srct = s_qq[:, hc * 128:(hc + 1) * 128]
                            b.op("tensor", lambda e, s=srct: e.transpose(
                                T["p_tr"], s, T["ident"][:]), inc="tE")
                            b.wait("vector", "tE")
                            col = ts2 * 2816 + jo + hc * 128
                            b.op("vector", lambda e, col=col: e.tensor_copy(
                                out=hq_fm[:, col:col + 128], in_=T["p_tr"]), inc="vE")
                hq_ready = b.count("vE")
                b.wait("tensor", "vE", hq_ready)
                if DEBUG and tg == 0:
                    b.wait("scalar", "vE", hq_ready)
                    b.dma("scalar", lambda e: e.dma_start(
                        out=dbg_hq, in_=hq_fm[:, 0:2816]), inc="dC")
                nhc = I_FF // 128
                HCG = [2] * 11
                for ts2 in range(2):
                    ti = tg * 2 + ts2
                    b.wait("tensor", "vE", ybank_free)
                    hc0 = 0
                    for gi, gn in enumerate(HCG):
                        dslot = gi % 2
                        b.wait("gpsimd", "tE", wd_free[dslot])
                        b.dma("gpsimd", lambda e, d=wdc_v[dslot], h0=hc0, gn=gn: e.dma_start(
                            out=d.rearrange("p (c f) -> p c f", f=1024)[:, 0:gn, :],
                            in_=wd_r[:, h0:h0 + gn, :]), inc="dG")
                        dld = b.count("dG")
                        b.wait("tensor", "dG", dld)
                        for gj in range(gn):
                            hc = hc0 + gj
                            lhsT = hq_fm[:, ts2 * 2816 + hc * 128: ts2 * 2816 + (hc + 1) * 128]
                            wcf = wdc_v[dslot][:, gj * 1024:(gj + 1) * 1024]
                            for half in range(2):
                                b.op("tensor", lambda e, l=lhsT, w=wcf, hf=half, hc=hc,
                                     st=(hc == 0), sp=(hc == nhc - 1): e.matmul(
                                         p_y[:, hf * 512:(hf + 1) * 512], l,
                                         w[:, hf * 512:(hf + 1) * 512], start=st, stop=sp),
                                     inc="tE" if (gj == gn - 1 and half == 1) else None)
                        wd_free[dslot] = b.count("tE")
                        hc0 += gn
                    b.wait("vector", "tE")
                    b.op("vector", lambda e, ti=ti: e.tensor_tensor(
                        out=s_x[ti][:], in0=p_y, in1=s_x[ti][:], op=ALU.add), inc="vE")
                    ybank_free = b.count("vE")
                hq_free = b.count("tE")

            emit_ln(b, [s_x[i][:] for i in range(NT)], None, T)
            lnv = b.count("vE")
            b.wait("scalar", "vE", lnv)
            for i in range(NT):
                b.dma("scalar", lambda e, i=i: e.dma_start(
                    out=y_out[i * 128:(i + 1) * 128, :], in_=s_x[i][:]), inc="dC")
            b.wait("scalar", "dC", b.count("dC"))
            b.finalize(block)
    return nc



def own_index(r):
    idx = []
    for j in range(NT):
        t = 4 * j + r
        idx.append(np.arange(t * 128, (t + 1) * 128))
    return np.concatenate(idx)


def cont(a):
    return np.ascontiguousarray(a)


def make_masks(r):
    m = np.zeros((4, 128, 128), dtype=bfd)
    tri = (np.arange(128)[:, None] <= np.arange(128)[None, :]).astype(bfd)
    for s in range(4):
        if s < r:
            m[s] = 1.0
        elif s == r:
            m[s] = tri
        else:
            m[s] = 0.0
    return m


_cache = {}


def prep_launch1(inp):
    x = np.asarray(inp["L0_attn_x"], np.float32)
    WQ = np.asarray(inp["L0_attn_WQ"], np.float32)
    WK = np.asarray(inp["L0_attn_WK"], np.float32)
    WV = np.asarray(inp["L0_attn_WV"], np.float32)
    WO = np.asarray(inp["L0_attn_WO"], np.float32)
    shared = {
        "wql": cont(WQ.transpose(2, 0, 1).reshape(D, D)),
        "wkl": cont(WK.transpose(2, 0, 1).reshape(D, D)),
        "wvr": cont(WV.transpose(2, 0, 1).reshape(D, D)),
        "wor": cont(WO.transpose(0, 2, 1).reshape(D, D)),
        "wg1": np.asarray(inp["L0_ffn_wg1"], np.float32),
        "bg1c": cont(np.asarray(inp["L0_ffn_bg1"], np.float32).reshape(2, 128).T),
        "wg2": np.asarray(inp["L0_ffn_wg2"], np.float32),
        "w_in": np.asarray(inp["L0_ffn_w_in"], np.float32),
        "w_out": np.asarray(inp["L0_ffn_w_out"], np.float32),
        "ident": np.eye(128, dtype=bfd),
        "ones1f": np.ones((1, 64), np.float32),
    }
    wgate = np.asarray(inp["L1_ffn_w_gate"], np.float32)
    wup = np.asarray(inp["L1_ffn_w_up"], np.float32)
    wdown = np.asarray(inp["L1_ffn_w_down"], np.float32)
    x_fm_b = [cont(x[g].T) for g in range(B)]
    maps = []
    for c in range(NCORE):
        g, r = c // RPG, c % RPG
        oi = own_index(r)
        m = dict(shared)
        m["x_fm"] = x_fm_b[g]
        m["x_fm_own"] = cont(x[g].T[:, oi])
        m["x_tm"] = cont(x[g][oi])
        m["masks"] = make_masks(r)
        m["wg_sl"] = cont(wgate[c * WSL:(c + 1) * WSL])
        m["wu_sl"] = cont(wup[c * WSL:(c + 1) * WSL])
        m["wd_sl"] = cont(wdown[:, c * WSL:(c + 1) * WSL])
        maps.append(m)
    return maps


def run_launch1(inp):
    if "l1" not in _cache:
        _cache["l1"] = build_launch1()
    nc = _cache["l1"]
    maps = prep_launch1(inp)
    res = run_bass_kernel_spmd(nc, maps, core_ids=list(range(NCORE)))
    return res.results


def assemble_x1(results):
    x1 = np.zeros((B, S, D), np.float32)
    for c in range(NCORE):
        g, r = c // RPG, c % RPG
        x1[g][own_index(r)] = results[c]["x1"]
    return x1


def assemble_qw(results):
    wgT = np.concatenate([results[c]["wgT_q"] for c in range(NCORE)], axis=1)
    wuT = np.concatenate([results[c]["wuT_q"] for c in range(NCORE)], axis=1)
    wdT = np.concatenate([results[c]["wdT_q"] for c in range(NCORE)], axis=0)
    return wgT, wuT, wdT


def prep_launch2(inp, x1, wgT, wuT, wdT):
    wq = np.asarray(inp["L1_attn_wq"], np.float32)
    wk = np.asarray(inp["L1_attn_wk"], np.float32)
    wv = np.asarray(inp["L1_attn_wv"], np.float32)
    wo = np.asarray(inp["L1_attn_wo"], np.float32)
    band = np.zeros((128, 31), dtype=bfd)
    band[:, 15] = 1.0
    shared = {
        "wqT": cont(wq.T), "wkT": cont(wk.T), "wvT": cont(wv.T), "woT": cont(wo.T),
        "bq_c": cont(np.asarray(inp["L1_attn_bq"], np.float32).reshape(8, 128).T),
        "bk_c": cont(np.asarray(inp["L1_attn_bk"], np.float32).reshape(8, 128).T),
        "bv_r": np.asarray(inp["L1_attn_bv"], np.float32).reshape(1, D),
        "bo_r": np.asarray(inp["L1_attn_bo"], np.float32).reshape(1, D),
        "wgq": wgT, "wuq": wuT, "wdq": wdT,
        "band": band,
        "ident": np.eye(128, dtype=bfd),
        "identf": np.eye(128, dtype=np.float32),
        "ones1f": np.ones((1, 64), np.float32),
        "ones1b": np.ones((1, 128), dtype=bfd),
    }
    x1_fm_b = [cont(x1[g].T) for g in range(B)]
    maps = []
    for c in range(NCORE):
        g, r = c // RPG, c % RPG
        oi = own_index(r)
        m = dict(shared)
        m["x_fm"] = x1_fm_b[g]
        m["x_fm_own"] = cont(x1[g].T[:, oi])
        m["x_tm"] = cont(x1[g][oi])
        maps.append(m)
    return maps


def kernel(**inp):
    if "l1" not in _cache:
        _cache["l1"] = build_launch1()
    maps1 = prep_launch1(inp)
    res1 = run_bass_kernel_spmd(_cache["l1"], maps1, core_ids=list(range(NCORE)))
    x1 = assemble_x1(res1.results)
    wgT, wuT, wdT = assemble_qw(res1.results)
    if "l2" not in _cache:
        _cache["l2"] = build_launch2()
    maps2 = prep_launch2(inp, x1, wgT, wuT, wdT)
    res2 = run_bass_kernel_spmd(_cache["l2"], maps2, core_ids=list(range(NCORE)))
    out = np.zeros((B, S, D), np.float32)
    for c in range(NCORE):
        g, r = c // RPG, c % RPG
        out[g][own_index(r)] = res2.results[c]["y"]
    return out


# version 08 (final)
# speedup vs baseline: 1.4529x; 1.0496x over previous
import contextlib
import numpy as np
import ml_dtypes
import concourse.bass as bass
import concourse.mybir as mybir
from concourse.bass_utils import run_bass_kernel_spmd

F32 = mybir.dt.float32
BF16 = mybir.dt.bfloat16
AF = mybir.ActivationFunctionType
ALU = mybir.AluOpType
AX = mybir.AxisListType.X
bfd = ml_dtypes.bfloat16

B, S, D, H, HD = 2, 8192, 1024, 16, 64
RPG = 4
TPC = S // RPG
NT = TPC // 128
NKT = S // 128
NCH = 4
QW = 512
E_FF, H_FF, G_FF = 8, 2048, 256
I_FF = 2816
WSL = I_FF // 8
NCORE = 8
DEBUG = False
L2_PHASE = 4
L2_P1N = 3
L2_K2TR = True
L2_K2SQ = True
L2_K2 = True
L2_VBIAS = True
L2_Q2 = True

ENGS = ("sync", "scalar", "vector", "tensor", "gpsimd")


class Builder:
    def __init__(self, nc, es):
        self.nc, self.es = nc, es
        self.steps = []
        self.sems = {}

    def sem(self, name):
        h = self.es.enter_context(self.nc.semaphore(name))
        self.sems[name] = [h, 0]
        return name

    def wait(self, eng, name, val=None):
        h, n = self.sems[name]
        tgt = n if val is None else val
        if tgt > 0:
            self.steps.append((eng, lambda e, h=h, t=tgt: e.wait_ge(h, t)))

    SELF_SEM = {"vector": "vE", "scalar": "sE", "gpsimd": "pE"}

    def op(self, eng, fn, inc=None, inc_by=1, serial=True):
        ss = self.SELF_SEM.get(eng)
        if ss is not None:
            rec = self.sems[ss]
            if serial and rec[1] > 0:
                self.steps.append(
                    (eng, lambda e, h=rec[0], t=rec[1]: e.wait_ge(h, t)))
            if inc is None:
                inc = ss
            assert inc == ss, f"compute op on {eng} must inc {ss}"
        if inc is not None:
            rec = self.sems[inc]
            rec[1] += inc_by
            self.steps.append(
                (eng, lambda e, fn=fn, h=rec[0], k=inc_by: fn(e).then_inc(h, k)))
        else:
            self.steps.append((eng, lambda e, fn=fn: fn(e)))

    def dma(self, eng, fn, inc):
        if inc is not None:
            rec = self.sems[inc]
            rec[1] += 16
            self.steps.append(
                (eng, lambda e, fn=fn, h=rec[0]: fn(e).then_inc(h, 16)))
        else:
            self.steps.append((eng, lambda e, fn=fn: fn(e)))

    def count(self, name):
        return self.sems[name][1]

    def finalize(self, block):
        for engname in ENGS:
            def make(engname=engname):
                def _(eng):
                    for e, fn in self.steps:
                        if e == engname:
                            fn(eng)
                return _
            getattr(block, engname)(make())


def sbt(nc, es, name, shape, dtype):
    return es.enter_context(nc.sbuf_tensor(name, shape, dtype))


def pst(nc, es, name, shape, dtype=F32):
    return es.enter_context(nc.psum_tensor(name, shape, dtype))


def ceil_div(a, b):
    return -(-a // b)



def emit_ln(b, x_tiles, xbf_tiles, T):
    scr = T["scr"]
    st = T["st"]
    for i, xt in enumerate(x_tiles):
        ssum, ssq = st[0], st[1]
        mean, var, m2, rstd = st[2], st[3], st[4], st[5]
        b.op("vector", lambda e, xt=xt, o=ssum: e.tensor_reduce(
            out=o, in_=xt, axis=AX, op=ALU.add))
        b.op("vector", lambda e, a=ssq: e.memset(a, 0.0), inc="vE")
        b.wait("scalar", "vE")
        b.op("scalar", lambda e, xt=xt, o=scr, a=ssq: e.activation(
            out=o, in_=xt, func=AF.Square, accum_out=a), inc="sE")
        b.wait("vector", "sE")
        b.op("vector", lambda e, i_=ssum, o=mean: e.tensor_scalar(
            out=o, in0=i_, scalar1=1.0 / D, scalar2=None, op0=ALU.mult))
        b.op("vector", lambda e, a=mean, o=m2: e.tensor_tensor(
            out=o, in0=a, in1=a, op=ALU.mult))
        b.op("vector", lambda e, i_=ssq, o=var: e.tensor_scalar(
            out=o, in0=i_, scalar1=1.0 / D, scalar2=None, op0=ALU.mult))
        b.op("vector", lambda e, o=var, a=m2: e.tensor_tensor(
            out=o, in0=o, in1=a, op=ALU.subtract), inc="vE")
        b.wait("scalar", "vE")
        b.op("scalar", lambda e, i_=var, o=rstd: e.activation(
            out=o, in_=i_, func=AF.Ln, bias=T["eps"]))
        b.op("scalar", lambda e, o=rstd: e.activation(
            out=o, in_=o, func=AF.Exp, scale=-0.5), inc="sE")
        b.wait("vector", "sE")
        b.op("vector", lambda e, xt=xt, m=mean, r=rstd: e.tensor_scalar(
            out=xt, in0=xt, scalar1=m, scalar2=r, op0=ALU.subtract, op1=ALU.mult),
            inc="vE")
        if xbf_tiles is not None:
            b.op("vector", lambda e, xt=xt, o=xbf_tiles[i]: e.tensor_copy(
                out=o, in_=xt), inc="vE")


def emit_tm_to_fm(b, xbf_tiles, xfm, T, ntiles):
    ident, p_tr = T["ident"], T["p_tr"]
    ntok = ntiles * 128
    for jt in range(ntiles):
        for dc in range(8):
            src = xbf_tiles[jt][:, dc * 128:(dc + 1) * 128]
            cons = b.count("vE")
            b.wait("tensor", "vE", cons)
            b.op("tensor", lambda e, s=src: e.transpose(p_tr[:], s, ident[:]),
                 inc="tE")
            b.wait("vector", "tE")
            dst = xfm[:, dc * ntok + jt * 128: dc * ntok + (jt + 1) * 128]
            b.op("vector", lambda e, d=dst: e.tensor_copy(out=d, in_=p_tr[:]),
                 inc="vE")


def emit_quant(b, xin, qout, T, P, F):
    nb = F // 16
    qs = T["qs"][0:P, 0:nb]
    qs2 = T["qs2"][0:P, 0:nb]
    qm = T["qm"][0:P, 0:F]
    qc = T["qc"][0:P, 0:F]
    qd = T["qd"][0:P, 0:F]
    qe = T["qe"][0:P, 0:F]

    def bview(ap):
        return ap.rearrange("p (b w) -> p b w", w=16)

    def sview(ap):
        return ap.rearrange("p (b o) -> p b o", o=1).broadcast_to((P, nb, 16))

    b.op("vector", lambda e: e.tensor_reduce(
        out=qs, in_=bview(xin), axis=AX, op=ALU.max, apply_absolute_value=True))
    b.op("vector", lambda e: e.tensor_scalar(
        out=qs, in0=qs, scalar1=1.0 / 6.0, scalar2=1e-12, op0=ALU.mult, op1=ALU.max))
    b.op("vector", lambda e: e.reciprocal(out=qs2, in_=qs))
    b.op("vector", lambda e: e.tensor_tensor(
        out=bview(qm), in0=bview(xin), in1=sview(qs2), op=ALU.mult))
    b.op("vector", lambda e: e.tensor_tensor(out=qc, in0=qm, in1=qm, op=ALU.mult))
    b.op("vector", lambda e: e.tensor_scalar(
        out=qd, in0=qc, scalar1=0.0625, scalar2=None, op0=ALU.is_gt))
    for t2 in (0.5625, 1.5625, 3.0625):
        b.op("vector", lambda e, t2=t2: e.scalar_tensor_tensor(
            out=qd, in0=qc, scalar=t2, in1=qd, op0=ALU.is_gt, op1=ALU.add))
    b.op("vector", lambda e: e.tensor_scalar(
        out=qe, in0=qc, scalar1=6.25, scalar2=None, op0=ALU.is_gt))
    for t2 in (12.25, 25.0, 25.0):
        b.op("vector", lambda e, t2=t2: e.scalar_tensor_tensor(
            out=qe, in0=qc, scalar=t2, in1=qe, op0=ALU.is_gt, op1=ALU.add))
    b.op("vector", lambda e: e.scalar_tensor_tensor(
        out=qc, in0=qd, scalar=0.5, in1=qe, op0=ALU.mult, op1=ALU.add))
    b.op("vector", lambda e: e.tensor_scalar(
        out=qe, in0=qm, scalar1=0.0, scalar2=2.0, op0=ALU.is_ge, op1=ALU.mult))
    b.op("vector", lambda e: e.tensor_scalar(
        out=qe, in0=qe, scalar1=1.0, scalar2=None, op0=ALU.subtract))
    b.op("vector", lambda e: e.tensor_tensor(
        out=bview(qd), in0=bview(qe), in1=sview(qs), op=ALU.mult))
    b.op("vector", lambda e: e.tensor_tensor(
        out=qout, in0=qc, in1=qd, op=ALU.mult), inc="vE")



def build_launch1():
    nc = bass.Bass("TRN2", target_bir_lowering=False)
    es = contextlib.ExitStack()

    def din(name, shape, dtype=F32):
        return nc.dram_tensor(name, shape, dtype, kind="ExternalInput").ap()

    def dout(name, shape, dtype=F32):
        return nc.dram_tensor(name, shape, dtype, kind="ExternalOutput").ap()

    x_fm = din("x_fm", [D, S])
    x_fm_own = din("x_fm_own", [D, TPC])
    x_tm = din("x_tm", [TPC, D])
    wql = din("wql", [D, D])
    wkl = din("wkl", [D, D])
    wvr = din("wvr", [D, D])
    wor = din("wor", [D, D])
    wg1 = din("wg1", [D, G_FF])
    bg1c = din("bg1c", [128, 2])
    wg2 = din("wg2", [G_FF, E_FF])
    w_in = din("w_in", [E_FF, D, H_FF])
    w_out = din("w_out", [E_FF, H_FF, D])
    masks = din("masks", [4, 128, 128], BF16)
    ident_d = din("ident", [128, 128], BF16)
    ones1f = din("ones1f", [1, 64])
    wg_sl = din("wg_sl", [WSL, D])
    wu_sl = din("wu_sl", [WSL, D])
    wd_sl = din("wd_sl", [D, WSL])

    x1_out = dout("x1", [TPC, D])
    if DEBUG:
        dbg_pre = dout("dbg_pre", [TPC, D])
        dbg_k = dout("dbg_k", [128, 512], BF16)
        dbg_q = dout("dbg_q", [128, 512], BF16)
        dbg_v = dout("dbg_v", [128, 16 * 65], BF16)
        dbg_at = dout("dbg_at", [128, QW], BF16)
        dbg_gate = dout("dbg_gate", [128, E_FF])
        dbg_xfm = dout("dbg_xfm", [128, 512], BF16)
        dbg_gfm = dout("dbg_gfm", [128, 512], BF16)
        dbg_log = dout("dbg_log", [128, E_FF])
        dbg_ln0 = dout("dbg_ln0", [128, D])
        dbg_st = dout("dbg_st", [128, 8])
        dbg_scr = dout("dbg_scr", [128, D])
        dbg_ln1 = dout("dbg_ln1", [128, D])
    wgT_q = dout("wgT_q", [D, WSL], BF16)
    wuT_q = dout("wuT_q", [D, WSL], BF16)
    wdT_q = dout("wdT_q", [WSL, D], BF16)

    K0 = nc.dram_tensor("K0", [D, S], BF16).ap()
    V0 = nc.dram_tensor("V0", [S, 16 * 65], BF16).ap()
    Q0 = nc.dram_tensor("Q0", [D, TPC], BF16).ap()

    with es:
        b = Builder(nc, es)
        for s in ("dS", "dC", "dG", "tE", "vE", "sE", "pE"):
            b.sem(s)

        T = {}
        AW = sbt(nc, es, "AW", [128, 24576], BF16)
        AB = sbt(nc, es, "AB", [128, 8192], BF16)
        wq_v = AW[:, 0:8192]
        wk_v = AW[:, 8192:16384]
        wv_v = AW[:, 16384:24576]
        wo_v = AW[:, 0:8192]
        win_v = AW[:, 0:16384]
        wouth_v = AW[:, 16384:24576]
        s_gfm = AW[:, 0:2 * TPC]
        s_xc = [AB[:, 0:4096], AB[:, 4096:8192]]
        h_v = AB[:, 0:8192]

        s_x = [sbt(nc, es, f"s_x{i}", [128, D], F32) for i in range(NT)]
        s_xfm = sbt(nc, es, "s_xfm", [128, 8 * TPC], BF16)
        s_tbf = sbt(nc, es, "s_tbf", [128, D], BF16)
        s_kv = [sbt(nc, es, f"s_kv{i}", [128, 512], BF16) for i in range(2)]
        s_vst = [sbt(nc, es, f"s_vst{i}", [128, 16 * 65], BF16) for i in range(2)]
        s_qt = [sbt(nc, es, f"s_qt{i}", [128, 512], BF16) for i in range(2)]
        s_kt = [sbt(nc, es, f"s_kt{i}", [128, 512], BF16) for i in range(2)]
        s_vt = [sbt(nc, es, f"s_vt{i}", [128, 4 * 130], BF16) for i in range(2)]
        s_pt = [sbt(nc, es, f"s_pt{i}", [128, 512], BF16) for i in range(4)]
        s_mask = sbt(nc, es, "s_mask", [128, 4 * 128], BF16)
        s_attn = [sbt(nc, es, f"s_attn{p}", [128, QW], BF16) for p in range(8)]
        s_recip = sbt(nc, es, "s_recip", [1, QW], F32)
        s_bcast = sbt(nc, es, "s_bcast", [64, QW], F32)
        s_ones1 = sbt(nc, es, "s_ones1", [1, 64], F32)
        T["ident"] = sbt(nc, es, "s_ident", [128, 128], BF16)
        T["st"] = [sbt(nc, es, f"s_st{i}", [128, 1], F32)[:] for i in range(8)]
        T["eps"] = T["st"][6]
        s_wg1 = sbt(nc, es, "s_wg1", [128, 8 * G_FF], BF16)
        s_bg1 = sbt(nc, es, "s_bg1", [128, 2], F32)
        s_wg2 = sbt(nc, es, "s_wg2", [128, 2 * E_FF], BF16)
        s_gate = [sbt(nc, es, f"s_gate{i}", [128, E_FF], F32) for i in range(NT)]
        s_gsc = sbt(nc, es, "s_gsc", [128, 32], F32)
        T["qs"] = sbt(nc, es, "s_qs", [128, 64], F32)
        T["qs2"] = sbt(nc, es, "s_qs2", [128, 64], F32)
        T["qm"] = s_xfm[:, 0:1024].bitcast(F32)
        T["qc"] = s_xfm[:, 1024:2048].bitcast(F32)
        T["qd"] = s_xfm[:, 2048:3072].bitcast(F32)
        T["qe"] = s_xfm[:, 3072:4096].bitcast(F32)
        s_qin = sbt(nc, es, "s_qin", [128, D], F32)
        T["scr"] = s_qin[:]
        s_qq = sbt(nc, es, "s_qq", [128, D], BF16)
        s_qst = sbt(nc, es, "s_qst", [128, 128], BF16)

        p_yt = pst(nc, es, "p_y", [128, 1024])
        p_y = p_yt[:]
        p_s = [pst(nc, es, "p_s0", [128, 512])[:], pst(nc, es, "p_s1", [128, 512])[:],
               p_yt[:, 0:512], p_yt[:, 512:1024]]
        p_o = [pst(nc, es, f"p_o{i}", [65, 512]) for i in range(2)]
        p_mix = pst(nc, es, "p_mix", [128, 512])
        p_bc = p_mix[0:64, 0:512]
        p_g = p_mix[0:128, 0:16]
        T["p_tr"] = pst(nc, es, "p_tr", [128, 128], BF16)[:]

        with nc.Block() as block:
            for i in range(NT):
                b.dma("scalar", lambda e, i=i: e.dma_start(
                    out=s_x[i][:], in_=x_tm[i * 128:(i + 1) * 128, :]), inc="dC")
            b.dma("scalar", lambda e: e.dma_start(
                out=s_mask[:].rearrange("p (s q) -> p s q", s=4),
                in_=masks.rearrange("s p q -> p s q")), inc="dC")
            b.dma("scalar", lambda e: e.dma_start(out=T["ident"][:], in_=ident_d), inc="dC")
            b.dma("scalar", lambda e: e.dma_start(out=s_ones1[:], in_=ones1f), inc="dC")
            b.dma("scalar", lambda e: e.dma_start(out=s_bg1[:], in_=bg1c), inc="dC")
            for w_d, w_s in [(wql, wq_v), (wkl, wk_v), (wvr, wv_v)]:
                b.dma("gpsimd", lambda e, wd=w_d, ws=w_s: e.dma_start(
                    out=ws.rearrange("p (c f) -> p c f", f=D),
                    in_=wd.rearrange("(c p) f -> p c f", p=128)), inc="dG")
            b.dma("gpsimd", lambda e: e.dma_start(
                out=s_wg1[:].rearrange("p (c f) -> p c f", f=G_FF),
                in_=wg1.rearrange("(c p) f -> p c f", p=128)), inc="dG")
            b.dma("gpsimd", lambda e: e.dma_start(
                out=s_wg2[:].rearrange("p (c f) -> p c f", f=E_FF),
                in_=wg2.rearrange("(c p) f -> p c f", p=128)), inc="dG")
            for i in range(2):
                b.op("gpsimd", lambda e, i=i: e.memset(
                    s_vst[i][:].rearrange("p (h c) -> p h c", c=65)[:, :, 64:65], 1.0),
                    inc="pE")
            b.op("gpsimd", lambda e: e.memset(T["eps"], 1e-5), inc="pE")
            b.wait("scalar", "pE")
            consts_dC = b.count("dC")
            consts_dG = b.count("dG")
            consts_pE = b.count("pE")

            b.wait("scalar", "dC", consts_dC)

            def quant_transpose_out(src_dram, out_dram, rows):
                cols = src_dram.shape[1]
                ntr = ceil_div(rows, 128)
                for rt in range(ntr):
                    p0, p1 = rt * 128, min(rows, (rt + 1) * 128)
                    P = p1 - p0
                    b.wait("scalar", "vE")
                    b.wait("scalar", "tE")
                    b.dma("scalar", lambda e, s=src_dram, p0=p0, p1=p1: e.dma_start(
                        out=s_qin[0:p1 - p0, 0:cols], in_=s[p0:p1, :]), inc="dC")
                    b.wait("vector", "dC")
                    for c0q in range(0, cols, 512):
                        cw = min(512, cols - c0q)
                        emit_quant(b, s_qin[0:P, c0q:c0q + cw],
                                   s_qq[0:P, c0q:c0q + cw], T, P, cw)
                    nct = ceil_div(cols, 128)
                    for ct in range(nct):
                        c0, c1 = ct * 128, min(cols, (ct + 1) * 128)
                        cons = b.count("dC")
                        b.wait("tensor", "vE")
                        b.wait("tensor", "dC", cons)
                        b.op("tensor", lambda e, P=P, c0=c0, c1=c1: e.transpose(
                            T["p_tr"][0:c1 - c0, 0:P], s_qq[0:P, c0:c1],
                            T["ident"][0:P, 0:P]), inc="tE")
                        b.wait("vector", "tE")
                        b.op("vector", lambda e, c0=c0, c1=c1, P=P: e.tensor_copy(
                            out=s_qst[0:c1 - c0, 0:P], in_=T["p_tr"][0:c1 - c0, 0:P]),
                            inc="vE")
                        b.wait("scalar", "vE")
                        b.dma("scalar", lambda e, o=out_dram, c0=c0, c1=c1, p0=p0, p1=p1:
                              e.dma_start(out=o[c0:c1, p0:p1],
                                          in_=s_qst[0:c1 - c0, 0:p1 - p0]), inc="dC")

            quant_transpose_out(wg_sl, wgT_q, WSL)
            quant_transpose_out(wu_sl, wuT_q, WSL)
            quant_transpose_out(wd_sl, wdT_q, D)

            b.wait("tensor", "dG", consts_dG)
            b.wait("tensor", "pE", consts_pE)
            xc_free = [0, 0]
            kbank_free = [0, 0]
            ybank_free = 0
            kvstage_free = [0, 0]
            vstage_free = [0, 0]

            def proj_fm(wview, kc, slot, dst_dram, dst_cols):
                for pc in range(8):
                    bank = p_s[pc % 2]
                    b.wait("tensor", "vE", kbank_free[pc % 2])
                    for dc in range(8):
                        lhsT = wview[:, dc * 1024 + pc * 128: dc * 1024 + (pc + 1) * 128]
                        rhs = s_xc[slot][:, dc * 512:(dc + 1) * 512]
                        b.op("tensor", lambda e, o=bank, l=lhsT, r=rhs,
                             st=(dc == 0), sp=(dc == 7): e.matmul(o, l, r, start=st, stop=sp),
                             inc="tE" if dc == 7 else None)
                    b.wait("vector", "tE")
                    b.wait("vector", "dC", kvstage_free[pc % 2])
                    stg = s_kv[pc % 2]
                    b.op("vector", lambda e, p=bank, s=stg: e.tensor_copy(
                        out=s[:], in_=p[:]), inc="vE")
                    kbank_free[pc % 2] = b.count("vE")
                    b.wait("scalar", "vE")
                    b.dma("scalar", lambda e, s=stg, pc=pc: e.dma_start(
                        out=dst_dram[pc * 128:(pc + 1) * 128,
                                     kc * dst_cols:(kc + 1) * dst_cols],
                        in_=s[:]), inc="dC")
                    kvstage_free[pc % 2] = b.count("dC")

            for kc in range(16):
                slot = kc % 2
                b.wait("gpsimd", "tE", xc_free[slot])
                for dc in range(8):
                    src = x_fm[dc * 128:(dc + 1) * 128, kc * 512:(kc + 1) * 512]
                    b.dma("gpsimd", lambda e, s=src, d=s_xc[slot], dc=dc: e.dma_start(
                        out=d[:, dc * 512:(dc + 1) * 512], in_=s), inc="dG")
                b.wait("tensor", "dG")
                proj_fm(wk_v, kc, slot, K0, 512)
                for sub in range(4):
                    b.wait("tensor", "vE", ybank_free)
                    for half in range(2):
                        for dc in range(8):
                            lhsT = s_xc[slot][:, dc * 512 + sub * 128: dc * 512 + (sub + 1) * 128]
                            rhs = wv_v[:, dc * 1024 + half * 512: dc * 1024 + (half + 1) * 512]
                            b.op("tensor", lambda e, l=lhsT, r=rhs, st=(dc == 0),
                                 sp=(dc == 7), hf=half: e.matmul(
                                     p_y[:, hf * 512:(hf + 1) * 512], l, r, start=st, stop=sp),
                                 inc="tE" if (dc == 7 and half == 1) else None)
                    b.wait("vector", "tE")
                    b.wait("vector", "dC", vstage_free[sub % 2])
                    vstg = s_vst[sub % 2]
                    dstv = vstg[:].rearrange("p (h c) -> p h c", c=65)[:, :, 0:64]
                    b.op("vector", lambda e, d=dstv: e.tensor_copy(
                        out=d, in_=p_y.rearrange("p (h c) -> p h c", c=64)), inc="vE")
                    ybank_free = b.count("vE")
                    b.wait("scalar", "vE")
                    row0 = kc * 512 + sub * 128
                    b.dma("scalar", lambda e, s=vstg, r0=row0: e.dma_start(
                        out=V0[r0:r0 + 128, :], in_=s[:]), inc="dC")
                    vstage_free[sub % 2] = b.count("dC")
                xc_free[slot] = b.count("tE")

            for Jc in range(NCH):
                slot = Jc % 2
                b.wait("tensor", "vE", ybank_free)
                b.wait("gpsimd", "tE", xc_free[slot])
                for dc in range(8):
                    src = x_fm_own[dc * 128:(dc + 1) * 128, Jc * 512:(Jc + 1) * 512]
                    b.dma("gpsimd", lambda e, s=src, d=s_xc[slot], dc=dc: e.dma_start(
                        out=d[:, dc * 512:(dc + 1) * 512], in_=s), inc="dG")
                b.wait("tensor", "dG")
                proj_fm(wq_v, Jc, slot, Q0, 512)
                xc_free[slot] = b.count("tE")
            p1_dC = b.count("dC")

            p1_tE = b.count("tE")
            b.wait("gpsimd", "tE", p1_tE)
            b.dma("gpsimd", lambda e: e.dma_start(
                out=wo_v.rearrange("p (c f) -> p c f", f=D),
                in_=wor.rearrange("(c p) f -> p c f", p=128)), inc="dG")
            wo_dG = b.count("dG")

            b.wait("sync", "dC", p1_dC)
            b.wait("tensor", "dG", wo_dG)
            kt_free = [0, 0]
            vt_free = [0, 0]
            qt_free = [0, 0]
            sb_free = [0, 0]
            pt_free = [0, 0]
            att_free = [0] * 8
            for Jc in range(NCH):
                kmax = 16 * Jc + 16
                for pc in range(8):
                    qslot = pc % 2
                    b.wait("tensor", "vE")
                    b.wait("sync", "tE", qt_free[qslot])
                    b.dma("sync", lambda e, pc=pc, Jc=Jc, d=s_qt[qslot]: e.dma_start(
                        out=d[:], in_=Q0[pc * 128:(pc + 1) * 128,
                                         Jc * 512:(Jc + 1) * 512]), inc="dS")
                    qld = b.count("dS")
                    b.wait("tensor", "dS", qld)
                    for kg in range(kmax // 4):
                        kslot = kg % 2
                        b.wait("sync", "tE", kt_free[kslot])
                        b.dma("sync", lambda e, pc=pc, kg=kg, d=s_kt[kslot]: e.dma_start(
                            out=d[:], in_=K0[pc * 128:(pc + 1) * 128,
                                             kg * 512:(kg + 1) * 512]), inc="dS")
                        kld = b.count("dS")
                        b.wait("sync", "tE", vt_free[kslot])
                        srcv = V0[kg * 512:(kg + 1) * 512, 2 * pc * 65:(2 * pc + 2) * 65]
                        b.dma("sync", lambda e, s=srcv, d=s_vt[kslot]: e.dma_start(
                            out=d[:].rearrange("p (g c) -> p g c", g=4),
                            in_=s.rearrange("(g p) c -> p g c", p=128)), inc="dS")
                        vld = b.count("dS")
                        for ki in range(4):
                            kt = kg * 4 + ki
                            j0 = max(0, ceil_div(kt - 3, 4))
                            c0 = max(0, (j0 - 4 * Jc)) * 128
                            sbank = [p_s[2 * (kt % 2)], p_s[2 * (kt % 2) + 1]]
                            b.wait("tensor", "dS", kld)
                            b.wait("tensor", "sE", sb_free[kt % 2])
                            for hh in range(2):
                                lhsT = s_kt[kslot][hh * 64:(hh + 1) * 64,
                                                   ki * 128:(ki + 1) * 128]
                                rhs = s_qt[qslot][hh * 64:(hh + 1) * 64, c0:512]
                                b.op("tensor", lambda e, o=sbank[hh], l=lhsT, r=rhs, c0=c0:
                                     e.matmul(o[:, c0:512], l, r, start=True, stop=True),
                                     inc="tE")
                            mm_s = b.count("tE")
                            b.wait("scalar", "tE", mm_s)
                            b.wait("scalar", "tE", pt_free[kt % 2])
                            for hh in range(2):
                                b.op("scalar", lambda e, p=sbank[hh],
                                     o=s_pt[2 * (kt % 2) + hh], c0=c0:
                                     e.activation(out=o[:, c0:512], in_=p[:, c0:512],
                                                  func=AF.Exp, scale=0.125), inc="sE")
                            sb_free[kt % 2] = b.count("sE")
                            exp_n = b.count("sE")
                            need_mask = j0 >= 4 * Jc
                            if need_mask:
                                sstar = kt - 4 * j0
                                b.wait("vector", "sE", exp_n)
                                for hh in range(2):
                                    b.op("vector", lambda e, o=s_pt[2 * (kt % 2) + hh],
                                         c0=c0, ss=sstar: e.tensor_tensor(
                                             out=o[:, c0:c0 + 128], in0=o[:, c0:c0 + 128],
                                             in1=s_mask[:, ss * 128:(ss + 1) * 128],
                                             op=ALU.mult), inc="vE")
                                b.wait("tensor", "vE")
                            else:
                                b.wait("tensor", "sE", exp_n)
                            b.wait("tensor", "dS", vld)
                            for hh in range(2):
                                lhsT = s_vt[kslot][:, ki * 130 + hh * 65: ki * 130 + (hh + 1) * 65]
                                rhs = s_pt[2 * (kt % 2) + hh][:, c0:512]
                                b.op("tensor", lambda e, o=p_o[hh], l=lhsT, r=rhs, c0=c0,
                                     st=(kt == 0), sp=(kt == kmax - 1): e.matmul(
                                         o[:, c0:512], l, r, start=st, stop=sp),
                                     inc="tE")
                            pt_free[kt % 2] = b.count("tE")
                        kt_free[kslot] = b.count("tE")
                        vt_free[kslot] = b.count("tE")
                    qt_free[qslot] = b.count("tE")
                    for hh in range(2):
                        b.wait("vector", "tE")
                        b.op("vector", lambda e, p=p_o[hh]: e.reciprocal(
                            out=s_recip[:], in_=p[64:65, :]), inc="vE")
                        rec_n = b.count("vE")
                        b.wait("tensor", "vE", rec_n)
                        b.op("tensor", lambda e: e.matmul(
                            p_bc, s_ones1[:], s_recip[:], start=True, stop=True),
                            inc="tE")
                        bc_n = b.count("tE")
                        b.wait("vector", "tE", bc_n)
                        b.op("vector", lambda e: e.tensor_copy(
                            out=s_bcast[:], in_=p_bc), inc="vE")
                        b.wait("vector", "tE", att_free[pc])
                        b.op("vector", lambda e, p=p_o[hh], pc=pc, hh=hh: e.tensor_tensor(
                            out=s_attn[pc][hh * 64:(hh + 1) * 64, :],
                            in0=p[0:64, :], in1=s_bcast[:], op=ALU.mult), inc="vE")
                attn_ready = b.count("vE")
                b.wait("tensor", "vE", attn_ready)
                for sub in range(4):
                    b.wait("tensor", "vE", ybank_free)
                    for pc in range(8):
                        for half in range(2):
                            lhsT = s_attn[pc][:, sub * 128:(sub + 1) * 128]
                            rhs = wo_v[:, pc * 1024 + half * 512: pc * 1024 + (half + 1) * 512]
                            b.op("tensor", lambda e, l=lhsT, r=rhs, hf=half,
                                 st=(pc == 0), sp=(pc == 7): e.matmul(
                                     p_y[:, hf * 512:(hf + 1) * 512], l, r,
                                     start=st, stop=sp),
                                 inc="tE" if (pc == 7 and half == 1) else None)
                    for pc in range(8):
                        att_free[pc] = b.count("tE")
                    b.wait("vector", "tE")
                    ti = Jc * 4 + sub
                    b.op("vector", lambda e, ti=ti: e.tensor_tensor(
                        out=s_x[ti][:], in0=p_y, in1=s_x[ti][:], op=ALU.add),
                        inc="vE")
                    ybank_free = b.count("vE")

            if DEBUG:
                pre_v = b.count("vE")
                b.wait("scalar", "vE", pre_v)
                for i in range(NT):
                    b.dma("scalar", lambda e, i=i: e.dma_start(
                        out=dbg_pre[i * 128:(i + 1) * 128, :], in_=s_x[i][:]), inc="dC")
                b.dma("scalar", lambda e: e.dma_start(out=dbg_k, in_=K0[0:128, 0:512]), inc="dC")
                b.dma("scalar", lambda e: e.dma_start(out=dbg_q, in_=Q0[0:128, 0:512]), inc="dC")
                b.dma("scalar", lambda e: e.dma_start(out=dbg_v, in_=V0[0:128, :]), inc="dC")
                b.dma("scalar", lambda e: e.dma_start(out=dbg_at, in_=s_attn[0][:]), inc="dC")
            for i in range(NT):
                b.wait("vector", "tE")
                emit_ln(b, [s_x[i][:]], [s_tbf[:]], T)
                if DEBUG and i == 0:
                    dn = b.count("vE")
                    b.wait("scalar", "vE", dn)
                    b.wait("scalar", "sE")
                    b.dma("scalar", lambda e: e.dma_start(out=dbg_scr, in_=T["scr"]), inc="dC")
                    for jj in range(8):
                        b.op("vector", lambda e, jj=jj: e.tensor_copy(
                            out=s_qin[:, 512 + jj:512 + jj + 1], in_=T["st"][jj]), inc="vE")
                    b.wait("scalar", "vE")
                    b.dma("scalar", lambda e: e.dma_start(
                        out=dbg_st, in_=s_qin[:, 512:520]), inc="dC")
                for dc in range(8):
                    srct = s_tbf[:, dc * 128:(dc + 1) * 128]
                    cons = b.count("vE")
                    b.wait("tensor", "vE", cons)
                    b.op("tensor", lambda e, s=srct: e.transpose(
                        T["p_tr"], s, T["ident"][:]), inc="tE")
                    b.wait("vector", "tE")
                    dst = s_xfm[:, dc * TPC + i * 128: dc * TPC + (i + 1) * 128]
                    b.op("vector", lambda e, d=dst: e.tensor_copy(
                        out=d, in_=T["p_tr"]), inc="vE")
            fm_ready_vE = b.count("vE")
            if DEBUG:
                b.wait("scalar", "vE", fm_ready_vE)
                b.dma("scalar", lambda e: e.dma_start(out=dbg_xfm, in_=s_xfm[:, 0:512]), inc="dC")
                b.dma("scalar", lambda e: e.dma_start(out=dbg_ln0, in_=s_x[0][:]), inc="dC")
                b.dma("scalar", lambda e: e.dma_start(out=dbg_ln1, in_=s_x[1][:]), inc="dC")

            b.wait("tensor", "vE", fm_ready_vE)
            glog_free = 0
            for Jc in range(NCH):
                for gc in range(2):
                    bank = p_s[gc]
                    b.wait("tensor", "sE", sb_free[gc])
                    for dc in range(8):
                        lhsT = s_wg1[:, dc * G_FF + gc * 128: dc * G_FF + (gc + 1) * 128]
                        rhs = s_xfm[:, dc * TPC + Jc * 512: dc * TPC + (Jc + 1) * 512]
                        b.op("tensor", lambda e, o=bank, l=lhsT, r=rhs, st=(dc == 0),
                             sp=(dc == 7): e.matmul(o, l, r, start=st, stop=sp),
                             inc="tE")
                    b.wait("scalar", "tE")
                    b.op("scalar", lambda e, p=bank, gc=gc, Jc=Jc: e.activation(
                        out=s_gfm[:, gc * TPC + Jc * 512: gc * TPC + (Jc + 1) * 512],
                        in_=p[:], func=AF.Gelu, bias=s_bg1[:, gc:gc + 1]), inc="sE")
                    sb_free[gc] = b.count("sE")
            gelu_all = b.count("sE")
            b.wait("tensor", "sE", gelu_all)
            for ti in range(NT):
                b.wait("tensor", "vE", glog_free)
                for gc in range(2):
                    lhsT = s_gfm[:, gc * TPC + ti * 128: gc * TPC + (ti + 1) * 128]
                    rhs = s_wg2[:, gc * E_FF:(gc + 1) * E_FF]
                    b.op("tensor", lambda e, l=lhsT, r=rhs, st=(gc == 0), sp=(gc == 1):
                         e.matmul(p_g[:, 0:E_FF], l, r, start=st, stop=sp),
                         inc="tE" if gc == 1 else None)
                b.wait("vector", "tE")
                g0 = s_gsc[:, 0:8]
                gex = s_gsc[:, 8:16]
                gpr = s_gsc[:, 16:24]
                st0 = T["st"]
                m1, negm = st0[0], st0[3]
                sm, rc = st0[1], st0[2]
                m1p, m2 = st0[4], st0[5]
                b.op("vector", lambda e: e.tensor_copy(out=g0, in_=p_g[:, 0:E_FF]),
                     inc="vE")
                glog_free = b.count("vE")
                if DEBUG and ti == 0:
                    b.wait("scalar", "vE", glog_free)
                    b.dma("scalar", lambda e: e.dma_start(out=dbg_log, in_=g0), inc="dC")
                b.op("vector", lambda e: e.tensor_reduce(
                    out=m1, in_=g0, axis=AX, op=ALU.max))
                b.op("vector", lambda e: e.tensor_scalar(
                    out=negm, in0=m1, scalar1=-1.0, scalar2=None, op0=ALU.mult),
                    inc="vE")
                b.wait("scalar", "vE")
                b.op("scalar", lambda e: e.activation(
                    out=gex, in_=g0, func=AF.Exp, bias=negm), inc="sE")
                b.wait("vector", "sE")
                b.op("vector", lambda e: e.tensor_reduce(
                    out=sm, in_=gex, axis=AX, op=ALU.add))
                b.op("vector", lambda e: e.reciprocal(out=rc, in_=sm))
                b.op("vector", lambda e: e.tensor_scalar(
                    out=gpr, in0=gex, scalar1=rc, scalar2=None, op0=ALU.mult))
                b.op("vector", lambda e: e.tensor_reduce(
                    out=m1p, in_=gpr, axis=AX, op=ALU.max))
                b.op("vector", lambda e: e.tensor_scalar(
                    out=gex, in0=gpr, scalar1=m1p, scalar2=None, op0=ALU.is_lt))
                b.op("vector", lambda e: e.tensor_tensor(
                    out=gex, in0=gex, in1=gpr, op=ALU.mult))
                b.op("vector", lambda e: e.tensor_reduce(
                    out=m2, in_=gex, axis=AX, op=ALU.max))
                b.op("vector", lambda e: e.tensor_scalar(
                    out=gex, in0=gpr, scalar1=m2, scalar2=None, op0=ALU.is_ge))
                b.op("vector", lambda e, ti=ti: e.tensor_tensor(
                    out=s_gate[ti][:], in0=gex, in1=gpr, op=ALU.mult), inc="vE")

            h_free = 0
            w_in_r = w_in.rearrange("e (c p) f -> p e c f", p=128)
            w_out_r = w_out.rearrange("e (c p) (h f) -> p e c h f", p=128, h=2)
            win_free = b.count("tE")
            wout_free = b.count("tE")
            for ex in range(E_FF):
                b.wait("gpsimd", "tE", win_free)
                b.dma("gpsimd", lambda e, ex=ex: e.dma_start(
                    out=win_v.rearrange("p (o c f) -> p o c f", o=1, f=H_FF),
                    in_=w_in_r[:, ex:ex + 1]), inc="dG")
                wld = b.count("dG")
                b.wait("tensor", "dG", wld)
                for tt in range(NCH):
                    b.wait("gpsimd", "tE", wout_free)
                    b.dma("gpsimd", lambda e, ex=ex: e.dma_start(
                        out=wouth_v.rearrange("p (a c o f) -> p a c o f",
                                              a=1, o=1, f=512),
                        in_=w_out_r[:, ex:ex + 1, :, 0:1, :]), inc="dG")
                    whld0 = b.count("dG")
                    b.wait("tensor", "tE", h_free)
                    for hf in range(16):
                        bank = p_s[hf % 2]
                        b.wait("tensor", "sE", sb_free[hf % 2])
                        for dc in range(8):
                            lhsT = win_v[:, dc * H_FF + hf * 128: dc * H_FF + (hf + 1) * 128]
                            rhs = s_xfm[:, dc * TPC + tt * 512: dc * TPC + (tt + 1) * 512]
                            b.op("tensor", lambda e, o=bank, l=lhsT, r=rhs, st=(dc == 0),
                                 sp=(dc == 7): e.matmul(o, l, r, start=st, stop=sp),
                                 inc="tE")
                        b.wait("scalar", "tE")
                        b.op("scalar", lambda e, p=bank, hf=hf: e.activation(
                            out=h_v[:, hf * 512:(hf + 1) * 512], in_=p[:], func=AF.Gelu),
                            inc="sE")
                        sb_free[hf % 2] = b.count("sE")
                    if tt == NCH - 1:
                        win_free = b.count("tE")
                    hdone = b.count("sE")
                    b.wait("tensor", "sE", hdone)
                    for half in range(2):
                        if half == 0:
                            whld = whld0
                        else:
                            b.wait("gpsimd", "tE", wout_free)
                            b.dma("gpsimd", lambda e, ex=ex, half=half: e.dma_start(
                                out=wouth_v.rearrange("p (a c o f) -> p a c o f",
                                                      a=1, o=1, f=512),
                                in_=w_out_r[:, ex:ex + 1, :, half:half + 1, :]), inc="dG")
                            whld = b.count("dG")
                        b.wait("tensor", "dG", whld)
                        for sub in range(4):
                            ti = tt * 4 + sub
                            b.wait("tensor", "vE", ybank_free)
                            for hf in range(16):
                                lhsT = h_v[:, hf * 512 + sub * 128: hf * 512 + (sub + 1) * 128]
                                rhs = wouth_v[:, hf * 512:(hf + 1) * 512]
                                b.op("tensor", lambda e, l=lhsT, r=rhs, half=half,
                                     st=(hf == 0), sp=(hf == 15): e.matmul(
                                         p_y[:, half * 512:(half + 1) * 512], l, r,
                                         start=st, stop=sp),
                                     inc="tE" if hf == 15 else None)
                            b.wait("vector", "tE")
                            b.op("vector", lambda e, ti=ti, ex=ex, half=half:
                                 e.scalar_tensor_tensor(
                                     out=s_x[ti][:, half * 512:(half + 1) * 512],
                                     in0=p_y[:, half * 512:(half + 1) * 512],
                                     scalar=s_gate[ti][:, ex:ex + 1],
                                     in1=s_x[ti][:, half * 512:(half + 1) * 512],
                                     op0=ALU.mult, op1=ALU.add), inc="vE")
                            ybank_free = b.count("vE")
                        wout_free = b.count("tE")
                    h_free = b.count("tE")

            if DEBUG:
                gv = b.count("vE")
                b.wait("scalar", "vE", gv)
                b.dma("scalar", lambda e: e.dma_start(out=dbg_gate, in_=s_gate[0][:]), inc="dC")
                b.dma("scalar", lambda e: e.dma_start(out=dbg_gfm, in_=s_gfm[:, 0:512]), inc="dC")
            emit_ln(b, [s_x[i][:] for i in range(NT)], None, T)
            lnv = b.count("vE")
            b.wait("scalar", "vE", lnv)
            for i in range(NT):
                b.dma("scalar", lambda e, i=i: e.dma_start(
                    out=x1_out[i * 128:(i + 1) * 128, :], in_=s_x[i][:]), inc="dC")
            b.wait("scalar", "dC", b.count("dC"))
            b.finalize(block)
    return nc



def build_launch2():
    nc = bass.Bass("TRN2", target_bir_lowering=False)
    es = contextlib.ExitStack()

    def din(name, shape, dtype=F32):
        return nc.dram_tensor(name, shape, dtype, kind="ExternalInput").ap()

    def dout(name, shape, dtype=F32):
        return nc.dram_tensor(name, shape, dtype, kind="ExternalOutput").ap()

    x_fm = din("x_fm", [D, S])
    x_fm_own = din("x_fm_own", [D, TPC])
    x_tm = din("x_tm", [TPC, D])
    wqT = din("wqT", [D, D])
    wkT = din("wkT", [D, D])
    wvT = din("wvT", [D, D])
    woT = din("woT", [D, D])
    bq_c = din("bq_c", [128, 8])
    bk_c = din("bk_c", [128, 8])
    bv_r = din("bv_r", [1, D])
    bo_r = din("bo_r", [1, D])
    wgq = din("wgq", [D, I_FF], BF16)
    wuq = din("wuq", [D, I_FF], BF16)
    wdq = din("wdq", [I_FF, D], BF16)
    band = din("band", [128, 31], BF16)
    ident_d = din("ident", [128, 128], BF16)
    ones1f = din("ones1f", [1, 64])
    ones1b = din("ones1b", [1, 128], BF16)
    identf_d = din("identf", [128, 128])

    y_out = dout("y", [TPC, D])
    if DEBUG:
        dbg_k2 = dout("dbg_k2", [128, NKT * 16], BF16)
        dbg_q1 = dout("dbg_q1", [65, 512], BF16)
        dbg_pt = dout("dbg_pt", [128, 512], BF16)
        dbg_den = dout("dbg_den", [1, 512])
        dbg_k2d = dout("dbg_k2d", [16, 512], BF16)
        dbg_xq = dout("dbg_xq", [128, 1024], BF16)
        dbg_hq = dout("dbg_hq", [128, 2816], BF16)
        dbg_g = dout("dbg_g", [128, 512])
        dbg_up = dout("dbg_up", [128, 512])

    K1 = nc.dram_tensor("K1", [D, S], BF16).ap()
    K2D = nc.dram_tensor("K2D", [16, S], BF16).ap()
    V1 = nc.dram_tensor("V1", [S, 16 * 65], BF16).ap()
    Q1 = nc.dram_tensor("Q1", [16 * 65, TPC], BF16).ap()

    JCH = [512, 512, 512, 512, 512, 256]
    JOF = [0, 512, 1024, 1536, 2048, 2560]

    with es:
        b = Builder(nc, es)
        for s in ("dS", "dC", "dG", "tE", "vE", "sE", "pE"):
            b.sem(s)

        T = {}
        AW = sbt(nc, es, "AW", [128, 24576], BF16)
        AB = sbt(nc, es, "AB", [128, 8192], BF16)
        wq_v = AW[:, 0:8192]
        wk_v = AW[:, 8192:16384]
        wv_v = AW[:, 16384:24576]
        wo_v = AW[:, 0:8192]
        wgc_v = [AW[:, 0:4096], AW[:, 4096:8192]]
        wuc_v = [AW[:, 8192:12288], AW[:, 12288:16384]]
        wdc_v = [AW[:, 16384:18432], AW[:, 18432:20480]]
        s_xc = [AB[:, 0:4096], AB[:, 4096:8192]]
        hq_fm = AB[:, 0:5632]

        s_x = [sbt(nc, es, f"s_x{i}", [128, D], F32) for i in range(NT)]
        s_xfm = sbt(nc, es, "s_xfm", [128, 8 * TPC], BF16)
        s_tbf = sbt(nc, es, "s_tbf", [128, D], BF16)
        s_kv = [sbt(nc, es, f"s_kv{i}", [128, 512], BF16) for i in range(2)]
        s_vst = [sbt(nc, es, f"s_vst{i}", [128, 16 * 65], BF16) for i in range(2)]
        s_qp = sbt(nc, es, "s_qp", [65, 512], BF16)
        s_ka = [sbt(nc, es, f"s_ka{i}", [65, 512], BF16) for i in range(2)]
        s_vt = [sbt(nc, es, f"s_vt{i}", [128, 4 * 65], BF16) for i in range(2)]
        s_pt = [sbt(nc, es, f"s_pt{i}", [128, 512], BF16) for i in range(2)]

        s_attn = [sbt(nc, es, f"s_attn{p}", [128, QW], BF16) for p in range(8)]
        s_recip = sbt(nc, es, "s_recip", [1, QW], F32)
        s_bcast = sbt(nc, es, "s_bcast", [64, QW], F32)
        s_ones1 = sbt(nc, es, "s_ones1", [1, 64], F32)
        s_ones1b = sbt(nc, es, "s_ones1b", [1, 128], BF16)
        s_band = sbt(nc, es, "s_band", [128, 31], BF16)
        s_identf = sbt(nc, es, "s_identf", [128, 128], F32)
        s_bq = sbt(nc, es, "s_bq", [128, 8], F32)
        s_bk = sbt(nc, es, "s_bk", [128, 8], F32)
        s_bv = sbt(nc, es, "s_bv", [1, D], BF16)
        s_bo = sbt(nc, es, "s_bo", [1, D], BF16)
        s_k2 = sbt(nc, es, "s_k2", [128, NKT * 16], BF16)
        T["ident"] = sbt(nc, es, "s_ident", [128, 128], BF16)
        T["st"] = [sbt(nc, es, f"s_st{i}", [128, 1], F32)[:] for i in range(8)]
        T["eps"] = T["st"][6]
        s_g = sbt(nc, es, "s_g", [128, 512], F32)
        s_up = sbt(nc, es, "s_up", [128, 512], F32)
        s_h = sbt(nc, es, "s_h", [128, 512], F32)
        s_u = s_g
        s_v2 = s_up
        T["qs"] = sbt(nc, es, "s_qs", [128, 64], F32)
        T["qs2"] = sbt(nc, es, "s_qs2", [128, 64], F32)
        T["qm"] = AW[:, 20480:21504].bitcast(F32)
        T["qc"] = AW[:, 21504:22528].bitcast(F32)
        T["qd"] = AW[:, 22528:23552].bitcast(F32)
        T["qe"] = AW[:, 23552:24576].bitcast(F32)
        s_sqscr = sbt(nc, es, "s_sqscr", [128, 512], BF16)
        s_qq = sbt(nc, es, "s_qq", [128, 512], BF16)
        T["scr"] = s_tbf[:]

        p_yt = pst(nc, es, "p_y", [128, 1024])
        p_y = p_yt[:]
        p_s = [pst(nc, es, "p_s0", [128, 512])[:], pst(nc, es, "p_s1", [128, 512])[:],
               p_yt[:, 0:512], p_yt[:, 512:1024]]
        p_of = pst(nc, es, "p_of", [128, 512])
        p_o = p_of[0:65, 0:512]
        p_mix = pst(nc, es, "p_mix", [128, 512])
        p_mix2 = pst(nc, es, "p_mix2", [128, 512])
        p_bc = p_mix[0:64, 0:512]
        p_k2 = [p_mix[0:8, 0:512], p_mix2[0:8, 0:512]]
        p_q2 = [p_mix[0:1, 0:512], p_mix2[0:1, 0:512]]
        T["p_tr"] = pst(nc, es, "p_tr", [128, 128], BF16)[:]

        with nc.Block() as block:
            for i in range(NT):
                b.dma("scalar", lambda e, i=i: e.dma_start(
                    out=s_x[i][:], in_=x_tm[i * 128:(i + 1) * 128, :]), inc="dC")
            for dst, srcd in [(T["ident"][:], ident_d), (s_ones1[:], ones1f),
                              (s_ones1b[:], ones1b), (s_band[:], band),
                              (s_identf[:], identf_d),
                              (s_bq[:], bq_c), (s_bk[:], bk_c)]:
                b.dma("scalar", lambda e, d=dst, s=srcd: e.dma_start(out=d, in_=s), inc="dC")
            b.dma("gpsimd", lambda e: e.dma_start(out=s_bv[:], in_=bv_r), inc="dG")
            b.dma("gpsimd", lambda e: e.dma_start(out=s_bo[:], in_=bo_r), inc="dG")
            for w_d, w_s in [(wqT, wq_v), (wkT, wk_v), (wvT, wv_v)]:
                b.dma("gpsimd", lambda e, wd=w_d, ws=w_s: e.dma_start(
                    out=ws.rearrange("p (c f) -> p c f", f=D),
                    in_=wd.rearrange("(c p) f -> p c f", p=128)), inc="dG")
            for i in range(2):
                b.op("gpsimd", lambda e, i=i: e.memset(
                    s_vst[i][:].rearrange("p (h c) -> p h c", c=65)[:, :, 64:65], 1.0))
                b.op("gpsimd", lambda e, i=i: e.memset(s_ka[i][64:65, :], 1.0))
            b.op("gpsimd", lambda e: e.memset(T["eps"], 1e-5), inc="pE")
            b.wait("scalar", "pE")
            b.wait("tensor", "pE")
            consts_dC = b.count("dC")
            consts_dG = b.count("dG")
            b.wait("tensor", "dG", consts_dG)
            b.wait("scalar", "dC", consts_dC)
            b.wait("scalar", "dG", consts_dG)

            if L2_PHASE < 1:
                b.wait("scalar", "dC", consts_dC)
                b.wait("scalar", "dG", consts_dG)
                for i in range(NT):
                    b.dma("scalar", lambda e, i=i: e.dma_start(
                        out=y_out[i * 128:(i + 1) * 128, :], in_=s_x[i][:]), inc="dC")
                b.wait("scalar", "dC", b.count("dC"))
                b.finalize(block)
                return nc
            xc_free = [0, 0]
            kbank_free = [0, 0]
            ybank_free = 0
            kvstage_free = [0, 0]
            vstage_free = [0, 0]
            qm_free = 0
            sqb_free = [0, 0]
            for kc in range(16):
                slot = kc % 2
                b.wait("gpsimd", "tE", xc_free[slot])
                for dc in range(8):
                    src = x_fm[dc * 128:(dc + 1) * 128, kc * 512:(kc + 1) * 512]
                    b.dma("gpsimd", lambda e, s=src, d=s_xc[slot], dc=dc: e.dma_start(
                        out=d[:, dc * 512:(dc + 1) * 512], in_=s), inc="dG")
                b.wait("tensor", "dG")
                b.wait("tensor", "vE")
                for pc in range(8):
                    bank = p_s[pc % 2]
                    b.wait("tensor", "vE", kbank_free[pc % 2])
                    b.wait("tensor", "sE", sqb_free[pc % 2])
                    for dc in range(8):
                        lhsT = wk_v[:, dc * 1024 + pc * 128: dc * 1024 + (pc + 1) * 128]
                        rhs = s_xc[slot][:, dc * 512:(dc + 1) * 512]
                        b.op("tensor", lambda e, o=bank, l=lhsT, r=rhs,
                             st=(dc == 0), sp=(dc == 7): e.matmul(o, l, r, start=st, stop=sp),
                             inc="tE" if dc == 7 else None)
                    mmk = b.count("tE")
                    b.wait("vector", "tE", mmk)
                    b.wait("vector", "dC", kvstage_free[pc % 2])
                    stg = s_kv[pc % 2]
                    b.op("vector", lambda e, p=bank, s=stg, pc=pc: e.tensor_scalar(
                        out=s[:], in0=p, scalar1=s_bk[:, pc:pc + 1],
                        scalar2=-2.0, op0=ALU.add, op1=ALU.mult), inc="vE")
                    kbank_free[pc % 2] = b.count("vE")
                    b.wait("scalar", "vE")
                    b.dma("scalar", lambda e, s=stg, pc=pc, kc=kc: e.dma_start(
                        out=K1[pc * 128:(pc + 1) * 128, kc * 512:(kc + 1) * 512],
                        in_=s[:]), inc="dC")
                    kvstage_free[pc % 2] = b.count("dC")
                    if L2_K2:
                        b.wait("scalar", "tE", mmk)
                        b.wait("scalar", "tE", qm_free)
                        b.wait("scalar", "vE", kbank_free[pc % 2])
                        if L2_K2SQ:
                            b.op("scalar", lambda e, p=bank, pc=pc: e.activation(
                                out=s_sqscr[:], in_=p, func=AF.Square,
                                bias=s_bk[:, pc:pc + 1]), inc="sE")
                        else:
                            b.op("scalar", lambda e, p=bank: e.copy(
                                out=s_sqscr[:], in_=p), inc="sE")
                        sq_n = b.count("sE")
                        b.wait("tensor", "sE", sq_n)
                        for hl in range(2):
                            lhsT = s_band[hl * 64:(hl + 1) * 64, 15 - pc:23 - pc]
                            rhs = s_sqscr[hl * 64:(hl + 1) * 64, 0:512]
                            b.op("tensor", lambda e, l=lhsT, r=rhs, hl=hl,
                                 st=(pc == 0), sp=(pc == 7): e.matmul(
                                     p_k2[hl], l, r, start=st, stop=sp), inc="tE")
                        qm_free = b.count("tE")
                        sqb_free[pc % 2] = b.count("sE")
                if L2_K2:
                    k2mm = b.count("tE")
                    for par in range(2):
                        b.wait("vector", "tE", k2mm)
                        b.wait("vector", "dC")
                        b.op("vector", lambda e, par=par: e.tensor_copy(
                            out=s_pt[1][0:8, 0:512], in_=p_k2[par]), inc="vE")
                        b.wait("scalar", "vE")
                        b.dma("scalar", lambda e, par=par, kc=kc: e.dma_start(
                            out=K2D[par * 8:(par + 1) * 8, kc * 512:(kc + 1) * 512],
                            in_=s_pt[1][0:8, 0:512]), inc="dC")
                for sub in range(4 if L2_P1N >= 2 else 0):
                    b.wait("tensor", "vE", ybank_free)
                    for half in range(2):
                        for dc in range(8):
                            lhsT = s_xc[slot][:, dc * 512 + sub * 128: dc * 512 + (sub + 1) * 128]
                            rhs = wv_v[:, dc * 1024 + half * 512: dc * 1024 + (half + 1) * 512]
                            b.op("tensor", lambda e, l=lhsT, r=rhs, st=(dc == 0),
                                 hf=half: e.matmul(
                                     p_y[:, hf * 512:(hf + 1) * 512], l, r, start=st,
                                     stop=False), inc=None)
                        if L2_VBIAS:
                            b.op("tensor", lambda e, hf=half: e.matmul(
                                p_y[:, hf * 512:(hf + 1) * 512], s_ones1b[:],
                                s_bv[:, hf * 512:(hf + 1) * 512], start=False, stop=True),
                                inc="tE")
                        else:
                            b.op("tensor", lambda e, hf=half: e.matmul(
                                p_y[:, hf * 512:(hf + 1) * 512],
                                s_xc[slot][:, 0:128], wv_v[:, 0:512],
                                start=False, stop=True), inc="tE")
                    b.wait("vector", "tE")
                    b.wait("vector", "dC", vstage_free[sub % 2])
                    vstg = s_vst[sub % 2]
                    dstv = vstg[:].rearrange("p (h c) -> p h c", c=65)[:, :, 0:64]
                    b.op("vector", lambda e, d=dstv: e.tensor_copy(
                        out=d, in_=p_y.rearrange("p (h c) -> p h c", c=64)), inc="vE")
                    ybank_free = b.count("vE")
                    b.wait("scalar", "vE")
                    row0 = kc * 512 + sub * 128
                    b.dma("scalar", lambda e, s=vstg, r0=row0: e.dma_start(
                        out=V1[r0:r0 + 128, :], in_=s[:]), inc="dC")
                    vstage_free[sub % 2] = b.count("dC")
                xc_free[slot] = b.count("tE")

            for Jc in range(NCH if L2_P1N >= 3 else 0):
                slot = Jc % 2
                b.wait("tensor", "vE", ybank_free)
                b.wait("gpsimd", "tE", xc_free[slot])
                for dc in range(8):
                    src = x_fm_own[dc * 128:(dc + 1) * 128, Jc * 512:(Jc + 1) * 512]
                    b.dma("gpsimd", lambda e, s=src, d=s_xc[slot], dc=dc: e.dma_start(
                        out=d[:, dc * 512:(dc + 1) * 512], in_=s), inc="dG")
                b.wait("tensor", "dG")
                for pc in range(8):
                    bank = p_s[2 + pc % 2]
                    b.wait("tensor", "vE", kbank_free[pc % 2])
                    b.wait("tensor", "sE", sqb_free[pc % 2])
                    for dc in range(8):
                        lhsT = wq_v[:, dc * 1024 + pc * 128: dc * 1024 + (pc + 1) * 128]
                        rhs = s_xc[slot][:, dc * 512:(dc + 1) * 512]
                        b.op("tensor", lambda e, o=bank, l=lhsT, r=rhs,
                             st=(dc == 0), sp=(dc == 7): e.matmul(o, l, r, start=st, stop=sp),
                             inc="tE" if dc == 7 else None)
                    mmq = b.count("tE")
                    b.wait("vector", "tE", mmq)
                    b.wait("vector", "dC", kvstage_free[pc % 2])
                    stg = s_kv[pc % 2]
                    b.op("vector", lambda e, p=bank, s=stg, pc=pc: e.tensor_scalar(
                        out=s[:], in0=p, scalar1=s_bq[:, pc:pc + 1], scalar2=None,
                        op0=ALU.add), inc="vE")
                    evq = b.count("vE")
                    b.wait("scalar", "vE", evq)
                    for hl in range(2):
                        h = 2 * pc + hl
                        b.dma("scalar", lambda e, s=stg, h=h, hl=hl, Jc=Jc: e.dma_start(
                            out=Q1[h * 65:h * 65 + 64, Jc * 512:(Jc + 1) * 512],
                            in_=s[hl * 64:(hl + 1) * 64, :]), inc="dC")
                    kvstage_free[pc % 2] = b.count("dC")
                    if L2_Q2:
                        b.wait("scalar", "tE", mmq)
                        b.wait("scalar", "tE", qm_free)
                        b.wait("scalar", "vE", evq)
                        b.op("scalar", lambda e, p=bank, pc=pc: e.activation(
                            out=s_sqscr[:], in_=p, func=AF.Square,
                            bias=s_bq[:, pc:pc + 1]), inc="sE")
                        sq_n = b.count("sE")
                        b.wait("tensor", "sE", sq_n)
                        for hl in range(2):
                            h = 2 * pc + hl
                            b.wait("tensor", "vE")
                            lhsT = s_band[hl * 64:(hl + 1) * 64, 15:16]
                            rhs = s_sqscr[hl * 64:(hl + 1) * 64, 0:512]
                            b.op("tensor", lambda e, l=lhsT, r=rhs, hl=hl: e.matmul(
                                p_q2[hl], l, r, start=True, stop=True), inc="tE")
                            q2mm = b.count("tE")
                            b.wait("vector", "tE", q2mm)
                            b.wait("vector", "dC")
                            b.op("vector", lambda e, hl=hl: e.tensor_copy(
                                out=s_pt[0][0:1, 0:512], in_=p_q2[hl]), inc="vE")
                            q2c = b.count("vE")
                            b.wait("scalar", "vE", q2c)
                            b.dma("scalar", lambda e, h=h, Jc=Jc: e.dma_start(
                                out=Q1[h * 65 + 64:h * 65 + 65, Jc * 512:(Jc + 1) * 512],
                                in_=s_pt[0][0:1, 0:512]), inc="dC")
                        qm_free = b.count("tE")
                        sqb_free[pc % 2] = b.count("sE")
                xc_free[slot] = b.count("tE")
            p1_dC = b.count("dC")

            p1_tE = b.count("tE")
            b.wait("gpsimd", "tE", p1_tE)
            b.dma("gpsimd", lambda e: e.dma_start(
                out=wo_v.rearrange("p (c f) -> p c f", f=D),
                in_=woT.rearrange("(c p) f -> p c f", p=128)), inc="dG")
            wo_dG = b.count("dG")

            if L2_PHASE < 2:
                emit_ln(b, [], None, T)
                lnv = b.count("vE")
                b.wait("scalar", "vE", lnv)
                b.wait("scalar", "tE")
                b.wait("scalar", "sE")
                b.wait("scalar", "dS")
                b.wait("scalar", "dG")
                b.wait("scalar", "dC")
                for i in range(NT):
                    b.dma("scalar", lambda e, i=i: e.dma_start(
                        out=y_out[i * 128:(i + 1) * 128, :], in_=s_x[i][:]), inc="dC")
                b.wait("scalar", "dC", b.count("dC"))
                b.finalize(block)
                return nc
            b.wait("sync", "dC", p1_dC)
            if L2_K2:
                for kt in range(NKT):
                    b.dma("sync", lambda e, kt=kt: e.dma_start(
                        out=s_k2[:, kt * 16:(kt + 1) * 16],
                        in_=K2D[0:16, kt * 128:(kt + 1) * 128], transpose=True),
                        inc="dS")
                b.wait("scalar", "dS")
            if DEBUG:
                b.dma("scalar", lambda e: e.dma_start(out=dbg_k2, in_=s_k2[:]), inc="dC")
                b.dma("scalar", lambda e: e.dma_start(
                    out=dbg_k2d, in_=K2D[0:16, 0:512]), inc="dC")
                b.dma("scalar", lambda e: e.dma_start(
                    out=dbg_q1, in_=Q1[0:65, 0:512]), inc="dC")
            b.wait("tensor", "dG", wo_dG)
            ka_free = [0, 0]
            vt_free = [0, 0]
            qp_free = 0
            sb_free = [0, 0]
            chain_free = 0
            att_free = [0] * 8
            s_ub = [s_u[:], s_h[:]]
            s_vb = [s_v2[:], T["qm"]]
            last_e5 = [0, 0]
            last_em = [0, 0]
            last_av = [0, 0]
            for Jc in range(NCH):
                for pc in range(8):
                    for hl in range(2):
                        h = 2 * pc + hl
                        b.wait("tensor", "vE")
                        b.wait("sync", "tE", qp_free)
                        b.dma("sync", lambda e, h=h, Jc=Jc: e.dma_start(
                            out=s_qp[:], in_=Q1[h * 65:(h + 1) * 65,
                                               Jc * 512:(Jc + 1) * 512]), inc="dS")
                        qld = b.count("dS")
                        b.wait("tensor", "dS", qld)
                        hcol = (h // 2) + 8 * (h % 2)
                        mm_c = [0] * NKT
                        ln_c = [0] * NKT
                        e5_c = [0] * NKT
                        em_c = [0] * NKT
                        av_c = [0] * NKT
                        kld_c = [0] * (NKT // 4)
                        vld_c = [0] * (NKT // 4)
                        for it in range(-1, NKT + 2):
                            ks = it + 1
                            if 0 <= ks < NKT:
                                kg, ki = ks // 4, ks % 4
                                kslot = kg % 2
                                if ki == 0:
                                    b.wait("sync", "tE", ka_free[kslot])
                                    b.dma("sync", lambda e, h=h, kg=kg, d=s_ka[kslot]:
                                          e.dma_start(out=d[0:64, :],
                                                      in_=K1[h * 64:(h + 1) * 64,
                                                             kg * 512:(kg + 1) * 512]),
                                          inc="dS")
                                    kld_c[kg] = b.count("dS")
                                    b.wait("sync", "tE", vt_free[kslot])
                                    srcv = V1[kg * 512:(kg + 1) * 512,
                                              h * 65:(h + 1) * 65]
                                    b.dma("sync", lambda e, s=srcv, d=s_vt[kslot]:
                                          e.dma_start(
                                              out=d[:].rearrange("p (g c) -> p g c", g=4),
                                              in_=s.rearrange("(g p) c -> p g c", p=128)),
                                          inc="dS")
                                    vld_c[kg] = b.count("dS")
                                sbank = p_s[ks % 2]
                                b.wait("tensor", "dS", kld_c[kg])
                                b.wait("tensor", "sE", sb_free[ks % 2])
                                lhsT = s_ka[kslot][:, ki * 128:(ki + 1) * 128]
                                b.op("tensor", lambda e, o=sbank, l=lhsT: e.matmul(
                                    o, l, s_qp[:], start=True, stop=True), inc="tE")
                                mm_c[ks] = b.count("tE")
                                if ks % 4 == 3:
                                    ka_free[kslot] = mm_c[ks]
                            kt = it
                            if 0 <= kt < NKT:
                                if kt == 0 and Jc == 0:
                                    b.wait("scalar", "dS")
                                b.wait("scalar", "tE", mm_c[kt])
                                b.wait("scalar", "sE", last_e5[kt % 2])
                                k2col = s_k2[:, kt * 16 + hcol: kt * 16 + hcol + 1]
                                b.op("scalar", lambda e, p=p_s[kt % 2], kc2=k2col,
                                     o=s_ub[kt % 2]: e.activation(
                                         out=o, in_=p, func=AF.Ln, bias=kc2),
                                     inc="sE", serial=False)
                                ln_c[kt] = b.count("sE")
                                sb_free[kt % 2] = ln_c[kt]
                            j = it - 1
                            if 0 <= j < NKT:
                                b.wait("scalar", "sE", max(ln_c[j], last_em[j % 2]))
                                b.op("scalar", lambda e, i_=s_ub[j % 2], o=s_vb[j % 2]:
                                     e.activation(out=o, in_=i_, func=AF.Exp, scale=0.5),
                                     inc="sE", serial=False)
                                e5_c[j] = b.count("sE")
                                last_e5[j % 2] = e5_c[j]
                            i2 = it - 2
                            if 0 <= i2 < NKT:
                                b.wait("scalar", "sE", e5_c[i2])
                                if last_av[i2 % 2] > 0:
                                    b.wait("scalar", "tE", last_av[i2 % 2])
                                b.op("scalar", lambda e, i_=s_vb[i2 % 2],
                                     o=s_pt[i2 % 2]: e.activation(
                                         out=o[:], in_=i_, func=AF.Exp, scale=-1.0),
                                     inc="sE", serial=False)
                                em_c[i2] = b.count("sE")
                                last_em[i2 % 2] = em_c[i2]
                                if DEBUG and Jc == 0 and h == 0 and i2 == 0:
                                    b.wait("scalar", "sE", em_c[0])
                                    b.dma("scalar", lambda e: e.dma_start(
                                        out=dbg_pt, in_=s_pt[0][:]), inc="dC")
                                b.wait("tensor", "sE", em_c[i2])
                                b.wait("tensor", "dS", vld_c[i2 // 4])
                                lhsT = s_vt[(i2 // 4) % 2][:, (i2 % 4) * 65:
                                                           (i2 % 4 + 1) * 65]
                                b.op("tensor", lambda e, l=lhsT, r=s_pt[i2 % 2],
                                     st=(i2 == 0), sp=(i2 == NKT - 1): e.matmul(
                                         p_o, l, r[:], start=st, stop=sp), inc="tE")
                                av_c[i2] = b.count("tE")
                                last_av[i2 % 2] = av_c[i2]
                                if i2 % 4 == 3:
                                    vt_free[(i2 // 4) % 2] = av_c[i2]
                        qp_free = mm_c[NKT - 1]
                        if DEBUG and Jc == 0 and h == 0:
                            dn = b.count("tE")
                            b.wait("vector", "tE", dn)
                            b.op("vector", lambda e: e.tensor_copy(
                                out=s_u[0:1, 0:512], in_=p_of[64:65, 0:512]), inc="vE")
                            b.wait("scalar", "vE")
                            b.dma("scalar", lambda e: e.dma_start(
                                out=dbg_den, in_=s_u[0:1, 0:512]), inc="dC")
                        b.wait("vector", "tE")
                        b.op("vector", lambda e: e.reciprocal(
                            out=s_recip[:], in_=p_of[64:65, 0:512]), inc="vE")
                        rec_n = b.count("vE")
                        b.wait("tensor", "vE", rec_n)
                        b.op("tensor", lambda e: e.matmul(
                            p_bc, s_ones1[:], s_recip[:], start=True, stop=True),
                            inc="tE")
                        bc_n = b.count("tE")
                        b.wait("vector", "tE", bc_n)
                        b.op("vector", lambda e: e.tensor_copy(
                            out=s_bcast[:], in_=p_bc), inc="vE")
                        b.wait("vector", "tE", att_free[pc])
                        b.op("vector", lambda e, pc=pc, hl=hl: e.tensor_tensor(
                            out=s_attn[pc][hl * 64:(hl + 1) * 64, :],
                            in0=p_of[0:64, 0:512], in1=s_bcast[:], op=ALU.mult), inc="vE")
                attn_ready = b.count("vE")
                b.wait("tensor", "vE", attn_ready)
                for sub in range(4):
                    b.wait("tensor", "vE", ybank_free)
                    for half in range(2):
                        for pc in range(8):
                            lhsT = s_attn[pc][:, sub * 128:(sub + 1) * 128]
                            rhs = wo_v[:, pc * 1024 + half * 512: pc * 1024 + (half + 1) * 512]
                            b.op("tensor", lambda e, l=lhsT, r=rhs, hf=half,
                                 st=(pc == 0): e.matmul(
                                     p_y[:, hf * 512:(hf + 1) * 512], l, r,
                                     start=st, stop=False), inc=None)
                        b.op("tensor", lambda e, hf=half: e.matmul(
                            p_y[:, hf * 512:(hf + 1) * 512], s_ones1b[:],
                            s_bo[:, hf * 512:(hf + 1) * 512], start=False, stop=True),
                            inc="tE")
                    for pc in range(8):
                        att_free[pc] = b.count("tE")
                    b.wait("vector", "tE")
                    ti = Jc * 4 + sub
                    b.op("vector", lambda e, ti=ti: e.tensor_tensor(
                        out=s_x[ti][:], in0=p_y, in1=s_x[ti][:], op=ALU.add),
                        inc="vE")
                    ybank_free = b.count("vE")

            if L2_PHASE < 3:
                lnv = b.count("vE")
                b.wait("scalar", "vE", lnv)
                b.wait("scalar", "tE")
                b.wait("scalar", "sE")
                for i in range(NT):
                    b.dma("scalar", lambda e, i=i: e.dma_start(
                        out=y_out[i * 128:(i + 1) * 128, :], in_=s_x[i][:]), inc="dC")
                b.wait("scalar", "dC", b.count("dC"))
                b.finalize(block)
                return nc
            for i in range(NT):
                b.wait("vector", "tE")
                emit_ln(b, [s_x[i][:]], None, T)
                lnn = b.count("vE")
                b.wait("vector", "vE", lnn)
                for xh in range(2):
                    b.wait("vector", "tE")
                    emit_quant(b, s_x[i][:, xh * 512:(xh + 1) * 512], s_qq[:], T, 128, 512)
                    for dc4 in range(4):
                        dc = xh * 4 + dc4
                        srct = s_qq[:, dc4 * 128:(dc4 + 1) * 128]
                        cons = b.count("vE")
                        b.wait("tensor", "vE", cons)
                        b.op("tensor", lambda e, s=srct: e.transpose(
                            T["p_tr"], s, T["ident"][:]), inc="tE")
                        b.wait("vector", "tE")
                        dst = s_xfm[:, dc * TPC + i * 128: dc * TPC + (i + 1) * 128]
                        b.op("vector", lambda e, d=dst: e.tensor_copy(
                            out=d, in_=T["p_tr"]), inc="vE")
            fm_ready = b.count("vE")
            b.wait("tensor", "vE", fm_ready)
            if DEBUG:
                b.wait("scalar", "vE", fm_ready)
                b.dma("scalar", lambda e: e.dma_start(
                    out=dbg_xq, in_=s_xfm[:].rearrange("p (c t) -> p c t", c=8)[:, :, 0:128]
                    .rearrange("p c t -> p c t")), inc="dC")

            if L2_PHASE < 4:
                lnv = b.count("vE")
                b.wait("scalar", "vE", lnv)
                b.wait("scalar", "tE")
                b.wait("scalar", "sE")
                for i in range(NT):
                    b.dma("scalar", lambda e, i=i: e.dma_start(
                        out=y_out[i * 128:(i + 1) * 128, :], in_=s_x[i][:]), inc="dC")
                b.wait("scalar", "dC", b.count("dC"))
                b.finalize(block)
                return nc
            wg_r = wgq.rearrange("(c p) f -> p c f", p=128)
            wu_r = wuq.rearrange("(c p) f -> p c f", p=128)
            wd_r = wdq.rearrange("(c p) f -> p c f", p=128)
            hq_free = 0
            gbank_free = sb_free[0]
            ubank_free = 0
            wd_free = [0, 0]
            wgu_free = [b.count("tE"), b.count("tE")]
            for tg in range(8):
                b.wait("tensor", "tE", hq_free)
                for jci, jw in enumerate(JCH):
                    jo = JOF[jci]
                    wslot = jci % 2
                    b.wait("gpsimd", "tE", wgu_free[wslot])
                    b.dma("gpsimd", lambda e, jo=jo, jw=jw, d=wgc_v[wslot]: e.dma_start(
                        out=d.rearrange("p (c f) -> p c f", f=512)[:, :, 0:jw],
                        in_=wg_r[:, :, jo:jo + jw]), inc="dG")
                    b.dma("gpsimd", lambda e, jo=jo, jw=jw, d=wuc_v[wslot]: e.dma_start(
                        out=d.rearrange("p (c f) -> p c f", f=512)[:, :, 0:jw],
                        in_=wu_r[:, :, jo:jo + jw]), inc="dG")
                    wld = b.count("dG")
                    b.wait("tensor", "dG", wld)
                    for ts2 in range(2):
                        ti = tg * 2 + ts2
                        b.wait("tensor", "sE", gbank_free)
                        for dc in range(8):
                            lhsT = s_xfm[:, dc * TPC + ti * 128: dc * TPC + (ti + 1) * 128]
                            rhs = wgc_v[wslot][:, dc * 512: dc * 512 + jw]
                            b.op("tensor", lambda e, l=lhsT, r=rhs, jw=jw, st=(dc == 0),
                                 sp=(dc == 7): e.matmul(p_s[0][:, 0:jw], l, r,
                                                        start=st, stop=sp),
                                 inc="tE" if dc == 7 else None)
                        mg = b.count("tE")
                        b.wait("scalar", "tE", mg)
                        b.op("scalar", lambda e, jw=jw: e.activation(
                            out=s_g[:, 0:jw], in_=p_s[0][:, 0:jw], func=AF.Silu), inc="sE")
                        gbank_free = b.count("sE")
                        if DEBUG and tg == 0 and ts2 == 0 and jci == 0:
                            b.wait("scalar", "sE", gbank_free)
                            b.dma("scalar", lambda e: e.dma_start(
                                out=dbg_g, in_=s_g[:, 0:512]), inc="dC")
                        b.wait("tensor", "vE", ubank_free)
                        for dc in range(8):
                            lhsT = s_xfm[:, dc * TPC + ti * 128: dc * TPC + (ti + 1) * 128]
                            rhs = wuc_v[wslot][:, dc * 512: dc * 512 + jw]
                            b.op("tensor", lambda e, l=lhsT, r=rhs, jw=jw, st=(dc == 0),
                                 sp=(dc == 7): e.matmul(p_s[1][:, 0:jw], l, r,
                                                        start=st, stop=sp),
                                 inc="tE" if dc == 7 else None)
                        mu = b.count("tE")
                        b.wait("vector", "tE", mu)
                        b.op("vector", lambda e, jw=jw: e.tensor_copy(
                            out=s_up[:, 0:jw], in_=p_s[1][:, 0:jw]), inc="vE")
                        ubank_free = b.count("vE")
                        b.wait("vector", "sE")
                        b.op("vector", lambda e, jw=jw: e.tensor_tensor(
                            out=s_h[:, 0:jw], in0=s_g[:, 0:jw], in1=s_up[:, 0:jw],
                            op=ALU.mult), inc="vE")
                        emit_quant(b, s_h[:, 0:jw], s_qq[:, 0:jw], T, 128, jw)
                        for hc in range(jw // 128):
                            cons = b.count("vE")
                            b.wait("tensor", "vE", cons)
                            srct = s_qq[:, hc * 128:(hc + 1) * 128]
                            b.op("tensor", lambda e, s=srct: e.transpose(
                                T["p_tr"], s, T["ident"][:]), inc="tE")
                            b.wait("vector", "tE")
                            col = ts2 * 2816 + jo + hc * 128
                            b.op("vector", lambda e, col=col: e.tensor_copy(
                                out=hq_fm[:, col:col + 128], in_=T["p_tr"]), inc="vE")
                    wgu_free[wslot] = b.count("tE")
                hq_ready = b.count("vE")
                b.wait("tensor", "vE", hq_ready)
                if DEBUG and tg == 0:
                    b.wait("scalar", "vE", hq_ready)
                    b.dma("scalar", lambda e: e.dma_start(
                        out=dbg_hq, in_=hq_fm[:, 0:2816]), inc="dC")
                nhc = I_FF // 128
                HCG = [2] * 11
                for ts2 in range(2):
                    ti = tg * 2 + ts2
                    b.wait("tensor", "vE", ybank_free)
                    hc0 = 0
                    for gi, gn in enumerate(HCG):
                        dslot = gi % 2
                        b.wait("gpsimd", "tE", wd_free[dslot])
                        b.dma("gpsimd", lambda e, d=wdc_v[dslot], h0=hc0, gn=gn: e.dma_start(
                            out=d.rearrange("p (c f) -> p c f", f=1024)[:, 0:gn, :],
                            in_=wd_r[:, h0:h0 + gn, :]), inc="dG")
                        dld = b.count("dG")
                        b.wait("tensor", "dG", dld)
                        for gj in range(gn):
                            hc = hc0 + gj
                            lhsT = hq_fm[:, ts2 * 2816 + hc * 128: ts2 * 2816 + (hc + 1) * 128]
                            wcf = wdc_v[dslot][:, gj * 1024:(gj + 1) * 1024]
                            for half in range(2):
                                b.op("tensor", lambda e, l=lhsT, w=wcf, hf=half, hc=hc,
                                     st=(hc == 0), sp=(hc == nhc - 1): e.matmul(
                                         p_y[:, hf * 512:(hf + 1) * 512], l,
                                         w[:, hf * 512:(hf + 1) * 512], start=st, stop=sp),
                                     inc="tE" if (gj == gn - 1 and half == 1) else None)
                        wd_free[dslot] = b.count("tE")
                        hc0 += gn
                    b.wait("vector", "tE")
                    b.op("vector", lambda e, ti=ti: e.tensor_tensor(
                        out=s_x[ti][:], in0=p_y, in1=s_x[ti][:], op=ALU.add), inc="vE")
                    ybank_free = b.count("vE")
                hq_free = b.count("tE")

            emit_ln(b, [s_x[i][:] for i in range(NT)], None, T)
            lnv = b.count("vE")
            b.wait("scalar", "vE", lnv)
            for i in range(NT):
                b.dma("scalar", lambda e, i=i: e.dma_start(
                    out=y_out[i * 128:(i + 1) * 128, :], in_=s_x[i][:]), inc="dC")
            b.wait("scalar", "dC", b.count("dC"))
            b.finalize(block)
    return nc



def own_index(r):
    idx = []
    for j in range(NT):
        t = 4 * j + r
        idx.append(np.arange(t * 128, (t + 1) * 128))
    return np.concatenate(idx)


def cont(a):
    return np.ascontiguousarray(a)


def make_masks(r):
    m = np.zeros((4, 128, 128), dtype=bfd)
    tri = (np.arange(128)[:, None] <= np.arange(128)[None, :]).astype(bfd)
    for s in range(4):
        if s < r:
            m[s] = 1.0
        elif s == r:
            m[s] = tri
        else:
            m[s] = 0.0
    return m


_cache = {}


def prep_launch1(inp):
    x = np.asarray(inp["L0_attn_x"], np.float32)
    WQ = np.asarray(inp["L0_attn_WQ"], np.float32)
    WK = np.asarray(inp["L0_attn_WK"], np.float32)
    WV = np.asarray(inp["L0_attn_WV"], np.float32)
    WO = np.asarray(inp["L0_attn_WO"], np.float32)
    shared = {
        "wql": cont(WQ.transpose(2, 0, 1).reshape(D, D)),
        "wkl": cont(WK.transpose(2, 0, 1).reshape(D, D)),
        "wvr": cont(WV.transpose(2, 0, 1).reshape(D, D)),
        "wor": cont(WO.transpose(0, 2, 1).reshape(D, D)),
        "wg1": np.asarray(inp["L0_ffn_wg1"], np.float32),
        "bg1c": cont(np.asarray(inp["L0_ffn_bg1"], np.float32).reshape(2, 128).T),
        "wg2": np.asarray(inp["L0_ffn_wg2"], np.float32),
        "w_in": np.asarray(inp["L0_ffn_w_in"], np.float32),
        "w_out": np.asarray(inp["L0_ffn_w_out"], np.float32),
        "ident": np.eye(128, dtype=bfd),
        "ones1f": np.ones((1, 64), np.float32),
    }
    wgate = np.asarray(inp["L1_ffn_w_gate"], np.float32)
    wup = np.asarray(inp["L1_ffn_w_up"], np.float32)
    wdown = np.asarray(inp["L1_ffn_w_down"], np.float32)
    x_fm_b = [cont(x[g].T) for g in range(B)]
    maps = []
    for c in range(NCORE):
        g, r = c // RPG, c % RPG
        oi = own_index(r)
        m = dict(shared)
        m["x_fm"] = x_fm_b[g]
        m["x_fm_own"] = cont(x[g].T[:, oi])
        m["x_tm"] = cont(x[g][oi])
        m["masks"] = make_masks(r)
        m["wg_sl"] = cont(wgate[c * WSL:(c + 1) * WSL])
        m["wu_sl"] = cont(wup[c * WSL:(c + 1) * WSL])
        m["wd_sl"] = cont(wdown[:, c * WSL:(c + 1) * WSL])
        maps.append(m)
    return maps


def run_launch1(inp):
    if "l1" not in _cache:
        _cache["l1"] = build_launch1()
    nc = _cache["l1"]
    maps = prep_launch1(inp)
    res = run_bass_kernel_spmd(nc, maps, core_ids=list(range(NCORE)))
    return res.results


def assemble_x1(results):
    x1 = np.zeros((B, S, D), np.float32)
    for c in range(NCORE):
        g, r = c // RPG, c % RPG
        x1[g][own_index(r)] = results[c]["x1"]
    return x1


def assemble_qw(results):
    wgT = np.concatenate([results[c]["wgT_q"] for c in range(NCORE)], axis=1)
    wuT = np.concatenate([results[c]["wuT_q"] for c in range(NCORE)], axis=1)
    wdT = np.concatenate([results[c]["wdT_q"] for c in range(NCORE)], axis=0)
    return wgT, wuT, wdT


def prep_launch2(inp, x1, wgT, wuT, wdT):
    wq = np.asarray(inp["L1_attn_wq"], np.float32)
    wk = np.asarray(inp["L1_attn_wk"], np.float32)
    wv = np.asarray(inp["L1_attn_wv"], np.float32)
    wo = np.asarray(inp["L1_attn_wo"], np.float32)
    band = np.zeros((128, 31), dtype=bfd)
    band[:, 15] = 1.0
    shared = {
        "wqT": cont(wq.T), "wkT": cont(wk.T), "wvT": cont(wv.T), "woT": cont(wo.T),
        "bq_c": cont(np.asarray(inp["L1_attn_bq"], np.float32).reshape(8, 128).T),
        "bk_c": cont(np.asarray(inp["L1_attn_bk"], np.float32).reshape(8, 128).T),
        "bv_r": np.asarray(inp["L1_attn_bv"], np.float32).reshape(1, D),
        "bo_r": np.asarray(inp["L1_attn_bo"], np.float32).reshape(1, D),
        "wgq": wgT, "wuq": wuT, "wdq": wdT,
        "band": band,
        "ident": np.eye(128, dtype=bfd),
        "identf": np.eye(128, dtype=np.float32),
        "ones1f": np.ones((1, 64), np.float32),
        "ones1b": np.ones((1, 128), dtype=bfd),
    }
    x1_fm_b = [cont(x1[g].T) for g in range(B)]
    maps = []
    for c in range(NCORE):
        g, r = c // RPG, c % RPG
        oi = own_index(r)
        m = dict(shared)
        m["x_fm"] = x1_fm_b[g]
        m["x_fm_own"] = cont(x1[g].T[:, oi])
        m["x_tm"] = cont(x1[g][oi])
        maps.append(m)
    return maps


def kernel(**inp):
    if "l1" not in _cache:
        _cache["l1"] = build_launch1()
    maps1 = prep_launch1(inp)
    res1 = run_bass_kernel_spmd(_cache["l1"], maps1, core_ids=list(range(NCORE)))
    x1 = assemble_x1(res1.results)
    wgT, wuT, wdT = assemble_qw(res1.results)
    if "l2" not in _cache:
        _cache["l2"] = build_launch2()
    maps2 = prep_launch2(inp, x1, wgT, wuT, wdT)
    res2 = run_bass_kernel_spmd(_cache["l2"], maps2, core_ids=list(range(NCORE)))
    out = np.zeros((B, S, D), np.float32)
    for c in range(NCORE):
        g, r = c // RPG, c % RPG
        out[g][own_index(r)] = res2.results[c]["y"]
    return out
